# version 02
# baseline (speedup 1.0000x reference)
import math
import numpy as np
import ml_dtypes
import concourse.bass as bass
import concourse.mybir as mybir
from concourse.bass_utils import run_bass_kernel_spmd

F32 = mybir.dt.float32
BF16 = mybir.dt.bfloat16
I32 = mybir.dt.int32
U32 = mybir.dt.uint32
U8 = mybir.dt.uint8
AF = mybir.ActivationFunctionType
ALU = mybir.AluOpType
AX = mybir.AxisListType

NEG = -3.0e38

ENGINES = ("sync", "scalar", "vector", "tensor", "gpsimd")
DMAQ = {"syq": "sync", "scq": "scalar", "gpq": "gpsimd", "vq": "vector"}


def _regions(ap_or_list):
    out = []
    if ap_or_list is None:
        return out
    items = ap_or_list if isinstance(ap_or_list, (list, tuple)) else [ap_or_list]
    for ap in items:
        if ap is None:
            continue
        t = ap.tensor
        name = t.name
        esz = mybir.dt.size(ap.dtype)
        lo = ap.offset
        span = 1
        try:
            for st, cnt in ap.ap:
                span += abs(int(st)) * (int(cnt) - 1)
        except Exception:
            span = t.size if hasattr(t, "size") else 1 << 40
        out.append((name, lo * esz, (lo + span) * esz))
    return out


class Op:
    __slots__ = ("idx", "eng", "queue", "fn", "reads", "writes", "seq", "inc_needed",
                 "waits", "tag", "deps", "pre_drain", "dma_sid", "dma_use")

    def __init__(self, idx, eng, queue, fn, reads, writes, tag=""):
        self.idx = idx
        self.eng = eng
        self.queue = queue
        self.fn = fn
        self.reads = reads
        self.writes = writes
        self.seq = None
        self.inc_needed = False
        self.pre_drain = False
        self.waits = []
        self.tag = tag


class A:

    SERIALIZE = False

    def __init__(self, nc):
        self.nc = nc
        self.ops = []
        self.access = {}
        self.nwaits = 0

    def op(self, eng, fn, reads=None, writes=None, queue=None, tag=""):
        o = Op(len(self.ops), eng, queue, fn, _regions(reads), _regions(writes), tag)
        self.ops.append(o)
        deps = set()
        for (name, lo, hi) in o.reads:
            for (l2, h2, prev, w) in self.access.get(name, ()):
                if w and l2 < hi and lo < h2:
                    deps.add(prev)
        for (name, lo, hi) in o.writes:
            for (l2, h2, prev, w) in self.access.get(name, ()):
                if l2 < hi and lo < h2:
                    deps.add(prev)
        if A.SERIALIZE and self.ops[:-1]:
            deps.add(self.ops[-2])
        o.deps = deps
        for (name, lo, hi) in o.reads:
            self.access.setdefault(name, []).append((lo, hi, o, False))
        for (name, lo, hi) in o.writes:
            lst = self.access.setdefault(name, [])
            keep = [e for e in lst if not (lo <= e[0] and e[1] <= hi)]
            keep.append((lo, hi, o, True))
            self.access[name] = keep
        return o

    def _stream_name(self, o):
        return o.queue if o.queue else o.eng

    NDMASEM = 48

    def schedule(self):
        counters = {}
        dma_n = 0
        self.sem_uses = {}
        for o in self.ops:
            sname = self._stream_name(o)
            counters[sname] = counters.get(sname, 0) + 1
            o.seq = counters[sname]
            if o.queue is not None:
                sid = dma_n % self.NDMASEM
                dma_n += 1
                self.sem_uses[sid] = self.sem_uses.get(sid, 0) + 1
                o.dma_sid = sid
                o.dma_use = self.sem_uses[sid]
        hi_water = {}
        for o in self.ops:
            cons = self._stream_name(o)
            need = {}
            for d in o.deps:
                if d.queue is not None:
                    key = ("D", d.dma_sid)
                    need[key] = max(need.get(key, 0), d.dma_use)
                else:
                    pstream = self._stream_name(d)
                    if pstream == cons and o.queue is None:
                        if cons != "tensor" and (o.seq - d.seq) <= 2:
                            o.pre_drain = True
                        continue
                    if o.queue is not None and d.eng == o.eng:
                        continue
                    need[("E", pstream)] = max(need.get(("E", pstream), 0), d.seq)
            for k, seq in sorted(need.items()):
                key = (cons, k)
                if hi_water.get(key, 0) >= seq:
                    continue
                hi_water[key] = seq
                o.waits.append((k, seq))
        targets = {}
        for o in self.ops:
            for (k, seq) in o.waits:
                if k[0] == "E":
                    targets.setdefault(k[1], set()).add(seq)
        by_stream = {}
        for o in self.ops:
            if o.queue is None:
                by_stream.setdefault(self._stream_name(o), {})[o.seq] = o
        for pstream, seqs in targets.items():
            for sq in seqs:
                by_stream[pstream][sq].inc_needed = True
        for o in self.ops:
            if o.queue is not None:
                o.inc_needed = True

    def emit(self, stack):
        self.schedule()
        sems = {}
        streams = set(self._stream_name(o) for o in self.ops if o.queue is None)
        for s in streams:
            sems[s] = stack.enter_context(self.nc.semaphore("sem_" + s))
        dsems = [stack.enter_context(self.nc.semaphore(f"dma{i}"))
                 for i in range(min(self.NDMASEM, len(self.sem_uses) if self.sem_uses else 0) or 1)]
        inc_prefix = {}
        for o in self.ops:
            if o.queue is None:
                sname = self._stream_name(o)
                lst = inc_prefix.setdefault(sname, [0])
                lst.append(lst[-1] + (1 if o.inc_needed else 0))
        per_engine_ops = {}
        for o in self.ops:
            per_engine_ops.setdefault(o.eng, []).append(o)
        block = stack.enter_context(self.nc.Block())
        nwaits = [0]
        for eng_name in ENGINES:
            ops = per_engine_ops.get(eng_name, [])
            if not ops:
                continue
            def mk(ops, eng_name):
                def section(eng):
                    for o in ops:
                        for (k, seq) in o.waits:
                            if k[0] == "D":
                                eng.wait_ge(dsems[k[1]], seq * 16)
                            else:
                                val = inc_prefix[k[1]][seq]
                                if val > 0:
                                    eng.wait_ge(sems[k[1]], val)
                            nwaits[0] += 1
                        if o.queue is not None:
                            if o.dma_use > 1:
                                eng.wait_ge(dsems[o.dma_sid], (o.dma_use - 1) * 16)
                            instr = o.fn(eng)
                            instr.then_inc(dsems[o.dma_sid], 16)
                        else:
                            if o.pre_drain:
                                eng.drain()
                            instr = o.fn(eng)
                            if o.inc_needed:
                                if instr is None:
                                    instr = eng.nop()
                                if eng_name in ("vector", "scalar", "gpsimd", "tensor"):
                                    instr = eng.drain()
                                instr.then_inc(sems[self._stream_name(o)], 1)
                return section
            getattr(block, eng_name)(mk(ops, eng_name))
        self.nwaits = nwaits[0]

    def dma(self, queue, out, in_, tag="", **kw):
        eng = DMAQ[queue]
        return self.op(eng, lambda e: e.dma_start(out=out, in_=in_, **kw),
                       reads=[in_], writes=[out], queue=queue, tag=tag)

    def tt(self, out, in0, in1, op, eng="vector", tag=""):
        return self.op(eng, lambda e: e.tensor_tensor(out=out, in0=in0, in1=in1, op=op),
                       reads=[in0, in1], writes=[out], tag=tag)

    def ts(self, out, in0, s1, op0, s2=None, op1=None, eng="vector", accum_out=None, tag=""):
        r = [in0]
        if isinstance(s1, bass.AP) or hasattr(s1, "tensor"):
            r.append(s1)
        if s2 is not None and hasattr(s2, "tensor"):
            r.append(s2)
        w = [out] + ([accum_out] if accum_out is not None else [])
        if op1 is None:
            return self.op(eng, lambda e: e.tensor_scalar(out=out, in0=in0, scalar1=s1, scalar2=None, op0=op0, accum_out=accum_out),
                           reads=r, writes=w, tag=tag)
        return self.op(eng, lambda e: e.tensor_scalar(out=out, in0=in0, scalar1=s1, scalar2=s2, op0=op0, op1=op1, accum_out=accum_out),
                       reads=r, writes=w, tag=tag)

    def stt(self, out, in0, scalar, in1, op0, op1, eng="vector", tag=""):
        r = [in0, in1]
        if hasattr(scalar, "tensor"):
            r.append(scalar)
        return self.op(eng, lambda e: e.scalar_tensor_tensor(out=out, in0=in0, scalar=scalar, in1=in1, op0=op0, op1=op1),
                       reads=r, writes=[out], tag=tag)

    def act(self, out, in_, func, bias=0.0, scale=1.0, accum_out=None, tag=""):
        r = [in_]
        if hasattr(bias, "tensor"):
            r.append(bias)
        if hasattr(scale, "tensor"):
            r.append(scale)
        w = [out] + ([accum_out] if accum_out is not None else [])
        return self.op("scalar", lambda e: e.activation(out=out, in_=in_, func=func, bias=bias, scale=scale, accum_out=accum_out),
                       reads=r, writes=w, tag=tag)

    def copy(self, out, in_, eng="vector", tag=""):
        return self.op(eng, lambda e: e.tensor_copy(out=out, in_=in_),
                       reads=[in_], writes=[out], tag=tag)

    def memset(self, out, val, eng="vector", tag=""):
        return self.op(eng, lambda e: e.memset(out, val), reads=[], writes=[out], tag=tag)

    def mm(self, out, lhsT, rhs, start=True, stop=True, tag=""):
        rw = [out] if not start else []
        return self.op("tensor", lambda e: e.matmul(out, lhsT, rhs, start=start, stop=stop),
                       reads=[lhsT, rhs] + rw, writes=[out], tag=tag)

    def trans(self, out, in_, ident, tag=""):
        return self.op("tensor", lambda e: e.transpose(out, in_, ident),
                       reads=[in_, ident], writes=[out], tag=tag)

    def reduce(self, out, in_, op, absval=None, negate=None, eng="vector", tag=""):
        if op == ALU.add:
            return self.op(eng, lambda e: e.reduce_sum(out=out, in_=in_, axis=AX.X, apply_absolute_value=absval, negate=negate),
                           reads=[in_], writes=[out], tag=tag)
        if op == ALU.max:
            return self.op(eng, lambda e: e.reduce_max(out=out, in_=in_, axis=AX.X, apply_absolute_value=absval, negate=negate),
                           reads=[in_], writes=[out], tag=tag)
        return self.op(eng, lambda e: e.tensor_reduce(out=out, in_=in_, axis=AX.X, op=op, apply_absolute_value=absval, negate=negate),
                       reads=[in_], writes=[out], tag=tag)

    def warmup(self, junk, n=64):
        for i in range(n):
            self.ts(junk, junk, 1.0, ALU.mult, tag="warm")

    def barrier_outputs(self, outs):
        self.op("gpsimd", lambda e: e.nop(), reads=list(outs), writes=[], tag="fin")


class Ctx:

    def __init__(self, nc, stack):
        self.nc = nc
        self.stack = stack
        self.tiles = {}

    def sb(self, name, shape, dtype):
        t = self.stack.enter_context(self.nc.sbuf_tensor(name, shape, dtype))
        self.tiles[name] = t[:]
        return self.tiles[name]

    def ps(self, name, shape, dtype):
        t = self.stack.enter_context(self.nc.psum_tensor(name, shape, dtype))
        self.tiles[name] = t[:]
        return self.tiles[name]


B_PER_CORE = 2
T = 2048
D = 1024
NCORES = 8


def host_constants():
    c = {}
    c["ident"] = np.eye(128, dtype=np.float32)
    c["ident_bf"] = np.eye(128, dtype=ml_dtypes.bfloat16)
    m = np.zeros((128, 128), np.float32)
    m[np.triu_indices(128, 1)] = NEG
    c["diagmask"] = m
    return c


def emit_ln_tile(a, t, x_tile, xln_tm, scratch, extra_scale=1.0, eps=1e-5, width=1024):
    xsq, s1, s2 = scratch["xsq"], scratch["s1"], scratch["s2"]
    mv, rstd = scratch["mv"], scratch["rstd"]
    inv = 1.0 / width
    a.reduce(s1, x_tile, ALU.add)
    a.act(xsq, x_tile, AF.Square, accum_out=s2)
    a.ts(mv[:, 0:1], s1, inv, ALU.mult)
    a.ts(mv[:, 1:2], s2, inv, ALU.mult)
    a.ts(s1, mv[:, 0:1], mv[:, 0:1], ALU.mult)
    a.stt(rstd, s1, -1.0, mv[:, 1:2], ALU.mult, ALU.add)
    a.ts(rstd, rstd, eps, ALU.add)
    a.op("vector", lambda e: e.reciprocal(out=rstd, in_=rstd), reads=[rstd], writes=[rstd])
    a.act(rstd, rstd, AF.Sqrt, scale=1.0)
    if extra_scale != 1.0:
        a.ts(rstd, rstd, float(extra_scale), ALU.mult)
    a.ts(xln_tm, x_tile, mv[:, 0:1], ALU.subtract, rstd[:, 0:1], ALU.mult)


def emit_tm_to_fm(a, x_tm, out_fm_chunks, identb, pxp, nchunk=8):
    for c in range(nchunk):
        a.trans(pxp, x_tm[:, c * 128:(c + 1) * 128], identb)
        a.copy(out_fm_chunks[c], pxp)


def build_l0_ffn():
    hidden = 3072
    HC = hidden // 128
    NTOK = B_PER_CORE * T
    NT = NTOK // 128
    NT512 = NTOK // 512
    nc = bass.Bass("TRN2", target_bir_lowering=False)
    x = nc.dram_tensor("x", [NTOK, D], F32, kind="ExternalInput").ap()
    wgT = nc.dram_tensor("wgT", [D, hidden], I32, kind="ExternalInput").ap()
    wuT = nc.dram_tensor("wuT", [D, hidden], I32, kind="ExternalInput").ap()
    wdT = nc.dram_tensor("wdT", [hidden, D], I32, kind="ExternalInput").ap()
    ident_in = nc.dram_tensor("ident", [128, 128], BF16, kind="ExternalInput").ap()
    xo = nc.dram_tensor("xo", [NTOK, D], F32, kind="ExternalOutput").ap()

    from contextlib import ExitStack
    with ExitStack() as stack:
        ctx = Ctx(nc, stack)
        sb, ps = ctx.sb, ctx.ps
        junk = sb("junk", [128, 256], F32)
        identb = sb("identb", [128, 128], BF16)
        xtile = [sb(f"xt{i}", [128, D], F32) for i in range(2)]
        xln_tm = sb("xln_tm", [128, D], BF16)
        xlnf = sb("xlnf", [128, 8, T], BF16)
        scratch = {"xsq": sb("xsq", [128, D], BF16), "s1": sb("s1", [128, 1], F32),
                   "s2": sb("s2", [128, 1], F32), "mv": sb("mv", [128, 2], F32),
                   "rstd": sb("rstd", [128, 1], F32)}
        wgt = sb("wgt", [128, 8, hidden], BF16)
        wut = sb("wut", [128, 8, hidden], BF16)
        wdt = [sb(f"wdt{i}", [128, D], BF16) for i in range(2)]
        hbuf = sb("hbuf", [128, HC, 512], BF16)
        sg = [sb(f"sg{i}", [128, 512], BF16) for i in range(2)]
        pg = [ps(f"pg{i}", [128, 512], F32) for i in range(2)]
        pu = [ps(f"pu{i}", [128, 512], F32) for i in range(2)]
        pd = ps("pd", [128, D], F32)
        pxp = ps("pxp", [128, 128], BF16)
        a = A(nc)

        a.dma("syq", identb, ident_in)
        for c in range(8):
            a.dma("gpq", wgt[:, c, :], wgT[c * 128:(c + 1) * 128, :])
            a.dma("gpq", wut[:, c, :], wuT[c * 128:(c + 1) * 128, :])
        a.warmup(junk)

        NTSEQ = T // 128
        NT512SEQ = T // 512
        for b in range(B_PER_CORE):
            base = b * T
            for j in range(NTSEQ):
                xt = xtile[j % 2]
                r0 = base + j * 128
                a.dma("syq", xt, x[r0:r0 + 128, :])
                emit_ln_tile(a, None, xt, xln_tm, scratch, extra_scale=1.0 / 127.0)
                emit_tm_to_fm(a, xln_tm, [xlnf[:, c, j * 128:(j + 1) * 128] for c in range(8)],
                              identb, pxp)
            for tt in range(NT512SEQ):
                tsl = slice(tt * 512, (tt + 1) * 512)
                for hc in range(HC):
                    g = pg[hc % 2]
                    u = pu[hc % 2]
                    sgt = sg[hc % 2]
                    for c in range(8):
                        a.mm(g, wgt[:, c, hc * 128:(hc + 1) * 128], xlnf[:, c, tsl],
                             start=(c == 0), stop=(c == 7))
                    for c in range(8):
                        a.mm(u, wut[:, c, hc * 128:(hc + 1) * 128], xlnf[:, c, tsl],
                             start=(c == 0), stop=(c == 7))
                    a.act(sgt, g, AF.Silu)
                    a.stt(hbuf[:, hc, :], sgt, 1.0 / 127.0, u, ALU.mult, ALU.mult)
                for st4 in range(4):
                    j = tt * 4 + st4
                    for hc in range(HC):
                        w = wdt[hc % 2]
                        a.dma("gpq", w, wdT[hc * 128:(hc + 1) * 128, :])
                        for half in range(2):
                            a.mm(pd[:, half * 512:(half + 1) * 512],
                                 hbuf[:, hc, st4 * 128:(st4 + 1) * 128],
                                 w[:, half * 512:(half + 1) * 512],
                                 start=(hc == 0), stop=(hc == HC - 1))
                    xt = xtile[j % 2]
                    r0 = base + j * 128
                    a.dma("syq", xt, x[r0:r0 + 128, :])
                    a.tt(xt, xt, pd, ALU.add)
                    a.dma("gpq", xo[r0:r0 + 128, :], xt)

        a.barrier_outputs([xo])
        a.emit(stack)
    return nc


def _shard(x, core):
    return np.ascontiguousarray(x.reshape(NCORES, -1, *x.shape[1:])[core])


def run_l0_ffn(x_full, w_gate, w_up, w_down):
    w_gate, w_up, w_down = [np.ascontiguousarray(np.asarray(w, np.int32)) for w in (w_gate, w_up, w_down)]
    """x_full [16, 2048, 1024] fp32; weights int32 as given by the reference."""
    consts = host_constants()
    nc = build_l0_ffn()
    wgT = np.ascontiguousarray(w_gate.T)
    wuT = np.ascontiguousarray(w_up.T)
    wdT = np.ascontiguousarray(w_down.T)
    in_maps = []
    for c in range(NCORES):
        in_maps.append({
            "x": x_full.reshape(16 * T, D)[c * B_PER_CORE * T:(c + 1) * B_PER_CORE * T],
            "wgT": wgT, "wuT": wuT, "wdT": wdT,
            "ident": consts["ident_bf"],
        })
    res = run_bass_kernel_spmd(nc, in_maps, core_ids=list(range(NCORES)))
    out = np.concatenate([res.results[c]["xo"] for c in range(NCORES)], axis=0)
    return out.reshape(16, T, D)


def build_attn(variant):
    NTOK = B_PER_CORE * T
    nc = bass.Bass("TRN2", target_bir_lowering=False)
    x = nc.dram_tensor("x", [NTOK, D], F32, kind="ExternalInput").ap()
    ident_in = nc.dram_tensor("ident", [128, 128], BF16, kind="ExternalInput").ap()
    diag_in = nc.dram_tensor("diagmask", [128, 128], F32, kind="ExternalInput").ap()
    xo = nc.dram_tensor("xo", [NTOK, D], F32, kind="ExternalOutput").ap()
    if variant == "b":
        wqT = nc.dram_tensor("wqT", [D, D], F32, kind="ExternalInput").ap()
        wkT = nc.dram_tensor("wkT", [D, D], F32, kind="ExternalInput").ap()
        wvT = nc.dram_tensor("wvT", [D, D], F32, kind="ExternalInput").ap()
        woT = nc.dram_tensor("woT", [D, D], F32, kind="ExternalInput").ap()
        alibik = nc.dram_tensor("alibik", [16, T], BF16, kind="ExternalInput").ap()
    else:
        wqT = nc.dram_tensor("wqT", [D, D], F32, kind="ExternalInput").ap()
        wqTs = nc.dram_tensor("wqTs", [D, D], F32, kind="ExternalInput").ap()
        wkT = nc.dram_tensor("wkT", [D, 192], F32, kind="ExternalInput").ap()
        wvT = nc.dram_tensor("wvT", [D, 192], F32, kind="ExternalInput").ap()
        dks = nc.dram_tensor("dks", [192, D], F32, kind="ExternalInput").ap()
        dkss = nc.dram_tensor("dkss", [192, D], F32, kind="ExternalInput").ap()
        dkm = nc.dram_tensor("dkm", [192, D], BF16, kind="ExternalInput").ap()
        dkms = nc.dram_tensor("dkms", [192, D], BF16, kind="ExternalInput").ap()
        dv = nc.dram_tensor("dv", [192, D], F32, kind="ExternalInput").ap()
        woT = nc.dram_tensor("woT", [D, D], F32, kind="ExternalInput").ap()
        c1 = nc.dram_tensor("c1", [64, T], BF16, kind="ExternalInput").ap()
        c2 = nc.dram_tensor("c2", [64, T], BF16, kind="ExternalInput").ap()

    from contextlib import ExitStack
    with ExitStack() as stack:
        ctx = Ctx(nc, stack)
        sb, ps = ctx.sb, ctx.ps
        junk = sb("junk", [128, 256], F32)
        identb = sb("identb", [128, 128], BF16)
        diagm = sb("diagm", [128, 128], F32)
        xtile = [sb("xt0", [128, D], F32)] * 2
        xln_tm = sb("xln_tm", [128, D], BF16)
        xlnf = sb("xlnf", [128, 8, T], BF16)
        scratch = {"xsq": sb("xsq", [128, D], BF16), "s1": sb("s1", [128, 1], F32),
                   "s2": sb("s2", [128, 1], F32), "mv": sb("mv", [128, 2], F32),
                   "rstd": sb("rstd", [128, 1], F32)}
        wq = sb("wq", [128, 8, D], BF16)
        wo = sb("wo", [128, 8, D], BF16)
        if variant == "b":
            wk = sb("wk", [128, 8, D], BF16)
            wv = sb("wv", [128, 8, D], BF16)

        else:
            wqs = sb("wqs", [128, 8, D], BF16)
            wkt = sb("wkt", [128, 8, 192], BF16)
            wvt = sb("wvt", [128, 8, 192], BF16)
            dkst = sb("dkst", [128, 2, D], BF16)
            dksst = sb("dksst", [128, 2, D], BF16)
            dvt = sb("dvt", [128, 2, D], BF16)
            cro = sb("cro", [128, T], BF16)
            zk = sb("zk", [128, 2, T], BF16)
            xv = None

            t1 = sb("t1", [128, 512], F32)
            t2 = sb("t2", [128, 512], F32)
        vtm = sb("vtm", [128, 16, D], BF16)
        oall = sb("oall", [128, 16, D], BF16)
        if variant == "a":
            xv = oall[:, 0:4, :].rearrange("p a b -> p (a b)").rearrange("p (c b) -> p c b", c=2)
        qs = sb("qs", [128, T], BF16)
        ks = sb("ks", [128, T], BF16)
        pbuf = sb("pbuf", [128, T], BF16)
        pts = sb("pts", [128, T], BF16)
        rowmax = sb("rowmax", [128, 1], F32)
        rowsum = sb("rowsum", [128, 1], F32)
        nbias = sb("nbias", [128, 1], F32)
        recl = sb("recl", [128, 1], F32)
        of = sb("of", [128, 8, 128], BF16)
        PSA = ps("PSA", [128, 2048], F32)
        PSB = ps("PSB", [128, 2048], BF16)
        PSC = ps("PSC", [128, 128], F32)
        a = A(nc)

        a.dma("syq", identb, ident_in)
        a.dma("syq", diagm, diag_in)
        for c in range(8):
            a.dma("gpq", wq[:, c, :], wqT[c * 128:(c + 1) * 128, :])
            a.dma("gpq", wo[:, c, :], woT[c * 128:(c + 1) * 128, :])
        if variant == "b":
            for c in range(8):
                a.dma("gpq", wk[:, c, :], wkT[c * 128:(c + 1) * 128, :])
                a.dma("gpq", wv[:, c, :], wvT[c * 128:(c + 1) * 128, :])

        else:
            for c in range(8):
                a.dma("gpq", wqs[:, c, :], wqTs[c * 128:(c + 1) * 128, :])
                a.dma("gpq", wkt[:, c, :192], wkT[c * 128:(c + 1) * 128, :])
                a.dma("gpq", wvt[:, c, :192], wvT[c * 128:(c + 1) * 128, :])
            a.dma("syq", cro[0:64, :], c1)
            a.dma("syq", cro[64:128, :], c2)
            a.dma("gpq", dkst[:, 0, :], dks[0:128, :])
            a.dma("gpq", dkst[0:64, 1, :], dks[128:192, :])
            a.dma("gpq", dksst[:, 0, :], dkss[0:128, :])
            a.dma("gpq", dksst[0:64, 1, :], dkss[128:192, :])
            mskt = vtm[:, 0:2, :]
            mskst = vtm[:, 2:4, :]
            a.dma("syq", mskt[:, 0, :], dkm[0:128, :])
            a.dma("syq", mskt[0:64, 1, :], dkm[128:192, :])
            a.dma("syq", mskst[:, 0, :], dkms[0:128, :])
            a.dma("syq", mskst[0:64, 1, :], dkms[128:192, :])
            a.tt(dkst[:, 0, :], dkst[:, 0, :], mskt[:, 0, :], ALU.mult)
            a.tt(dkst[0:64, 1, :], dkst[0:64, 1, :], mskt[0:64, 1, :], ALU.mult)
            a.tt(dksst[:, 0, :], dksst[:, 0, :], mskst[:, 0, :], ALU.mult)
            a.tt(dksst[0:64, 1, :], dksst[0:64, 1, :], mskst[0:64, 1, :], ALU.mult)
            a.dma("gpq", dvt[:, 0, :], dv[0:128, :])
            a.dma("gpq", dvt[0:64, 1, :], dv[128:192, :])
        a.warmup(junk)
        a.act(junk[:, 0:256], junk[:, 0:256], AF.Exp)
        a.act(junk[:, 0:256], junk[:, 0:256], AF.Square)

        HD = 64
        scale = 1.0 / math.sqrt(HD)
        for b in range(B_PER_CORE):
            base = b * T
            for j in range(T // 128):
                xt = xtile[j % 2]
                a.dma("syq", xt, x[base + j * 128:base + (j + 1) * 128, :])
                emit_ln_tile(a, None, xt, xln_tm, scratch)
                emit_tm_to_fm(a, xln_tm, [xlnf[:, c, j * 128:(j + 1) * 128] for c in range(8)],
                              identb, PSB[:, 0:128])
            if variant == "a":
                for tt in range(4):
                    tsl = slice(tt * 512, (tt + 1) * 512)
                    pz = PSA[:, 0:512]
                    pz2 = PSA[:, 512:1024]
                    for (wt, dst) in ((wkt, zk), (wvt, xv)):
                        for c in range(8):
                            a.mm(pz, wt[:, c, 0:128], xlnf[:, c, tsl], start=(c == 0), stop=(c == 7))
                        a.copy(dst[:, 0, tsl], pz)
                        for c in range(8):
                            a.mm(pz2[0:64, :], wt[:, c, 128:192], xlnf[:, c, tsl], start=(c == 0), stop=(c == 7))
                        a.copy(dst[0:64, 1, tsl], pz2[0:64, :])
            for j in range(T // 128):
                pv = PSA[:, 1024:2048]
                if variant == "b":
                    for hf in range(2):
                        for c in range(8):
                            a.mm(pv[:, hf * 512:(hf + 1) * 512],
                                 xlnf[:, c, j * 128:(j + 1) * 128],
                                 wv[:, c, hf * 512:(hf + 1) * 512],
                                 start=(c == 0), stop=(c == 7))
                else:
                    for hf in range(2):
                        a.mm(pv[:, hf * 512:(hf + 1) * 512], xv[:, 0, j * 128:(j + 1) * 128],
                             dvt[:, 0, hf * 512:(hf + 1) * 512], start=True, stop=False)
                        a.mm(pv[:, hf * 512:(hf + 1) * 512], xv[0:64, 1, j * 128:(j + 1) * 128],
                             dvt[0:64, 1, hf * 512:(hf + 1) * 512], start=False, stop=True)
                a.copy(vtm[:, j, :], pv)
            for h in range(16):
                hsl = slice(h * 64, (h + 1) * 64)
                for tt in range(4):
                    tsl = slice(tt * 512, (tt + 1) * 512)
                    pq = PSA[:, 0:512]
                    pk = PSA[:, 512:1024]
                    if variant == "b":
                        for c in range(8):
                            a.mm(pq[0:64, :], wq[:, c, hsl], xlnf[:, c, tsl], start=(c == 0), stop=(c == 7))
                        a.copy(qs[0:64, tsl], pq[0:64, :])
                        for c in range(8):
                            a.mm(pk[0:64, :], wk[:, c, hsl], xlnf[:, c, tsl], start=(c == 0), stop=(c == 7))
                        a.copy(ks[0:64, tsl], pk[0:64, :])
                    else:
                        pq2 = PSA[:, 1024:1536]
                        for c in range(8):
                            a.mm(pq[0:64, :], wq[:, c, hsl], xlnf[:, c, tsl], start=(c == 0), stop=(c == 7))
                        for c in range(8):
                            a.mm(pq2[0:64, :], wqs[:, c, hsl], xlnf[:, c, tsl], start=(c == 0), stop=(c == 7))
                        a.tt(t1[0:64, :], pq[0:64, :], cro[0:64, tsl], ALU.mult)
                        a.tt(t2[0:64, :], pq2[0:64, :], cro[64:128, tsl], ALU.mult)
                        a.tt(qs[0:64, tsl], t1[0:64, :], t2[0:64, :], ALU.add)
                        a.mm(pq[0:64, :], dkst[:, 0, hsl], zk[:, 0, tsl], start=True, stop=False)
                        a.mm(pq[0:64, :], dkst[0:64, 1, hsl], zk[0:64, 1, tsl], start=False, stop=True)
                        a.mm(pq2[0:64, :], dksst[:, 0, hsl], zk[:, 0, tsl], start=True, stop=False)
                        a.mm(pq2[0:64, :], dksst[0:64, 1, hsl], zk[0:64, 1, tsl], start=False, stop=True)
                        a.tt(t1[0:64, :], pq[0:64, :], cro[0:64, tsl], ALU.mult)
                        a.tt(t2[0:64, :], pq2[0:64, :], cro[64:128, tsl], ALU.mult)
                        a.tt(ks[0:64, tsl], t1[0:64, :], t2[0:64, :], ALU.add)
                if variant == "b":
                    a.memset(qs[64:65, :], 1.0)
                    a.dma("syq", ks[64:65, :], alibik[h:h + 1, :])
                    KR = 65
                else:
                    KR = 64
                for qt in range(T // 128):
                    L = (qt + 1) * 128
                    S = PSA[:, 0:L]
                    for kb in range((L + 511) // 512):
                        n = min(512, L - kb * 512)
                        a.mm(PSA[:, kb * 512:kb * 512 + n], qs[0:KR, qt * 128:(qt + 1) * 128],
                             ks[0:KR, kb * 512:kb * 512 + n], start=True, stop=True)
                    a.tt(PSA[:, L - 128:L], PSA[:, L - 128:L], diagm, ALU.add)
                    a.reduce(rowmax, S, ALU.max)
                    a.ts(nbias, rowmax, -scale, ALU.mult)
                    a.act(pbuf[:, 0:L], S, AF.Exp, bias=nbias[:, 0:1], scale=scale,
                          accum_out=rowsum)
                    a.op("vector", lambda e, _r=recl, _s=rowsum: e.reciprocal(out=_r, in_=_s),
                         reads=[rowsum], writes=[recl])
                    for kb in range(L // 128):
                        a.trans(PSB[:, kb * 128:(kb + 1) * 128], pbuf[:, kb * 128:(kb + 1) * 128], identb)
                    a.copy(pts[:, 0:L], PSB[:, 0:L])
                    for kb in range(L // 128):
                        a.mm(PSC[:, 0:64], pts[:, kb * 128:(kb + 1) * 128],
                             vtm[:, kb, hsl], start=(kb == 0), stop=(kb == L // 128 - 1))
                    a.ts(oall[:, qt, hsl], PSC[:, 0:64], recl[:, 0:1], ALU.mult)
            for j in range(T // 128):
                emit_tm_to_fm(a, oall[:, j, :], [of[:, c, :] for c in range(8)], identb,
                              PSB[:, 0:128])
                pd = PSA[:, 0:1024]
                for c in range(8):
                    for hf in range(2):
                        a.mm(pd[:, hf * 512:(hf + 1) * 512], of[:, c, :],
                             wo[:, c, hf * 512:(hf + 1) * 512],
                             start=(c == 0), stop=(c == 7))
                xt = xtile[j % 2]
                a.dma("scq", xt, x[base + j * 128:base + (j + 1) * 128, :])
                a.tt(xt, xt, pd, ALU.add)
                a.dma("gpq", xo[base + j * 128:base + (j + 1) * 128, :], xt)
        a.barrier_outputs([xo])
        a.emit(stack)
    return nc


def rope_tables_interleaved(hd=64, base=10000.0):
    half = hd // 2
    freq = 1.0 / base ** (np.arange(half, dtype=np.float32) / half)
    ang = np.outer(np.arange(T, dtype=np.float32), freq)
    cos = np.cos(ang).astype(np.float32).T
    sin = np.sin(ang).astype(np.float32).T
    C1 = np.zeros((hd, T), np.float32)
    C2 = np.zeros((hd, T), np.float32)
    C1[0::2] = cos
    C1[1::2] = cos
    C2[0::2] = -sin
    C2[1::2] = sin
    return C1, C2


def host_attn_a_decoders(key_decoder, value_decoder):
    F_, H_, KR_ = 32, 16, 3
    dk = np.zeros((192, D), np.float32)
    for cp in range(2):
        for c in range(2):
            cc = c ^ cp
            dk[np.arange(F_)[:, None, None] * 6 + np.arange(KR_)[None, :, None] * 2 + cp,
               np.arange(H_)[None, None, :] * 64 + 2 * np.arange(F_)[:, None, None] + c] =                 np.transpose(key_decoder[:, :, :, cc], (0, 2, 1))
    dks = dk.reshape(192, 16, 32, 2)[:, :, :, ::-1].reshape(192, D)
    row_odd = (np.arange(192) % 2 == 1)
    col_even = (np.arange(D) % 2 == 0)
    m = np.ones((192, D), np.float32)
    m[np.ix_(row_odd, col_even)] = -1.0
    ms = np.ones((192, D), np.float32)
    ms[np.ix_(row_odd, ~col_even)] = -1.0
    dv = np.zeros((192, D), np.float32)
    dv[:, :] = np.transpose(value_decoder, (1, 0, 2)).reshape(192, D)
    return (np.ascontiguousarray(dk), np.ascontiguousarray(dks),
            m.astype(ml_dtypes.bfloat16), ms.astype(ml_dtypes.bfloat16),
            np.ascontiguousarray(dv))


def run_attn_a(x_full, Wq, Wk, Wv, key_decoder, value_decoder, Wproj):
    Wq, Wk, Wv, Wproj = [np.asarray(w, np.float32) for w in (Wq, Wk, Wv, Wproj)]
    consts = host_constants()
    nc = build_attn("a")
    perm = np.arange(D) ^ 1
    dk, dks_sw, m, ms, dv = host_attn_a_decoders(key_decoder, value_decoder)
    C1, C2 = rope_tables_interleaved()
    in_common = {
        "wqT": np.ascontiguousarray(Wq), "wqTs": np.ascontiguousarray(Wq[:, perm]),
        "wkT": np.ascontiguousarray(Wk), "wvT": np.ascontiguousarray(Wv),
        "dks": dk, "dkss": dks_sw, "dkm": m, "dkms": ms, "dv": dv,
        "woT": np.ascontiguousarray(Wproj),
        "c1": C1.astype(ml_dtypes.bfloat16), "c2": C2.astype(ml_dtypes.bfloat16),
        "ident": consts["ident_bf"], "diagmask": consts["diagmask"],
    }
    in_maps = []
    for c in range(NCORES):
        mm_ = dict(in_common)
        mm_["x"] = x_full.reshape(16 * T, D)[c * B_PER_CORE * T:(c + 1) * B_PER_CORE * T]
        in_maps.append(mm_)
    res = run_bass_kernel_spmd(nc, in_maps, core_ids=list(range(NCORES)))
    out = np.concatenate([res.results[c]["xo"] for c in range(NCORES)], axis=0)
    return out.reshape(16, T, D)


def host_alibi(scale=8.0):
    slopes = np.array([2.0 ** (-(h + 1) / 2.0) for h in range(16)], np.float32)
    j = np.arange(T, dtype=np.float32)
    return (-slopes[:, None] * j[None, :] * scale).astype(ml_dtypes.bfloat16)


def run_attn_b(x_full, Wq, Wk, Wv, Wo):
    Wq, Wk, Wv, Wo = [np.asarray(w, np.float32) for w in (Wq, Wk, Wv, Wo)]
    consts = host_constants()
    nc = build_attn("b")
    in_common = {
        "wqT": np.ascontiguousarray(Wq.T), "wkT": np.ascontiguousarray(Wk.T),
        "wvT": np.ascontiguousarray(Wv.T), "woT": np.ascontiguousarray(Wo.T),
        "ident": consts["ident_bf"], "diagmask": consts["diagmask"],
        "alibik": host_alibi(),
    }
    in_maps = []
    for c in range(NCORES):
        m = dict(in_common)
        m["x"] = x_full.reshape(16 * T, D)[c * B_PER_CORE * T:(c + 1) * B_PER_CORE * T]
        in_maps.append(m)
    res = run_bass_kernel_spmd(nc, in_maps, core_ids=list(range(NCORES)))
    out = np.concatenate([res.results[c]["xo"] for c in range(NCORES)], axis=0)
    return out.reshape(16, T, D)


def emit_fp4_quant(a, src, dst, q, nb, width):
    absx, t1i, cq = q["absx"], q["t1i"], q["cq"]
    bmax, bscale, binv = q["bmax"], q["bscale"], q["binv"]
    w3 = (src, absx, AF.Abs)
    a.act(absx[:, 0:width], src, AF.Abs)
    a.op("vector", lambda e: e.tensor_reduce(
        out=bmax[:, 0:nb].unsqueeze(2), in_=absx[:, 0:width].rearrange("p (b w) -> p b w", w=16),
        axis=AX.X, op=ALU.max), reads=[absx], writes=[bmax])
    a.ts(bscale[:, 0:nb], bmax[:, 0:nb], 1.0 / 6.0, ALU.mult, 1e-12, ALU.max)
    a.op("vector", lambda e: e.reciprocal(out=binv[:, 0:nb], in_=bscale[:, 0:nb]),
         reads=[bscale], writes=[binv])
    bc = binv[:, 0:nb].unsqueeze(2).broadcast_to([128, nb, 16])
    a.op("vector", lambda e: e.tensor_tensor(
        out=absx[:, 0:width].rearrange("p (b w) -> p b w", w=16),
        in0=absx[:, 0:width].rearrange("p (b w) -> p b w", w=16), in1=bc, op=ALU.mult),
        reads=[absx, binv], writes=[absx])
    a.ts(t1i[:, 0:width], absx[:, 0:width], 4.0, ALU.min, 2.0, ALU.subtract, eng="gpsimd")
    a.ts(cq[:, 0:width], absx[:, 0:width], 5.0, ALU.is_gt, 2.0, ALU.mult)
    a.stt(cq[:, 0:width], t1i[:, 0:width], 0.0, cq[:, 0:width], ALU.max, ALU.add)
    a.ts(t1i[:, 0:width], absx[:, 0:width], 2.0, ALU.min, 2.0, ALU.mult, eng="gpsimd")
    a.stt(cq[:, 0:width], t1i[:, 0:width], 0.5, cq[:, 0:width], ALU.mult, ALU.add)
    a.act(absx[:, 0:width], src, AF.Sign)
    bs = bscale[:, 0:nb].unsqueeze(2).broadcast_to([128, nb, 16])
    a.op("vector", lambda e: e.tensor_tensor(
        out=absx[:, 0:width].rearrange("p (b w) -> p b w", w=16),
        in0=absx[:, 0:width].rearrange("p (b w) -> p b w", w=16), in1=bs, op=ALU.mult),
        reads=[absx, bscale], writes=[absx])
    a.tt(dst, cq[:, 0:width], absx[:, 0:width], ALU.mult)


def build_l1_ffn():
    hidden = 2816
    HC = hidden // 128
    NB_D = D // 16
    NB_H = hidden // 16
    NTOK = B_PER_CORE * T
    nc = bass.Bass("TRN2", target_bir_lowering=False)
    x = nc.dram_tensor("x", [NTOK, D], F32, kind="ExternalInput").ap()
    wg_in = nc.dram_tensor("wg", [hidden, D], F32, kind="ExternalInput").ap()
    wu_in = nc.dram_tensor("wu", [hidden, D], F32, kind="ExternalInput").ap()
    wd_in = nc.dram_tensor("wd", [D, hidden], F32, kind="ExternalInput").ap()
    ident_in = nc.dram_tensor("ident", [128, 128], BF16, kind="ExternalInput").ap()
    xo = nc.dram_tensor("xo", [NTOK, D], F32, kind="ExternalOutput").ap()

    from contextlib import ExitStack
    with ExitStack() as stack:
        ctx = Ctx(nc, stack)
        sb, ps = ctx.sb, ctx.ps
        junk = sb("junk", [128, 256], F32)
        identb = sb("identb", [128, 128], BF16)
        xtile = sb("xt0", [128, D], F32)
        xln_tm = sb("xln_tm", [128, D], F32)
        scratch = {"xsq": sb("xsq", [128, D], BF16), "s1": sb("s1", [128, 1], F32),
                   "s2": sb("s2", [128, 1], F32), "mv": sb("mv", [128, 2], F32),
                   "rstd": sb("rstd", [128, 1], F32)}
        q = {"absx": sb("absx", [128, hidden], F32),
             "t1i": sb("t1i", [128, hidden], mybir.dt.int16),
             
             "cq": sb("cq", [128, hidden], F32),
             "bmax": sb("bmax", [128, NB_H], F32), "bscale": sb("bscale", [128, NB_H], F32),
             "binv": sb("binv", [128, NB_H], F32)}
        wqtm = sb("wqtm", [128, hidden], BF16)
        wgq = sb("wgq", [128, 8, hidden], BF16)
        wuq = sb("wuq", [128, 8, hidden], BF16)
        wdq = sb("wdq", [128, HC, D], BF16)
        xq_tm = sb("xq_tm", [128, D], BF16)
        xqf = sb("xqf", [128, 8, 128], BF16)
        htile = sb("htile", [128, hidden], F32)
        hq_tm = sb("hq_tm", [128, hidden], BF16)
        hqf = sb("hqf", [128, HC, 128], BF16)
        PSA = ps("PSA", [128, 2048], F32)
        PSB = ps("PSB", [128, 256], BF16)
        a = A(nc)

        a.dma("syq", identb, ident_in)
        a.warmup(junk)
        a.act(junk, junk, AF.Exp)
        a.act(junk, junk, AF.Abs)
        a.act(junk, junk, AF.Sign)
        a.act(junk, junk, AF.Square)
        a.act(junk, junk, AF.Silu)

        for (w_in, dstfm, nrows, wwidth, nbw) in (
                (wg_in, wgq, hidden, D, NB_D), (wu_in, wuq, hidden, D, NB_D),
                (wd_in, wdq, D, hidden, NB_H)):
            RC = nrows // 128
            for rc in range(RC):
                src = htile[:, 0:wwidth]
                a.dma("syq", src, w_in[rc * 128:(rc + 1) * 128, :])
                emit_fp4_quant(a, src, wqtm[:, 0:wwidth], q, nbw, wwidth)
                for c in range(wwidth // 128):
                    a.trans(PSB[:, 0:128], wqtm[:, c * 128:(c + 1) * 128], identb)
                    a.copy(dstfm[:, c, rc * 128:(rc + 1) * 128], PSB[:, 0:128])

        ISL = [(i * 512, min(512, hidden - i * 512)) for i in range((hidden + 511) // 512)]
        for j in range(NTOK // 128):
            a.dma("syq", xtile, x[j * 128:(j + 1) * 128, :])
            emit_ln_tile(a, None, xtile, xln_tm, scratch)
            emit_fp4_quant(a, xln_tm, xq_tm, q, NB_D, D)
            for c in range(8):
                a.trans(PSB[:, 0:128], xq_tm[:, c * 128:(c + 1) * 128], identb)
                a.copy(xqf[:, c, :], PSB[:, 0:128])
            for (i0, n) in ISL:
                pg = PSA[:, 0:512]
                pu = PSA[:, 512:1024]
                for c in range(8):
                    a.mm(pg[:, 0:n], wgq[:, c, i0:i0 + n], xqf[:, c, :].transpose([0, 1]) if False else xqf[:, c, :],
                         start=(c == 0), stop=(c == 7)) if False else None
                for c in range(8):
                    a.mm(pg[:, 0:n], xqf[:, c, :], wgq[:, c, i0:i0 + n],
                         start=(c == 0), stop=(c == 7))
                for c in range(8):
                    a.mm(pu[:, 0:n], xqf[:, c, :], wuq[:, c, i0:i0 + n],
                         start=(c == 0), stop=(c == 7))
                a.act(htile[:, i0:i0 + n], pg[:, 0:n], AF.Silu)
                a.tt(htile[:, i0:i0 + n], htile[:, i0:i0 + n], pu[:, 0:n], ALU.mult)
            emit_fp4_quant(a, htile, hq_tm, q, NB_H, hidden)
            for c in range(HC):
                a.trans(PSB[:, 0:128], hq_tm[:, c * 128:(c + 1) * 128], identb)
                a.copy(hqf[:, c, :], PSB[:, 0:128])
            pd = PSA[:, 1024:2048]
            for hf in range(2):
                for c in range(HC):
                    a.mm(pd[:, hf * 512:(hf + 1) * 512], hqf[:, c, :],
                         wdq[:, c, hf * 512:(hf + 1) * 512],
                         start=(c == 0), stop=(c == HC - 1))
            a.tt(xtile, xtile, pd[:, 0:1024], ALU.add)
            a.dma("gpq", xo[j * 128:(j + 1) * 128, :], xtile)
        a.barrier_outputs([xo])
        a.emit(stack)
    return nc


def run_l1_ffn(x_full, w_gate, w_up, w_down):
    consts = host_constants()
    nc = build_l1_ffn()
    in_common = {"wg": np.ascontiguousarray(w_gate), "wu": np.ascontiguousarray(w_up),
                 "wd": np.ascontiguousarray(w_down), "ident": consts["ident_bf"]}
    in_maps = []
    for c in range(NCORES):
        m = dict(in_common)
        m["x"] = x_full.reshape(16 * T, D)[c * B_PER_CORE * T:(c + 1) * B_PER_CORE * T]
        in_maps.append(m)
    res = run_bass_kernel_spmd(nc, in_maps, core_ids=list(range(NCORES)))
    out = np.concatenate([res.results[c]["xo"] for c in range(NCORES)], axis=0)
    return out.reshape(16, T, D)


def build_mamba():
    E = 2048
    NSTATE = 16
    TC = 512
    NTC = T // TC
    NTOK = B_PER_CORE * T
    nc = bass.Bass("TRN2", target_bir_lowering=False)
    x = nc.dram_tensor("x", [NTOK, D], F32, kind="ExternalInput").ap()
    winT = nc.dram_tensor("winT", [D, 2 * E], F32, kind="ExternalInput").ap()
    wxT = nc.dram_tensor("wxT", [E, 96], F32, kind="ExternalInput").ap()
    wdtT = nc.dram_tensor("wdtT", [64, E], F32, kind="ExternalInput").ap()
    woutT = nc.dram_tensor("woutT", [E, D], F32, kind="ExternalInput").ap()
    convr = nc.dram_tensor("convr", [128, 64], F32, kind="ExternalInput").ap()
    convb = nc.dram_tensor("convb", [128, 16], F32, kind="ExternalInput").ap()
    bdt = nc.dram_tensor("bdt", [128, 16], F32, kind="ExternalInput").ap()
    dprm = nc.dram_tensor("dprm", [128, 16], F32, kind="ExternalInput").ap()
    alr = nc.dram_tensor("alr", [128, 256], F32, kind="ExternalInput").ap()
    sel_in = nc.dram_tensor("sel", [128, 16 * 128], BF16, kind="ExternalInput").ap()
    selt_in = nc.dram_tensor("selt", [128, 16 * 128], BF16, kind="ExternalInput").ap()
    selb_in = nc.dram_tensor("selb", [128, 256], BF16, kind="ExternalInput").ap()
    lnw_in = nc.dram_tensor("lnw", [128, D], F32, kind="ExternalInput").ap()
    lnb_in = nc.dram_tensor("lnb", [128, D], F32, kind="ExternalInput").ap()
    ident_in = nc.dram_tensor("ident", [128, 128], BF16, kind="ExternalInput").ap()
    xo = nc.dram_tensor("xo", [NTOK, D], F32, kind="ExternalOutput").ap()

    from contextlib import ExitStack
    with ExitStack() as stack:
        ctx = Ctx(nc, stack)
        sb, ps = ctx.sb, ctx.ps
        junk = sb("junk", [128, 256], F32)
        identb = sb("identb", [128, 128], BF16)
        win = sb("win", [128, 8, 2 * E], BF16)
        wx = sb("wx", [128, 16, 96], BF16)
        wdt = sb("wdt", [64, E], BF16)
        wout = [sb(f"wout{i}", [128, 2, D], BF16) for i in range(2)]
        conw = sb("conw", [128, 64], F32)
        conb = sb("conb", [128, 16], F32)
        bdtt = sb("bdtt", [128, 16], F32)
        dprt = sb("dprt", [128, 16], F32)
        alrt = sb("alrt", [128, 256], F32)
        nega = alrt
        self_ = sb("selr", [128, 16, 128], BF16)
        selt = sb("seltr", [128, 16, 128], BF16)
        selbc = sb("selbc", [128, 256], BF16)
        lnw = sb("lnwr", [128, D], F32)
        lnb = sb("lnbr", [128, D], F32)
        xtile = sb("xt0", [128, D], F32)
        xln_tm = sb("xln_tm", [128, D], BF16)
        xlnf = sb("xlnf", [128, 8, TC], BF16)
        scratch = {"xsq": sb("xsq", [128, D], BF16), "s1": sb("s1", [128, 1], F32),
                   "s2": sb("s2", [128, 1], F32), "mv": sb("mv", [128, 2], F32),
                   "rstd": sb("rstd", [128, 1], F32)}
        xs = sb("xs", [128, 16, TC + 3], BF16)
        uf = sb("uf", [128, 16, TC], BF16)
        zf = sb("zf", [128, 16, TC], BF16)
        yf = sb("yf", [128, 16, TC], BF16)
        cacc = sb("cacc", [128, TC], F32)
        reld = cacc
        xdb = sb("xdb", [128, TC], BF16)
        brep = sb("brep", [128, TC], F32)
        crep = sb("crep", [128, TC], F32)
        deltac = sb("deltac", [128, TC], BF16)
        absd = sb("absd", [128, TC], F32)
        duc = sb("duc", [128, TC], BF16)
        at = sb("at", [128, TC], F32)
        bt = sb("bt", [128, TC], F32)
        ht = sb("ht", [128, TC], F32)
        gt = sb("gt", [128, TC], BF16)
        y2 = sb("y2", [128, TC], F32)
        zs = sb("zs", [128, TC], F32)
        hstate = sb("hstate", [128, 256], F32)
        mo = sb("mo", [128, D], F32)
        PSA = ps("PSA", [128, 2048], F32)
        PSB = ps("PSB", [128, 256], BF16)
        a = A(nc)

        for c in range(8):
            a.dma("gpq", win[:, c, :], winT[c * 128:(c + 1) * 128, :])
        for c in range(16):
            a.dma("gpq", wx[:, c, :], wxT[c * 128:(c + 1) * 128, :])
        a.dma("gpq", wdt, wdtT)
        a.dma("syq", identb, ident_in)
        for (dst, src) in ((conw, convr), (conb, convb), (bdtt, bdt), (dprt, dprm),
                           (alrt, alr), (lnw, lnw_in), (lnb, lnb_in)):
            a.dma("syq", dst, src)
        a.dma("syq", self_.rearrange("p a b -> p (a b)"), sel_in)
        a.dma("syq", selt.rearrange("p a b -> p (a b)"), selt_in)
        a.dma("syq", selbc, selb_in)
        a.warmup(junk)
        a.act(junk, junk, AF.Exp)
        a.act(junk, junk, AF.Abs)
        a.act(junk, junk, AF.Square)
        a.act(junk, junk, AF.Silu)
        a.act(junk, junk, AF.Relu)
        a.act(junk, junk, AF.Ln)
        a.act(nega, alrt, AF.Exp)
        a.ts(nega, nega, -1.0, ALU.mult)

        for b in range(B_PER_CORE):
            base = b * T
            a.memset(hstate, 0.0)
            for tc in range(NTC):
                t0 = tc * TC
                for j in range(TC // 128):
                    a.dma("syq", xtile, x[base + t0 + j * 128:base + t0 + (j + 1) * 128, :])
                    emit_ln_tile(a, None, xtile, xln_tm, scratch)
                    emit_tm_to_fm(a, xln_tm, [xlnf[:, c, j * 128:(j + 1) * 128] for c in range(8)],
                                  identb, PSB[:, 0:128])
                if tc == 0:
                    for c in range(16):
                        a.memset(xs[:, c, 0:3], 0.0)
                for ec in range(16):
                    pz = PSA[:, 0:512]
                    for c in range(8):
                        a.mm(pz, win[:, c, ec * 128:(ec + 1) * 128], xlnf[:, c, :],
                             start=(c == 0), stop=(c == 7))
                    a.copy(xs[:, ec, 3:TC + 3], pz)
                    pz2 = PSA[:, 512:1024]
                    for c in range(8):
                        a.mm(pz2, win[:, c, E + ec * 128:E + (ec + 1) * 128], xlnf[:, c, :],
                             start=(c == 0), stop=(c == 7))
                    a.copy(zf[:, ec, :], pz2)
                for ec in range(16):
                    a.ts(cacc, xs[:, ec, 0:TC], conw[:, ec * 4 + 0:ec * 4 + 1], ALU.mult)
                    for k in (1, 2, 3):
                        a.stt(cacc, xs[:, ec, k:TC + k], conw[:, ec * 4 + k:ec * 4 + k + 1],
                              cacc, ALU.mult, ALU.add)
                    a.act(uf[:, ec, :], cacc, AF.Silu, bias=conb[:, ec:ec + 1], scale=1.0)
                for ec in range(16):
                    a.copy(xs[:, ec, 0:3], xs[:, ec, TC:TC + 3])
                pxd = PSA[:, 1024:1536]
                for c in range(16):
                    a.mm(pxd[0:96, :], wx[:, c, :], uf[:, c, :], start=(c == 0), stop=(c == 15))
                a.copy(xdb[0:96, :], pxd[0:96, :])
                a.mm(PSA[0:128, 0:512], selbc[0:96, 0:128], xdb[0:96, :], start=True, stop=True)
                a.copy(brep, PSA[:, 0:512])
                a.mm(PSA[0:128, 512:1024], selbc[0:96, 128:256], xdb[0:96, :], start=True, stop=True)
                a.copy(crep, PSA[:, 512:1024])
                for dc in range(16):
                    pdl = PSA[:, 1536:2048]
                    a.mm(pdl, wdt[:, dc * 128:(dc + 1) * 128], xdb[0:64, :], start=True, stop=True)
                    a.act(absd, pdl, AF.Abs, bias=bdtt[:, dc:dc + 1], scale=1.0)
                    a.act(reld, pdl, AF.Relu, bias=bdtt[:, dc:dc + 1], scale=1.0)
                    a.act(absd, absd, AF.Exp, scale=-1.0)
                    a.act(absd, absd, AF.Ln, bias=1.0, scale=1.0)
                    a.tt(deltac, reld, absd, ALU.add)
                    a.tt(duc, deltac, uf[:, dc, :], ALU.mult)
                    for sub in range(16):
                        ct = dc * 16 + sub
                        prd = PSA[:, 0:512]
                        a.mm(prd, self_[:, sub, :], deltac, start=True, stop=True)
                        a.act(at, prd, AF.Exp, bias=0.0, scale=nega[:, ct:ct + 1])
                        prd2 = PSA[:, 512:1024]
                        a.mm(prd2, self_[:, sub, :], duc, start=True, stop=True)
                        a.tt(bt, prd2, brep, ALU.mult)
                        a.op("vector", lambda e, _a=at, _b=bt, _h=ht, _i=hstate[:, ct:ct + 1]:
                             e.tensor_tensor_scan(out=_h, data0=_a, data1=_b, initial=_i,
                                                  op0=ALU.mult, op1=ALU.add),
                             reads=[at, bt, hstate[:, ct:ct + 1]], writes=[ht])
                        a.copy(hstate[:, ct:ct + 1], ht[:, TC - 1:TC])
                        a.tt(gt, ht, crep, ALU.mult)
                        a.mm(PSA[:, 1024:1536], selt[:, sub, :], gt,
                             start=(sub == 0), stop=(sub == 15))
                    a.stt(y2, uf[:, dc, :], dprt[:, dc:dc + 1], PSA[:, 1024:1536],
                          ALU.mult, ALU.add)
                    a.act(zs, zf[:, dc, :], AF.Silu)
                    a.tt(yf[:, dc, :], y2, zs, ALU.mult)
                for j in range(TC // 128):
                    pd = PSA[:, 0:1024]
                    for ec in range(16):
                        w = wout[ec % 2]
                        a.dma("gpq", w[:, 0, :], woutT[ec * 128:(ec + 1) * 128, :])
                        for hf in range(2):
                            a.mm(pd[:, hf * 512:(hf + 1) * 512], yf[:, ec, j * 128:(j + 1) * 128],
                                 w[:, 0, hf * 512:(hf + 1) * 512],
                                 start=(ec == 0), stop=(ec == 15))
                    r0 = base + t0 + j * 128
                    a.dma("scq", xtile, x[r0:r0 + 128, :])
                    emit_ln_tile(a, None, xtile, xln_tm, scratch)
                    a.op("vector", lambda e, _o=mo, _p=pd, _x=xln_tm:
                         e.tensor_tensor(out=_o, in0=_p, in1=_x, op=ALU.add),
                         reads=[pd, xln_tm], writes=[mo])
                    emit_ln_tile(a, None, mo, mo, scratch)
                    a.tt(mo, mo, lnw, ALU.mult)
                    a.tt(mo, mo, lnb, ALU.add)
                    a.tt(xtile, xtile, mo, ALU.add)
                    a.dma("gpq", xo[r0:r0 + 128, :], xtile)
        a.barrier_outputs([xo])
        a.emit(stack)
    return nc


def host_mamba_consts():
    sel = np.zeros((16, 128, 128), np.float32)
    for sub in range(16):
        for p in range(128):
            sel[sub, 8 * sub + p % 8, p] = 1.0
    selt = np.transpose(sel, (0, 2, 1))
    sel_h = np.ascontiguousarray(np.transpose(sel, (1, 0, 2)).reshape(128, 16 * 128)).astype(ml_dtypes.bfloat16)
    selt_h = np.ascontiguousarray(np.transpose(selt, (1, 0, 2)).reshape(128, 16 * 128)).astype(ml_dtypes.bfloat16)
    selbc = np.zeros((128, 256), np.float32)
    for p in range(128):
        selbc[64 + p // 8, p] = 1.0
        selbc[80 + p // 8, 128 + p] = 1.0
    return sel_h, selt_h, selbc.astype(ml_dtypes.bfloat16)


def run_mamba(x_full, W_in, conv_w, conv_b, W_x, W_dt, b_dt, A_log, D_param, W_out, ln_w, ln_b):
    consts = host_constants()
    nc = build_mamba()
    sel_h, selt_h, selbc = host_mamba_consts()
    alr = np.zeros((128, 256), np.float32)
    p = np.arange(128)
    for ct in range(256):
        alr[p, ct] = A_log[8 * ct + (p % 8), p // 8]
    in_common = {
        "winT": np.ascontiguousarray(W_in.T), "wxT": np.ascontiguousarray(W_x.T),
        "wdtT": np.ascontiguousarray(W_dt.T), "woutT": np.ascontiguousarray(W_out.T),
        "convr": np.ascontiguousarray(conv_w[:, 0, :].reshape(16, 128, 4).transpose(1, 0, 2).reshape(128, 64)),
        "convb": np.ascontiguousarray(conv_b.reshape(16, 128).T),
        "bdt": np.ascontiguousarray(b_dt.reshape(16, 128).T),
        "dprm": np.ascontiguousarray(D_param.reshape(16, 128).T),
        "alr": alr,
        "sel": sel_h, "selt": selt_h, "selb": selbc,
        "lnw": np.ascontiguousarray(np.tile(ln_w[None, :], (128, 1))),
        "lnb": np.ascontiguousarray(np.tile(ln_b[None, :], (128, 1))),
        "ident": consts["ident_bf"],
    }
    in_maps = []
    for c in range(NCORES):
        m = dict(in_common)
        m["x"] = x_full.reshape(16 * T, D)[c * B_PER_CORE * T:(c + 1) * B_PER_CORE * T]
        in_maps.append(m)
    res = run_bass_kernel_spmd(nc, in_maps, core_ids=list(range(NCORES)))
    out = np.concatenate([res.results[c]["xo"] for c in range(NCORES)], axis=0)
    return out.reshape(16, T, D)


def build_attn_pal():
    NTOK = B_PER_CORE * T
    nc = bass.Bass("TRN2", target_bir_lowering=False)
    x = nc.dram_tensor("x", [NTOK, D], F32, kind="ExternalInput").ap()
    wi_in = nc.dram_tensor("wi", [D, D], F32, kind="ExternalInput").ap()
    wp_in = nc.dram_tensor("wp", [D, D], F32, kind="ExternalInput").ap()
    wo_in = nc.dram_tensor("wo", [D, D], F32, kind="ExternalInput").ap()
    palT_in = nc.dram_tensor("palT", [256, D], F32, kind="ExternalInput").ap()
    w1g_in = nc.dram_tensor("w1g", [16, 64], F32, kind="ExternalInput").ap()
    w1d_in = nc.dram_tensor("w1d", [1, 64], F32, kind="ExternalInput").ap()
    b1_in = nc.dram_tensor("b1", [64, 1], F32, kind="ExternalInput").ap()
    w2_in = nc.dram_tensor("w2", [64, 64], F32, kind="ExternalInput").ap()
    b2_in = nc.dram_tensor("b2", [64, 1], F32, kind="ExternalInput").ap()
    wc0_in = nc.dram_tensor("wcc0", [64, 1], F32, kind="ExternalInput").ap()
    wc1_in = nc.dram_tensor("wcc1", [64, 1], F32, kind="ExternalInput").ap()
    bc_in = nc.dram_tensor("bc", [1, 2], F32, kind="ExternalInput").ap()
    wm_in = nc.dram_tensor("wm", [64, 1], F32, kind="ExternalInput").ap()
    bm_in = nc.dram_tensor("bm", [1, 1], F32, kind="ExternalInput").ap()
    cos_in = nc.dram_tensor("cosr", [T, 512], F32, kind="ExternalInput").ap()
    sin_in = nc.dram_tensor("sinr", [T, 512], F32, kind="ExternalInput").ap()
    ident_in = nc.dram_tensor("ident", [128, 128], BF16, kind="ExternalInput").ap()
    identf_in = nc.dram_tensor("identf", [128, 128], F32, kind="ExternalInput").ap()
    diag_in = nc.dram_tensor("diagmask", [128, 128], F32, kind="ExternalInput").ap()
    m0_in = nc.dram_tensor("m0", [16, T], BF16, kind="ExternalInput").ap()
    keepm_in = nc.dram_tensor("keepm", [128, 32], F32, kind="ExternalInput").ap()
    iota256_in = nc.dram_tensor("iota256", [128, 256], F32, kind="ExternalInput").ap()
    iota16r_in = nc.dram_tensor("iota16r", [128, 256], F32, kind="ExternalInput").ap()
    tcol_in = nc.dram_tensor("tcol", [128, 1], F32, kind="ExternalInput").ap()
    eqsel_in = nc.dram_tensor("eqsel", [128, 16], F32, kind="ExternalInput").ap()
    selg_in = nc.dram_tensor("selg", [128, 16 * 128], F32, kind="ExternalInput").ap()
    selx_in = nc.dram_tensor("selx", [128, 8 * 16], BF16, kind="ExternalInput").ap()
    xo = nc.dram_tensor("xo", [NTOK, D], F32, kind="ExternalOutput").ap()
    pnscr = nc.dram_tensor("pnscr", [B_PER_CORE * 4096, D], BF16, kind="Internal").ap()

    from contextlib import ExitStack
    with ExitStack() as stack:
        ctx = Ctx(nc, stack)
        sb, ps = ctx.sb, ctx.ps
        junk = sb("junk", [128, 256], F32)
        identb = sb("identb", [128, 128], BF16)
        identf = sb("identfs", [128, 128], F32)
        diagm = sb("diagm", [128, 128], F32)
        wi = sb("wit", [128, 8, D], BF16)
        wp = sb("wpt", [128, 8, D], BF16)
        wot = sb("wot", [128, 8, D], BF16)
        palt = sb("paltt", [128, 2, D], BF16)
        w1g = sb("w1gt", [16, 64], BF16)
        w1d = sb("w1dt", [1, 64], BF16)
        b1 = sb("b1t", [64, 1], F32)
        w2 = sb("w2t", [64, 64], BF16)
        b2 = sb("b2t", [64, 1], F32)
        wc0 = sb("wc0t", [64, 1], BF16)
        wc1 = sb("wc1t", [64, 1], BF16)
        bct = sb("bct", [1, 2], F32)
        wm = sb("wmt", [64, 1], BF16)
        bmt = sb("bmt", [1, 1], F32)
        m0 = sb("m0t", [16, T], BF16)
        keepm = sb("keepmt", [128, 32], F32)
        iota256 = sb("iota256t", [128, 256], F32)
        iota16r = sb("iota16rt", [128, 256], F32)
        tcol = sb("tcolt", [128, 1], F32)
        eqsel = sb("eqselt", [128, 16], F32)
        selg = sb("selgt", [128, 16, 128], F32)
        selx = sb("selxt", [128, 8, 16], BF16)
        xtile = sb("xt0", [128, D], F32)
        xln_tm = sb("xln_tm", [128, D], BF16)
        xlnf = sb("xlnf", [128, 8, 128], BF16)
        scratch = {"xsq": sb("xsq", [128, D], BF16), "s1": sb("s1", [128, 1], F32),
                   "s2": sb("s2", [128, 1], F32), "mv": sb("mv", [128, 2], F32),
                   "rstd": sb("rstd", [128, 1], F32)}
        costl = sb("costl", [128, 512], F32)
        sintl = sb("sintl", [128, 512], F32)
        ptm = sb("ptm", [128, D], BF16)
        pnrm = sb("pnrm", [128, D], BF16)
        pf = sb("pf", [128, 8, T], BF16)
        itm = sb("itm", [128, 16, D], BF16)
        iff = sb("iff", [128, 8, 128], BF16)
        sx2 = sb("sx2", [128, T], F32)
        m1 = sb("m1t", [128, 8], F32)
        m2 = sb("m2t", [128, 8], F32)
        iu32 = sb("iu32", [128, 16], U32)
        idxf = sb("idxf", [128, 16], F32)
        tq = sb("tq", [128, 1], F32)
        d0 = sb("d0", [128, 16], F32)
        d0b = sb("d0b", [128, 16], BF16)
        drow = sb("drow", [128, 2048], BF16)
        gidxf = sb("gidxf", [128, 1], F32)
        gidxu = sb("gidxu", [128, 1], U32)
        gtmp = sb("gtmp", [128, 16], F32)
        gath = sb("gath", [128, D], BF16)
        afm = sb("afm", [128, 8, 128], BF16)
        gsb = sb("gsb", [128, 128], BF16)
        relbuf = sb("relbuf", [16, T], BF16)
        h1sb = sb("h1sb", [64, T], BF16)
        h2sb = sb("h2sb", [64, T], BF16)
        mixrow = sx2
        gxrow = sb("gxrow", [1, T], BF16)
        gyrow = sb("gyrow", [1, T], BF16)
        mixT = sb("mixT", [128, 16], F32)
        wsoft = sb("wsoft", [128, 16], F32)
        gxT = sb("gxT", [128, 16], F32)
        gyT = sb("gyT", [128, 16], F32)
        rowmax = sb("rowmax", [128, 1], F32)
        rowsum = sb("rowsum", [128, 1], F32)
        nbias = sb("nbias", [128, 1], F32)
        recl = sb("recl", [128, 1], F32)
        fx = sb("fx", [128, 16], F32)
        fy = sb("fy", [128, 16], F32)
        x0i = sb("x0i", [128, 16], mybir.dt.int16)
        x0f = sb("x0f", [128, 16], F32)
        x1f = sb("x1f", [128, 16], F32)
        wx = sb("wxq", [128, 16], F32)
        y0f = sb("y0f", [128, 16], F32)
        y1f = sb("y1f", [128, 16], F32)
        wy = sb("wyq", [128, 16], F32)
        axw = sb("axw", [128, 256], F32)
        byw = sb("byw", [128, 256], F32)
        ctmp = sb("ctmp", [128, 256], F32)
        coeff = sb("coeff", [128, 256], F32)
        coefb = sb("coefb", [128, 256], BF16)
        cf = sb("cf", [128, 2, 128], BF16)
        vtmb = sb("vtmb", [128, D], BF16)
        vf = sb("vf", [128, 8, 128], BF16)
        PSA = ps("PSA", [128, 2048], F32)
        PSB = ps("PSB", [128, 1024], BF16)
        PSD = ps("PSD", [128, 512], F32)
        a = A(nc)

        a.dma("syq", identb, ident_in)
        a.dma("syq", identf, identf_in)
        a.dma("syq", diagm, diag_in)
        for c in range(8):
            a.dma("gpq", wi[:, c, :], wi_in[c * 128:(c + 1) * 128, :])
            a.dma("gpq", wp[:, c, :], wp_in[c * 128:(c + 1) * 128, :])
            a.dma("gpq", wot[:, c, :], wo_in[c * 128:(c + 1) * 128, :])
        a.dma("gpq", palt[:, 0, :], palT_in[0:128, :])
        a.dma("gpq", palt[:, 1, :], palT_in[128:256, :])
        for (dst, src) in ((w1g, w1g_in), (w1d, w1d_in), (w2, w2_in),
                           (wc0, wc0_in), (wc1, wc1_in), (wm, wm_in)):
            a.dma("gpq", dst, src)
        for (dst, src) in ((b1, b1_in), (b2, b2_in), (bct, bc_in), (bmt, bm_in),
                           (m0, m0_in), (keepm, keepm_in), (iota256, iota256_in),
                           (iota16r, iota16r_in), (tcol, tcol_in), (eqsel, eqsel_in)):
            a.dma("syq", dst, src)
        a.dma("syq", selg.rearrange("p a b -> p (a b)"), selg_in)
        a.dma("syq", selx.rearrange("p a b -> p (a b)"), selx_in)
        a.warmup(junk)
        for f in (AF.Exp, AF.Abs, AF.Square, AF.Gelu, AF.Tanh, AF.Sign, AF.Sqrt):
            a.act(junk, junk, f)

        import concourse.bass as _bass
        for b in range(B_PER_CORE):
            base = b * T
            for j in range(T // 128):
                r0 = base + j * 128
                a.dma("syq", xtile, x[r0:r0 + 128, :])
                emit_ln_tile(a, None, xtile, xln_tm, scratch)
                emit_tm_to_fm(a, xln_tm, [xlnf[:, c, :] for c in range(8)], identb,
                              PSB[:, 0:128])
                a.dma("syq", costl, cos_in[j * 128:(j + 1) * 128, :])
                a.dma("syq", sintl, sin_in[j * 128:(j + 1) * 128, :])
                for (wt, is_p) in ((wp, True), (wi, False)):
                    pp = PSA[:, 0:1024]
                    for hf in range(2):
                        for c in range(8):
                            a.mm(pp[:, hf * 512:(hf + 1) * 512], xlnf[:, c, :],
                                 wt[:, c, hf * 512:(hf + 1) * 512],
                                 start=(c == 0), stop=(c == 7))
                    a.tt(sx2[:, 0:512], pp[:, 0:512], costl, ALU.mult)
                    a.tt(sx2[:, 512:1024], pp[:, 512:1024], sintl, ALU.mult)
                    a.tt(ptm[:, 0:512], sx2[:, 0:512], sx2[:, 512:1024], ALU.subtract)
                    a.tt(sx2[:, 0:512], pp[:, 512:1024], costl, ALU.mult)
                    a.tt(sx2[:, 512:1024], pp[:, 0:512], sintl, ALU.mult)
                    a.tt(ptm[:, 512:1024], sx2[:, 0:512], sx2[:, 512:1024], ALU.add)
                    a.act(scratch["xsq"], ptm, AF.Square, accum_out=scratch["s2"])
                    a.act(scratch["s1"], scratch["s2"], AF.Sqrt)
                    a.ts(scratch["s1"], scratch["s1"], 1e-12, ALU.max)
                    a.op("vector", lambda e, _o=scratch["rstd"], _i=scratch["s1"]:
                         e.reciprocal(out=_o, in_=_i), reads=[scratch["s1"]], writes=[scratch["rstd"]])
                    a.ts(pnrm, ptm, scratch["rstd"][:, 0:1], ALU.mult)
                    if is_p:
                        a.dma("gpq", pnscr[b * 4096 + j * 128:b * 4096 + (j + 1) * 128, :], pnrm)
                        emit_tm_to_fm(a, ptm, [pf[:, c, j * 128:(j + 1) * 128] for c in range(8)],
                                      identb, PSB[:, 0:128])
                    else:
                        a.dma("gpq", pnscr[b * 4096 + 2048 + j * 128:b * 4096 + 2048 + (j + 1) * 128, :], pnrm)
                        a.copy(itm[:, j, :], ptm)
            for qt in range(T // 128):
                L = (qt + 1) * 128
                emit_tm_to_fm(a, itm[:, qt, :], [iff[:, c, :] for c in range(8)], identb,
                              PSB[:, 0:128])
                for kb in range((L + 511) // 512):
                    n = min(512, L - kb * 512)
                    for c in range(8):
                        a.mm(PSA[:, kb * 512:kb * 512 + n], iff[:, c, :],
                             pf[:, c, kb * 512:kb * 512 + n], start=(c == 0), stop=(c == 7))
                S = PSA[:, 0:L]
                a.tt(PSA[:, L - 128:L], PSA[:, L - 128:L], diagm, ALU.add)
                a.op("vector", lambda e, _m=m1, _s=S: e.max(out=_m, in_=_s), reads=[S], writes=[m1])
                a.op("vector", lambda e, _o=sx2, _m=m1, _s=S, _L=L:
                     e.match_replace(out=_o[:, 0:_L], in_to_replace=_m, in_values=_s, imm_value=NEG),
                     reads=[S, m1], writes=[sx2])
                a.op("vector", lambda e, _m=m2, _s=sx2, _L=L: e.max(out=_m, in_=_s[:, 0:_L]),
                     reads=[sx2], writes=[m2])
                a.op("vector", lambda e, _o=iu32, _m=m1, _s=S: e.max_index(out=_o[:, 0:8], in_max=_m, in_values=_s),
                     reads=[S, m1], writes=[iu32])
                a.op("vector", lambda e, _o=iu32, _m=m2, _s=sx2, _L=L: e.max_index(out=_o[:, 8:16], in_max=_m, in_values=_s[:, 0:_L]),
                     reads=[sx2, m2], writes=[iu32])
                a.copy(idxf, iu32)
                a.ts(tq, tcol, float(qt * 128), ALU.add)
                a.ts(d0, idxf, -1.0, ALU.mult, tq[:, 0:1], ALU.add)
                a.ts(d0, d0, 0.0, ALU.max, 1.0 / 2048.0, ALU.mult)
                a.memset(d0[:, 15:16], 0.0)
                a.copy(d0b, d0)
                a.op("gpsimd", lambda e, _o=drow, _i=d0b:
                     e.dma_start(out=_o[0:1, 0:2048].rearrange("o (t k) -> o t k", k=16), in_=_i,
                                 allow_slow_non_contiguous=True),
                     reads=[d0b], writes=[drow], queue="gpq")
                a.ts(idxf[:, 15:16], tq, float(2048), ALU.add)
                a.ts(idxf, idxf, float(b * 4096), ALU.add)
                for g in range(16):
                    a.mm(PSD[:, 0:16], selg[:, g, :], idxf, start=True, stop=True)
                    a.tt(gtmp, PSD[:, 0:16], eqsel, ALU.mult)
                    a.reduce(gidxf, gtmp, ALU.add)
                    a.copy(gidxu, gidxf)
                    a.op("gpsimd", lambda e, _o=gath, _i=pnscr, _x=gidxu:
                         e.indirect_dma_start(out=_o, out_offset=None, in_=_i,
                                              in_offset=_bass.IndirectOffsetOnAxis(ap=_x, axis=0)),
                         reads=[pnscr, gidxu], writes=[gath], queue="gpq")
                    for c in range(8):
                        a.trans(PSB[:, c * 128:(c + 1) * 128], gath[:, c * 128:(c + 1) * 128], identb)
                    a.copy(afm.rearrange("p a b -> p (a b)"), PSB[:, 0:1024])
                    for c in range(8):
                        a.mm(PSD[:, 128:256], afm[:, c, :], afm[:, c, :],
                             start=(c == 0), stop=(c == 7))
                    a.copy(gsb, PSD[:, 128:256])
                    for i in range(8):
                        a.mm(PSD[0:16, 256:272].rearrange("p a -> p a") if False else PSD[0:16, 256 + i * 16:256 + (i + 1) * 16],
                             selx[:, i, :], gsb[:, i * 16:(i + 1) * 16], start=True, stop=True)
                    a.ts(relbuf[0:16, g * 128:(g + 1) * 128], PSD[0:16, 256:384],
                         1.0, ALU.min, -1.0, ALU.max)
                if qt == 0:
                    a.tt(relbuf[0:16, 0:2048], relbuf[0:16, 0:2048], m0, ALU.mult)
                for sl in range(4):
                    ssl = slice(sl * 512, (sl + 1) * 512)
                    a.mm(PSA[0:64, ssl], w1g, relbuf[0:16, ssl], start=True, stop=False)
                    a.mm(PSA[0:64, ssl], w1d, drow[0:1, ssl], start=False, stop=True)
                a.act(h1sb, PSA[0:64, 0:2048], AF.Gelu, bias=b1[:, 0:1], scale=1.0)
                for sl in range(4):
                    ssl = slice(sl * 512, (sl + 1) * 512)
                    a.mm(PSA[64:128, ssl], w2, h1sb[:, ssl], start=True, stop=True)
                a.act(h2sb, PSA[64:128, 0:2048], AF.Gelu, bias=b2[:, 0:1], scale=1.0)
                for (wv_, bi_, dst, func) in ((wc0, bct[0:1, 0:1], gxrow, AF.Tanh),
                                              (wc1, bct[0:1, 1:2], gyrow, AF.Tanh),
                                              (wm, bmt[0:1, 0:1], mixrow, AF.Identity)):
                    for sl in range(4):
                        ssl = slice(sl * 512, (sl + 1) * 512)
                        a.mm(PSA[0:1, ssl], wv_, h2sb[:, ssl], start=True, stop=True)
                    a.act(dst[0:1, 0:2048], PSA[0:1, 0:2048], func, bias=bi_, scale=1.0)
                for (src, dstT, idn) in ((mixrow, mixT, identf), (gxrow, gxT, identb), (gyrow, gyT, identb)):
                    for k in range(16):
                        if idn is identf:
                            a.trans(PSD[:, 384 + k:384 + k + 1],
                                    src[0:1, 0:2048].rearrange("o (t k) -> o k t", k=16)[:, k, :], idn[0:1, 0:1])
                        else:
                            a.trans(PSB[:, 512 + 2 * k:512 + 2 * k + 1],
                                    src[0:1, 0:2048].rearrange("o (t k) -> o k t", k=16)[:, k, :], idn[0:1, 0:1])
                    if idn is identf:
                        a.copy(dstT, PSD[:, 384:400])
                    else:
                        a.copy(dstT, PSB[:, 512:544].rearrange("p (k two) -> p k two", two=2)[:, :, 0])
                kslice = slice(16, 32) if qt == 0 else slice(0, 16)
                a.tt(mixT, mixT, keepm[:, kslice], ALU.add)
                a.reduce(rowmax, mixT, ALU.max)
                a.ts(nbias, rowmax, -1.0, ALU.mult)
                a.act(wsoft, mixT, AF.Exp, bias=nbias[:, 0:1], scale=1.0, accum_out=rowsum)
                a.op("vector", lambda e, _o=recl, _i=rowsum: e.reciprocal(out=_o, in_=_i),
                     reads=[rowsum], writes=[recl])
                a.ts(wsoft, wsoft, recl[:, 0:1], ALU.mult)
                for (gT, f_, i0_, f0_, f1_, w_) in ((gxT, fx, x0i, x0f, x1f, wx),
                                                    (gyT, fy, x0i, y0f, y1f, wy)):
                    a.ts(f_, gT, 1.0, ALU.add, 7.5, ALU.mult)
                    a.ts(i0_, f_, 0.5, ALU.subtract)
                    a.copy(f0_, i0_)
                    a.tt(w_, f_, f0_, ALU.subtract)
                    a.ts(f1_, f0_, 1.0, ALU.add, 15.0, ALU.min)
                a.op("vector", lambda e, _o=axw, _i=iota16r, _x=x0f:
                     e.tensor_tensor(out=_o.rearrange("p (k w) -> p k w", w=16),
                                     in0=_i.rearrange("p (k w) -> p k w", w=16),
                                     in1=_x.unsqueeze(2).broadcast_to([128, 16, 16]), op=ALU.is_equal),
                     reads=[iota16r, x0f], writes=[axw])
                a.op("vector", lambda e, _o=ctmp, _i=iota16r, _x=x1f:
                     e.tensor_tensor(out=_o.rearrange("p (k w) -> p k w", w=16),
                                     in0=_i.rearrange("p (k w) -> p k w", w=16),
                                     in1=_x.unsqueeze(2).broadcast_to([128, 16, 16]), op=ALU.is_equal),
                     reads=[iota16r, x1f], writes=[ctmp])
                a.tt(ctmp, ctmp, axw, ALU.subtract)
                a.op("vector", lambda e, _o=ctmp, _i=ctmp, _x=wx:
                     e.tensor_tensor(out=_o.rearrange("p (k w) -> p k w", w=16),
                                     in0=_i.rearrange("p (k w) -> p k w", w=16),
                                     in1=_x.unsqueeze(2).broadcast_to([128, 16, 16]), op=ALU.mult),
                     reads=[ctmp, wx], writes=[ctmp])
                a.tt(axw, axw, ctmp, ALU.add)
                a.op("vector", lambda e, _o=byw, _i=iota16r, _x=y0f:
                     e.tensor_tensor(out=_o.rearrange("p (k w) -> p k w", w=16),
                                     in0=_i.rearrange("p (k w) -> p k w", w=16),
                                     in1=_x.unsqueeze(2).broadcast_to([128, 16, 16]), op=ALU.is_equal),
                     reads=[iota16r, y0f], writes=[byw])
                a.op("vector", lambda e, _o=ctmp, _i=iota16r, _x=y1f:
                     e.tensor_tensor(out=_o.rearrange("p (k w) -> p k w", w=16),
                                     in0=_i.rearrange("p (k w) -> p k w", w=16),
                                     in1=_x.unsqueeze(2).broadcast_to([128, 16, 16]), op=ALU.is_equal),
                     reads=[iota16r, y1f], writes=[ctmp])
                a.tt(ctmp, ctmp, byw, ALU.subtract)
                a.op("vector", lambda e, _o=ctmp, _i=ctmp, _x=wy:
                     e.tensor_tensor(out=_o.rearrange("p (k w) -> p k w", w=16),
                                     in0=_i.rearrange("p (k w) -> p k w", w=16),
                                     in1=_x.unsqueeze(2).broadcast_to([128, 16, 16]), op=ALU.mult),
                     reads=[ctmp, wy], writes=[ctmp])
                a.tt(byw, byw, ctmp, ALU.add)
                a.op("vector", lambda e, _o=byw, _i=byw, _x=wsoft:
                     e.tensor_tensor(out=_o.rearrange("p (k w) -> p k w", w=16),
                                     in0=_i.rearrange("p (k w) -> p k w", w=16),
                                     in1=_x.unsqueeze(2).broadcast_to([128, 16, 16]), op=ALU.mult),
                     reads=[byw, wsoft], writes=[byw])
                first = True
                for k in range(16):
                    eng = "vector" if k % 2 == 0 else "gpsimd"
                    a.op(eng, lambda e, _o=ctmp, _b=byw, _a=axw, _k=k:
                         e.tensor_tensor(out=_o, in0=_b[:, _k * 16:(_k + 1) * 16].unsqueeze(2).broadcast_to([128, 16, 16]),
                                         in1=_a[:, _k * 16:(_k + 1) * 16].unsqueeze(1).broadcast_to([128, 16, 16]),
                                         op=ALU.mult),
                         reads=[byw, axw], writes=[ctmp])
                    if first:
                        a.copy(coeff, ctmp, eng="gpsimd" if eng == "vector" else "vector")
                        first = False
                    else:
                        a.tt(coeff, coeff, ctmp, ALU.add, eng="gpsimd" if eng == "vector" else "vector")
                a.copy(coefb, coeff)
                for c2 in range(2):
                    a.trans(PSB[:, 0:128], coefb[:, c2 * 128:(c2 + 1) * 128], identb)
                    a.copy(cf[:, c2, :], PSB[:, 0:128])
                pv = PSA[:, 1024:2048]
                for hf in range(2):
                    for c2 in range(2):
                        a.mm(pv[:, hf * 512:(hf + 1) * 512], cf[:, c2, :],
                             palt[:, c2, hf * 512:(hf + 1) * 512], start=(c2 == 0), stop=(c2 == 1))
                a.copy(vtmb, pv)
                emit_tm_to_fm(a, vtmb, [vf[:, c, :] for c in range(8)], identb, PSB[:, 0:128])
                pd = PSA[:, 0:1024]
                for c in range(8):
                    for hf in range(2):
                        a.mm(pd[:, hf * 512:(hf + 1) * 512], vf[:, c, :],
                             wot[:, c, hf * 512:(hf + 1) * 512], start=(c == 0), stop=(c == 7))
                r0 = base + qt * 128
                a.dma("scq", xtile, x[r0:r0 + 128, :])
                a.tt(xtile, xtile, pd, ALU.add)
                a.dma("gpq", xo[r0:r0 + 128, :], xtile)
        a.barrier_outputs([xo])
        a.emit(stack)
    return nc


def host_pal_consts():
    c = {}
    half = 512
    inv_freq = 1.0 / 10000.0 ** (np.arange(half, dtype=np.float32) / half)
    ang = np.arange(T, dtype=np.float32)[:, None] * inv_freq[None, :]
    c["cosr"] = np.cos(ang).astype(np.float32)
    c["sinr"] = np.sin(ang).astype(np.float32)
    m0 = np.zeros((16, T), np.float32)
    for t in range(128):
        for k in range(16):
            for j in range(16):
                ok = (k <= t and k < 15) and (j <= t or j == 15)
                m0[j, t * 16 + k] = 1.0 if ok else 0.0
    c["m0"] = m0.astype(ml_dtypes.bfloat16)
    km = np.zeros((128, 32), np.float32)
    km[:, 15] = NEG
    for p in range(128):
        for k in range(16):
            km[p, 16 + k] = NEG if (k > p or k == 15) else 0.0
    c["keepm"] = km
    c["iota256"] = np.tile(np.arange(256, dtype=np.float32)[None, :], (128, 1))
    c["iota16r"] = np.tile(np.tile(np.arange(16, dtype=np.float32), 16)[None, :], (128, 1))
    c["tcol"] = np.arange(128, dtype=np.float32)[:, None].copy()
    eq = np.zeros((128, 16), np.float32)
    for p in range(128):
        eq[p, p % 16] = 1.0
    c["eqsel"] = eq
    selg = np.zeros((16, 128, 128), np.float32)
    for g in range(16):
        for p in range(128):
            selg[g, 8 * g + p // 16, p] = 1.0
    c["selg"] = np.ascontiguousarray(np.transpose(selg, (1, 0, 2)).reshape(128, 16 * 128))
    selx = np.zeros((8, 128, 16), np.float32)
    for i in range(8):
        for j in range(16):
            selx[i, 16 * i + j, j] = 1.0
    c["selx"] = np.ascontiguousarray(np.transpose(selx, (1, 0, 2)).reshape(128, 8 * 16)).astype(ml_dtypes.bfloat16)
    return c


def run_attn_pal(x_full, Wi, Wp, palette, W1, b1, W2, b2, Wc, bc, Wm, bm, Wo):
    consts = host_constants()
    pc = host_pal_consts()
    nc = build_attn_pal()
    in_common = {
        "wi": np.ascontiguousarray(Wi), "wp": np.ascontiguousarray(Wp),
        "wo": np.ascontiguousarray(Wo[0]),
        "palT": np.ascontiguousarray(palette.reshape(D, 256).T),
        "w1g": np.ascontiguousarray(W1[0:16, :]), "w1d": np.ascontiguousarray(W1[16:17, :]),
        "b1": np.ascontiguousarray(b1[:, None]), "w2": np.ascontiguousarray(W2),
        "b2": np.ascontiguousarray(b2[:, None]),
        "wcc0": np.ascontiguousarray(Wc[:, 0:1]), "wcc1": np.ascontiguousarray(Wc[:, 1:2]),
        "bc": np.ascontiguousarray(bc[None, :]), "wm": np.ascontiguousarray(Wm),
        "bm": np.ascontiguousarray(bm[None, :]),
        "ident": consts["ident_bf"], "identf": consts["ident"], "diagmask": consts["diagmask"],
    }
    in_common.update(pc)
    in_maps = []
    for c in range(NCORES):
        m = dict(in_common)
        m["x"] = x_full.reshape(16 * T, D)[c * B_PER_CORE * T:(c + 1) * B_PER_CORE * T]
        in_maps.append(m)
    res = run_bass_kernel_spmd(nc, in_maps, core_ids=list(range(NCORES)))
    out = np.concatenate([res.results[c]["xo"] for c in range(NCORES)], axis=0)
    return out.reshape(16, T, D)


def kernel(**inp):
    x = np.ascontiguousarray(np.asarray(inp["L0_attn_a_x"], dtype=np.float32))
    x = run_attn_a(x, inp["L0_attn_a_Wq"], inp["L0_attn_a_Wk"], inp["L0_attn_a_Wv"],
                   np.asarray(inp["L0_attn_a_key_decoder"], np.float32),
                   np.asarray(inp["L0_attn_a_value_decoder"], np.float32),
                   inp["L0_attn_a_Wproj"])
    x = run_l0_ffn(x, inp["L0_ffn_w_gate"], inp["L0_ffn_w_up"], inp["L0_ffn_w_down"])
    x = run_attn_b(x, inp["L0_attn_b_Wq"], inp["L0_attn_b_Wk"], inp["L0_attn_b_Wv"],
                   inp["L0_attn_b_Wo"])
    x = run_l0_ffn(x, inp["L0_ffn_w_gate"], inp["L0_ffn_w_up"], inp["L0_ffn_w_down"])
    x = run_mamba(x, *[np.asarray(inp[f"L1_attn_a_{n}"], np.float32) for n in
                       ("W_in", "conv_w", "conv_b", "W_x", "W_dt", "b_dt", "A_log",
                        "D_param", "W_out", "ln_w", "ln_b")])
    x = run_l1_ffn(x, np.asarray(inp["L1_ffn_w_gate"], np.float32),
                   np.asarray(inp["L1_ffn_w_up"], np.float32),
                   np.asarray(inp["L1_ffn_w_down"], np.float32))
    x = run_attn_pal(x, *[np.asarray(inp[f"L1_attn_b_{n}"], np.float32) for n in
                          ("Wi", "Wp", "palette", "W1", "b1", "W2", "b2", "Wc", "bc",
                           "Wm", "bm", "Wo")])
    x = run_l1_ffn(x, np.asarray(inp["L1_ffn_w_gate"], np.float32),
                   np.asarray(inp["L1_ffn_w_up"], np.float32),
                   np.asarray(inp["L1_ffn_w_down"], np.float32))
    return x.astype(np.float32)


def build_l0_ffn_dbg():
    hidden = 3072
    HC = hidden // 128
    NTOK = B_PER_CORE * T
    NT = NTOK // 128
    NT512 = NTOK // 512
    nc = bass.Bass("TRN2", target_bir_lowering=False)
    x = nc.dram_tensor("x", [NTOK, D], F32, kind="ExternalInput").ap()
    wgT = nc.dram_tensor("wgT", [D, hidden], I32, kind="ExternalInput").ap()
    wuT = nc.dram_tensor("wuT", [D, hidden], I32, kind="ExternalInput").ap()
    wdT = nc.dram_tensor("wdT", [hidden, D], I32, kind="ExternalInput").ap()
    ident_in = nc.dram_tensor("ident", [128, 128], BF16, kind="ExternalInput").ap()
    xo = nc.dram_tensor("xo", [NTOK, D], F32, kind="ExternalOutput").ap()
    oxln = nc.dram_tensor("oxln", [NTOK, D], F32, kind="ExternalOutput").ap()
    oh = nc.dram_tensor("oh", [24, 128, NTOK], F32, kind="ExternalOutput").ap()
    opd = nc.dram_tensor("opd", [NTOK, D], F32, kind="ExternalOutput").ap()

    from contextlib import ExitStack
    with ExitStack() as stack:
        ctx = Ctx(nc, stack)
        sb, ps = ctx.sb, ctx.ps
        junk = sb("junk", [128, 256], F32)
        identb = sb("identb", [128, 128], BF16)
        xtile = [sb(f"xt{i}", [128, D], F32) for i in range(2)]
        xln_tm = sb("xln_tm", [128, D], BF16)
        xlnf = sb("xlnf", [128, 8, T // 2], BF16)
        scratch = {"xsq": sb("xsq", [128, D], BF16), "s1": sb("s1", [128, 1], F32),
                   "s2": sb("s2", [128, 1], F32), "mv": sb("mv", [128, 2], F32),
                   "rstd": sb("rstd", [128, 1], F32)}
        wgt = sb("wgt", [128, 8, hidden], BF16)
        wut = sb("wut", [128, 8, hidden], BF16)
        wdt = sb("wdtr", [128, HC, D], BF16)
        hbuf = sb("hbuf", [128, HC, 512], BF16)
        sg = [sb(f"sg{i}", [128, 512], BF16) for i in range(2)]
        pg = [ps(f"pg{i}", [128, 512], F32) for i in range(2)]
        pu = [ps(f"pu{i}", [128, 512], F32) for i in range(2)]
        pd = ps("pd", [128, D], F32)
        pxp = ps("pxp", [128, 128], BF16)
        a = A(nc)

        a.dma("syq", identb, ident_in)
        for c in range(8):
            a.dma("gpq", wgt[:, c, :], wgT[c * 128:(c + 1) * 128, :])
            a.dma("gpq", wut[:, c, :], wuT[c * 128:(c + 1) * 128, :])
        for hc in range(HC):
            a.dma("gpq", wdt[:, hc, :], wdT[hc * 128:(hc + 1) * 128, :])
        a.warmup(junk)

        NTSEQ = T // 256
        NT512SEQ = T // 1024
        for b in range(2 * B_PER_CORE):
            base = b * (T // 2)
            for j in range(NTSEQ):
                xt = xtile[j % 2]
                r0 = base + j * 128
                a.dma("syq", xt, x[r0:r0 + 128, :])
                emit_ln_tile(a, None, xt, xln_tm, scratch, extra_scale=1.0 / 127.0)
                a.dma("gpq", oxln[r0:r0 + 128, :], xln_tm)
                emit_tm_to_fm(a, xln_tm, [xlnf[:, c, j * 128:(j + 1) * 128] for c in range(8)],
                              identb, pxp)
            for tt in range(NT512SEQ):
                tsl = slice(tt * 512, (tt + 1) * 512)
                for hc in range(HC):
                    g = pg[hc % 2]
                    u = pu[hc % 2]
                    sgt = sg[hc % 2]
                    for c in range(8):
                        a.mm(g, wgt[:, c, hc * 128:(hc + 1) * 128], xlnf[:, c, tsl],
                             start=(c == 0), stop=(c == 7))
                    for c in range(8):
                        a.mm(u, wut[:, c, hc * 128:(hc + 1) * 128], xlnf[:, c, tsl],
                             start=(c == 0), stop=(c == 7))
                    a.act(sgt, g, AF.Silu)
                    a.stt(hbuf[:, hc, :], sgt, 1.0 / 127.0, u, ALU.mult, ALU.mult)
                    a.dma("gpq", oh[hc, :, base + tt * 512:base + tt * 512 + 512], hbuf[:, hc, :], tag="dbg")
                for st4 in range(4):
                    j = tt * 4 + st4
                    for hc in range(HC):
                        for half in range(2):
                            a.mm(pd[:, half * 512:(half + 1) * 512],
                                 hbuf[:, hc, st4 * 128:(st4 + 1) * 128],
                                 wdt[:, hc, half * 512:(half + 1) * 512],
                                 start=(hc == 0), stop=(hc == HC - 1))
                    xt = xtile[j % 2]
                    r0 = base + j * 128
                    a.dma("syq", xt, x[r0:r0 + 128, :])
                    a.copy(mvd := xt, xt) if False else None
                    a.dma("vq", opd[r0:r0 + 128, :], xt, tag="preres") if False else None
                    a.tt(xt, xt, pd, ALU.add)
                    a.dma("gpq", xo[r0:r0 + 128, :], xt)

        a.barrier_outputs([xo, oxln, oh])
        a.emit(stack)
    return nc




# version 03
# speedup vs baseline: 1.0054x; 1.0054x over previous
import math
import numpy as np
import ml_dtypes
import concourse.bass as bass
import concourse.mybir as mybir
from concourse.bass_utils import run_bass_kernel_spmd

F32 = mybir.dt.float32
BF16 = mybir.dt.bfloat16
I32 = mybir.dt.int32
U32 = mybir.dt.uint32
U8 = mybir.dt.uint8
AF = mybir.ActivationFunctionType
ALU = mybir.AluOpType
AX = mybir.AxisListType

NEG = -3.0e38

ENGINES = ("sync", "scalar", "vector", "tensor", "gpsimd")
DMAQ = {"syq": "sync", "scq": "scalar", "gpq": "gpsimd", "vq": "vector"}


def _regions(ap_or_list):
    out = []
    if ap_or_list is None:
        return out
    items = ap_or_list if isinstance(ap_or_list, (list, tuple)) else [ap_or_list]
    for ap in items:
        if ap is None:
            continue
        t = ap.tensor
        name = t.name
        esz = mybir.dt.size(ap.dtype)
        lo = ap.offset
        span = 1
        try:
            for st, cnt in ap.ap:
                span += abs(int(st)) * (int(cnt) - 1)
        except Exception:
            span = t.size if hasattr(t, "size") else 1 << 40
        out.append((name, lo * esz, (lo + span) * esz))
    return out


class Op:
    __slots__ = ("idx", "eng", "queue", "fn", "reads", "writes", "seq", "inc_needed",
                 "waits", "tag", "deps", "pre_drain", "dma_sid", "dma_use")

    def __init__(self, idx, eng, queue, fn, reads, writes, tag=""):
        self.idx = idx
        self.eng = eng
        self.queue = queue
        self.fn = fn
        self.reads = reads
        self.writes = writes
        self.seq = None
        self.inc_needed = False
        self.pre_drain = False
        self.waits = []
        self.tag = tag


class A:

    SERIALIZE = False

    def __init__(self, nc):
        self.nc = nc
        self.ops = []
        self.access = {}
        self.nwaits = 0

    def op(self, eng, fn, reads=None, writes=None, queue=None, tag=""):
        o = Op(len(self.ops), eng, queue, fn, _regions(reads), _regions(writes), tag)
        self.ops.append(o)
        deps = set()
        for (name, lo, hi) in o.reads:
            for (l2, h2, prev, w) in self.access.get(name, ()):
                if w and l2 < hi and lo < h2:
                    deps.add(prev)
        for (name, lo, hi) in o.writes:
            for (l2, h2, prev, w) in self.access.get(name, ()):
                if l2 < hi and lo < h2:
                    deps.add(prev)
        if A.SERIALIZE and self.ops[:-1]:
            deps.add(self.ops[-2])
        o.deps = deps
        for (name, lo, hi) in o.reads:
            self.access.setdefault(name, []).append((lo, hi, o, False))
        for (name, lo, hi) in o.writes:
            lst = self.access.setdefault(name, [])
            keep = [e for e in lst if not (lo <= e[0] and e[1] <= hi)]
            keep.append((lo, hi, o, True))
            self.access[name] = keep
        return o

    def _stream_name(self, o):
        return o.queue if o.queue else o.eng

    NDMASEM = 48

    def schedule(self):
        counters = {}
        dma_n = 0
        self.sem_uses = {}
        for o in self.ops:
            sname = self._stream_name(o)
            counters[sname] = counters.get(sname, 0) + 1
            o.seq = counters[sname]
            if o.queue is not None:
                sid = dma_n % self.NDMASEM
                dma_n += 1
                self.sem_uses[sid] = self.sem_uses.get(sid, 0) + 1
                o.dma_sid = sid
                o.dma_use = self.sem_uses[sid]
        hi_water = {}
        for o in self.ops:
            cons = self._stream_name(o)
            need = {}
            for d in o.deps:
                if d.queue is not None:
                    key = ("D", d.dma_sid)
                    need[key] = max(need.get(key, 0), d.dma_use)
                else:
                    pstream = self._stream_name(d)
                    if pstream == cons and o.queue is None:
                        if cons != "tensor" and (o.seq - d.seq) <= 2:
                            o.pre_drain = True
                        continue
                    if o.queue is not None and d.eng == o.eng:
                        continue
                    need[("E", pstream)] = max(need.get(("E", pstream), 0), d.seq)
            for k, seq in sorted(need.items()):
                key = (cons, k)
                if hi_water.get(key, 0) >= seq:
                    continue
                hi_water[key] = seq
                o.waits.append((k, seq))
        targets = {}
        for o in self.ops:
            for (k, seq) in o.waits:
                if k[0] == "E":
                    targets.setdefault(k[1], set()).add(seq)
        by_stream = {}
        for o in self.ops:
            if o.queue is None:
                by_stream.setdefault(self._stream_name(o), {})[o.seq] = o
        for pstream, seqs in targets.items():
            for sq in seqs:
                by_stream[pstream][sq].inc_needed = True
        for o in self.ops:
            if o.queue is not None:
                o.inc_needed = True

    def emit(self, stack):
        self.schedule()
        sems = {}
        streams = set(self._stream_name(o) for o in self.ops if o.queue is None)
        for s in streams:
            sems[s] = stack.enter_context(self.nc.semaphore("sem_" + s))
        dsems = [stack.enter_context(self.nc.semaphore(f"dma{i}"))
                 for i in range(min(self.NDMASEM, len(self.sem_uses) if self.sem_uses else 0) or 1)]
        inc_prefix = {}
        for o in self.ops:
            if o.queue is None:
                sname = self._stream_name(o)
                lst = inc_prefix.setdefault(sname, [0])
                lst.append(lst[-1] + (1 if o.inc_needed else 0))
        per_engine_ops = {}
        for o in self.ops:
            per_engine_ops.setdefault(o.eng, []).append(o)
        block = stack.enter_context(self.nc.Block())
        nwaits = [0]
        for eng_name in ENGINES:
            ops = per_engine_ops.get(eng_name, [])
            if not ops:
                continue
            def mk(ops, eng_name):
                def section(eng):
                    for o in ops:
                        for (k, seq) in o.waits:
                            if k[0] == "D":
                                eng.wait_ge(dsems[k[1]], seq * 16)
                            else:
                                val = inc_prefix[k[1]][seq]
                                if val > 0:
                                    eng.wait_ge(sems[k[1]], val)
                            nwaits[0] += 1
                        if o.queue is not None:
                            if o.dma_use > 1:
                                eng.wait_ge(dsems[o.dma_sid], (o.dma_use - 1) * 16)
                            instr = o.fn(eng)
                            instr.then_inc(dsems[o.dma_sid], 16)
                        else:
                            if o.pre_drain:
                                eng.drain()
                            instr = o.fn(eng)
                            if o.inc_needed:
                                if instr is None:
                                    instr = eng.nop()
                                if eng_name in ("vector", "scalar", "gpsimd", "tensor"):
                                    instr = eng.drain()
                                instr.then_inc(sems[self._stream_name(o)], 1)
                return section
            getattr(block, eng_name)(mk(ops, eng_name))
        self.nwaits = nwaits[0]

    def dma(self, queue, out, in_, tag="", **kw):
        eng = DMAQ[queue]
        return self.op(eng, lambda e: e.dma_start(out=out, in_=in_, **kw),
                       reads=[in_], writes=[out], queue=queue, tag=tag)

    def tt(self, out, in0, in1, op, eng="vector", tag=""):
        return self.op(eng, lambda e: e.tensor_tensor(out=out, in0=in0, in1=in1, op=op),
                       reads=[in0, in1], writes=[out], tag=tag)

    def ts(self, out, in0, s1, op0, s2=None, op1=None, eng="vector", accum_out=None, tag=""):
        r = [in0]
        if isinstance(s1, bass.AP) or hasattr(s1, "tensor"):
            r.append(s1)
        if s2 is not None and hasattr(s2, "tensor"):
            r.append(s2)
        w = [out] + ([accum_out] if accum_out is not None else [])
        if op1 is None:
            return self.op(eng, lambda e: e.tensor_scalar(out=out, in0=in0, scalar1=s1, scalar2=None, op0=op0, accum_out=accum_out),
                           reads=r, writes=w, tag=tag)
        return self.op(eng, lambda e: e.tensor_scalar(out=out, in0=in0, scalar1=s1, scalar2=s2, op0=op0, op1=op1, accum_out=accum_out),
                       reads=r, writes=w, tag=tag)

    def stt(self, out, in0, scalar, in1, op0, op1, eng="vector", tag=""):
        r = [in0, in1]
        if hasattr(scalar, "tensor"):
            r.append(scalar)
        return self.op(eng, lambda e: e.scalar_tensor_tensor(out=out, in0=in0, scalar=scalar, in1=in1, op0=op0, op1=op1),
                       reads=r, writes=[out], tag=tag)

    def act(self, out, in_, func, bias=0.0, scale=1.0, accum_out=None, tag=""):
        r = [in_]
        if hasattr(bias, "tensor"):
            r.append(bias)
        if hasattr(scale, "tensor"):
            r.append(scale)
        w = [out] + ([accum_out] if accum_out is not None else [])
        return self.op("scalar", lambda e: e.activation(out=out, in_=in_, func=func, bias=bias, scale=scale, accum_out=accum_out),
                       reads=r, writes=w, tag=tag)

    def copy(self, out, in_, eng="vector", tag=""):
        return self.op(eng, lambda e: e.tensor_copy(out=out, in_=in_),
                       reads=[in_], writes=[out], tag=tag)

    def memset(self, out, val, eng="vector", tag=""):
        return self.op(eng, lambda e: e.memset(out, val), reads=[], writes=[out], tag=tag)

    def mm(self, out, lhsT, rhs, start=True, stop=True, tag=""):
        rw = [out] if not start else []
        return self.op("tensor", lambda e: e.matmul(out, lhsT, rhs, start=start, stop=stop),
                       reads=[lhsT, rhs] + rw, writes=[out], tag=tag)

    def trans(self, out, in_, ident, tag=""):
        return self.op("tensor", lambda e: e.transpose(out, in_, ident),
                       reads=[in_, ident], writes=[out], tag=tag)

    def reduce(self, out, in_, op, absval=None, negate=None, eng="vector", tag=""):
        if op == ALU.add:
            return self.op(eng, lambda e: e.reduce_sum(out=out, in_=in_, axis=AX.X, apply_absolute_value=absval, negate=negate),
                           reads=[in_], writes=[out], tag=tag)
        if op == ALU.max:
            return self.op(eng, lambda e: e.reduce_max(out=out, in_=in_, axis=AX.X, apply_absolute_value=absval, negate=negate),
                           reads=[in_], writes=[out], tag=tag)
        return self.op(eng, lambda e: e.tensor_reduce(out=out, in_=in_, axis=AX.X, op=op, apply_absolute_value=absval, negate=negate),
                       reads=[in_], writes=[out], tag=tag)

    def warmup(self, junk, n=64):
        for i in range(n):
            self.ts(junk, junk, 1.0, ALU.mult, tag="warm")

    def barrier_outputs(self, outs):
        self.op("gpsimd", lambda e: e.nop(), reads=list(outs), writes=[], tag="fin")

    def barrier(self):
        lasts = {}
        for o in self.ops:
            lasts[self._stream_name(o)] = o
        dep_ops = set(lasts.values())
        marks = []
        for e in ENGINES:
            o = self.op(e, lambda eng: eng.nop(), reads=[], writes=[], tag="bar")
            o.deps |= dep_ops
            marks.append(o)
        mset = set(marks)
        for e in ENGINES:
            o = self.op(e, lambda eng: eng.nop(), reads=[], writes=[], tag="bar2")
            o.deps |= mset


class Ctx:

    def __init__(self, nc, stack, prefix=""):
        self.nc = nc
        self.stack = stack
        self.prefix = prefix
        self.tiles = {}

    def sb(self, name, shape, dtype):
        name = self.prefix + name
        t = self.stack.enter_context(self.nc.sbuf_tensor(name, shape, dtype))
        self.tiles[name] = t[:]
        return self.tiles[name]

    def ps(self, name, shape, dtype):
        name = self.prefix + name
        t = self.stack.enter_context(self.nc.psum_tensor(name, shape, dtype))
        self.tiles[name] = t[:]
        return self.tiles[name]


B_PER_CORE = 2
T = 2048
D = 1024
NCORES = 8


def host_constants():
    c = {}
    c["ident"] = np.eye(128, dtype=np.float32)
    c["ident_bf"] = np.eye(128, dtype=ml_dtypes.bfloat16)
    m = np.zeros((128, 128), np.float32)
    m[np.triu_indices(128, 1)] = NEG
    c["diagmask"] = m
    return c


def emit_ln_tile(a, t, x_tile, xln_tm, scratch, extra_scale=1.0, eps=1e-5, width=1024):
    xsq, s1, s2 = scratch["xsq"], scratch["s1"], scratch["s2"]
    mv, rstd = scratch["mv"], scratch["rstd"]
    inv = 1.0 / width
    a.reduce(s1, x_tile, ALU.add)
    a.act(xsq, x_tile, AF.Square, accum_out=s2)
    a.ts(mv[:, 0:1], s1, inv, ALU.mult)
    a.ts(mv[:, 1:2], s2, inv, ALU.mult)
    a.ts(s1, mv[:, 0:1], mv[:, 0:1], ALU.mult)
    a.stt(rstd, s1, -1.0, mv[:, 1:2], ALU.mult, ALU.add)
    a.ts(rstd, rstd, eps, ALU.add)
    a.op("vector", lambda e: e.reciprocal(out=rstd, in_=rstd), reads=[rstd], writes=[rstd])
    a.act(rstd, rstd, AF.Sqrt, scale=1.0)
    if extra_scale != 1.0:
        a.ts(rstd, rstd, float(extra_scale), ALU.mult)
    a.ts(xln_tm, x_tile, mv[:, 0:1], ALU.subtract, rstd[:, 0:1], ALU.mult)


def emit_tm_to_fm(a, x_tm, out_fm_chunks, identb, pxp, nchunk=8):
    for c in range(nchunk):
        a.trans(pxp, x_tm[:, c * 128:(c + 1) * 128], identb)
        a.copy(out_fm_chunks[c], pxp)


def emit_l0_ffn(a, ctx, d, PS, com, xin, xout):
    hidden = 3072
    HC = hidden // 128
    NTOK = B_PER_CORE * T
    NT = NTOK // 128
    NT512 = NTOK // 512

    if True:
        sb = ctx.sb
        x, xo = xin, xout
        wgT, wuT, wdT = d["wgT"], d["wuT"], d["wdT"]
        identb = com["identb"]
        junk = com["junk"]
        xtile = [sb(f"xt{i}", [128, D], F32) for i in range(2)]
        xln_tm = sb("xln_tm", [128, D], BF16)
        xlnf = sb("xlnf", [128, 8, T], BF16)
        scratch = {"xsq": sb("xsq", [128, D], BF16), "s1": sb("s1", [128, 1], F32),
                   "s2": sb("s2", [128, 1], F32), "mv": sb("mv", [128, 2], F32),
                   "rstd": sb("rstd", [128, 1], F32)}
        wgt = sb("wgt", [128, 8, hidden], BF16)
        wut = sb("wut", [128, 8, hidden], BF16)
        wdt = [sb(f"wdt{i}", [128, D], BF16) for i in range(2)]
        hbuf = sb("hbuf", [128, HC, 512], BF16)
        sg = [sb(f"sg{i}", [128, 512], BF16) for i in range(2)]
        pg = [PS["PSD"], PS["PSE"]]
        pu = [PS["PSA"][:, 0:512], PS["PSA"][:, 512:1024]]
        pd = PS["PSA"][:, 1024:2048]
        pxp = PS["PSB"][:, 0:128]

        for c in range(8):
            a.dma("gpq", wgt[:, c, :], wgT[c * 128:(c + 1) * 128, :])
            a.dma("gpq", wut[:, c, :], wuT[c * 128:(c + 1) * 128, :])

        NTSEQ = T // 128
        NT512SEQ = T // 512
        for b in range(B_PER_CORE):
            base = b * T
            for j in range(NTSEQ):
                xt = xtile[j % 2]
                r0 = base + j * 128
                a.dma("syq", xt, x[r0:r0 + 128, :])
                emit_ln_tile(a, None, xt, xln_tm, scratch, extra_scale=1.0 / 127.0)
                emit_tm_to_fm(a, xln_tm, [xlnf[:, c, j * 128:(j + 1) * 128] for c in range(8)],
                              identb, pxp)
            for tt in range(NT512SEQ):
                tsl = slice(tt * 512, (tt + 1) * 512)
                for hc in range(HC):
                    g = pg[hc % 2]
                    u = pu[hc % 2]
                    sgt = sg[hc % 2]
                    for c in range(8):
                        a.mm(g, wgt[:, c, hc * 128:(hc + 1) * 128], xlnf[:, c, tsl],
                             start=(c == 0), stop=(c == 7))
                    for c in range(8):
                        a.mm(u, wut[:, c, hc * 128:(hc + 1) * 128], xlnf[:, c, tsl],
                             start=(c == 0), stop=(c == 7))
                    a.act(sgt, g, AF.Silu)
                    a.stt(hbuf[:, hc, :], sgt, 1.0 / 127.0, u, ALU.mult, ALU.mult)
                for st4 in range(4):
                    j = tt * 4 + st4
                    for hc in range(HC):
                        w = wdt[hc % 2]
                        a.dma("gpq", w, wdT[hc * 128:(hc + 1) * 128, :])
                        for half in range(2):
                            a.mm(pd[:, half * 512:(half + 1) * 512],
                                 hbuf[:, hc, st4 * 128:(st4 + 1) * 128],
                                 w[:, half * 512:(half + 1) * 512],
                                 start=(hc == 0), stop=(hc == HC - 1))
                    xt = xtile[j % 2]
                    r0 = base + j * 128
                    a.dma("syq", xt, x[r0:r0 + 128, :])
                    a.tt(xt, xt, pd, ALU.add)
                    a.dma("gpq", xo[r0:r0 + 128, :], xt)



def _shard(x, core):
    return np.ascontiguousarray(x.reshape(NCORES, -1, *x.shape[1:])[core])


def run_l0_ffn(x_full, w_gate, w_up, w_down):
    w_gate, w_up, w_down = [np.ascontiguousarray(np.asarray(w, np.int32)) for w in (w_gate, w_up, w_down)]
    """x_full [16, 2048, 1024] fp32; weights int32 as given by the reference."""
    consts = host_constants()
    nc = build_l0_ffn()
    wgT = np.ascontiguousarray(w_gate.T)
    wuT = np.ascontiguousarray(w_up.T)
    wdT = np.ascontiguousarray(w_down.T)
    in_maps = []
    for c in range(NCORES):
        in_maps.append({
            "x": x_full.reshape(16 * T, D)[c * B_PER_CORE * T:(c + 1) * B_PER_CORE * T],
            "wgT": wgT, "wuT": wuT, "wdT": wdT,
            "ident": consts["ident_bf"],
        })
    res = run_bass_kernel_spmd(nc, in_maps, core_ids=list(range(NCORES)))
    out = np.concatenate([res.results[c]["xo"] for c in range(NCORES)], axis=0)
    return out.reshape(16, T, D)


def emit_attn(a, ctx, d, PS, com, xin, xout, variant):
    NTOK = B_PER_CORE * T
    if True:
        sb = ctx.sb
        x, xo = xin, xout
        wqT, woT = d["wqT"], d["woT"]
        identb, diagm, junk = com["identb"], com["diagm"], com["junk"]
        if variant == "b":
            wkT, wvT, alibik = d["wkT"], d["wvT"], d["alibik"]
        else:
            wqTs, wkT, wvT = d["wqTs"], d["wkT"], d["wvT"]
            dks, dkss, dkm, dkms, dv = d["dks"], d["dkss"], d["dkm"], d["dkms"], d["dv"]
            c1, c2 = d["c1"], d["c2"]
        xtile = [sb("xt0", [128, D], F32)] * 2
        xln_tm = sb("xln_tm", [128, D], BF16)
        xlnf = sb("xlnf", [128, 8, T], BF16)
        scratch = {"xsq": sb("xsq", [128, D], BF16), "s1": sb("s1", [128, 1], F32),
                   "s2": sb("s2", [128, 1], F32), "mv": sb("mv", [128, 2], F32),
                   "rstd": sb("rstd", [128, 1], F32)}
        wq = sb("wq", [128, 8, D], BF16)
        wo = sb("wo", [128, 8, D], BF16)
        if variant == "b":
            wk = sb("wk", [128, 8, D], BF16)
            wv = sb("wv", [128, 8, D], BF16)

        else:
            wqs = sb("wqs", [128, 8, D], BF16)
            wkt = sb("wkt", [128, 8, 192], BF16)
            wvt = sb("wvt", [128, 8, 192], BF16)
            dkst = sb("dkst", [128, 2, D], BF16)
            dksst = sb("dksst", [128, 2, D], BF16)
            dvt = sb("dvt", [128, 2, D], BF16)
            cro = sb("cro", [128, T], BF16)
            zk = sb("zk", [128, 2, T], BF16)
            xv = None

            t1 = sb("t1", [128, 512], F32)
            t2 = sb("t2", [128, 512], F32)
        vtm = sb("vtm", [128, 16, D], BF16)
        oall = sb("oall", [128, 16, D], BF16)
        if variant == "a":
            xv = oall[:, 0:4, :].rearrange("p a b -> p (a b)").rearrange("p (c b) -> p c b", c=2)
        qs = sb("qs", [128, T], BF16)
        ks = sb("ks", [128, T], BF16)
        pbuf = sb("pbuf", [128, T], BF16)
        pts = sb("pts", [128, T], BF16)
        rowmax = sb("rowmax", [128, 1], F32)
        rowsum = sb("rowsum", [128, 1], F32)
        nbias = sb("nbias", [128, 1], F32)
        recl = sb("recl", [128, 1], F32)
        of = sb("of", [128, 8, 128], BF16)
        PSA = PS["PSA"]
        PSB = PS["PSB"]
        PSC = PS["PSD"][:, 0:128]

        for c in range(8):
            a.dma("gpq", wq[:, c, :], wqT[c * 128:(c + 1) * 128, :])
            a.dma("gpq", wo[:, c, :], woT[c * 128:(c + 1) * 128, :])
        if variant == "b":
            for c in range(8):
                a.dma("gpq", wk[:, c, :], wkT[c * 128:(c + 1) * 128, :])
                a.dma("gpq", wv[:, c, :], wvT[c * 128:(c + 1) * 128, :])

        else:
            for c in range(8):
                a.dma("gpq", wqs[:, c, :], wqTs[c * 128:(c + 1) * 128, :])
                a.dma("gpq", wkt[:, c, :192], wkT[c * 128:(c + 1) * 128, :])
                a.dma("gpq", wvt[:, c, :192], wvT[c * 128:(c + 1) * 128, :])
            a.dma("syq", cro[0:64, :], c1)
            a.dma("syq", cro[64:128, :], c2)
            a.dma("gpq", dkst[:, 0, :], dks[0:128, :])
            a.dma("gpq", dkst[0:64, 1, :], dks[128:192, :])
            a.dma("gpq", dksst[:, 0, :], dkss[0:128, :])
            a.dma("gpq", dksst[0:64, 1, :], dkss[128:192, :])
            mskt = vtm[:, 0:2, :]
            mskst = vtm[:, 2:4, :]
            a.dma("syq", mskt[:, 0, :], dkm[0:128, :])
            a.dma("syq", mskt[0:64, 1, :], dkm[128:192, :])
            a.dma("syq", mskst[:, 0, :], dkms[0:128, :])
            a.dma("syq", mskst[0:64, 1, :], dkms[128:192, :])
            a.tt(dkst[:, 0, :], dkst[:, 0, :], mskt[:, 0, :], ALU.mult)
            a.tt(dkst[0:64, 1, :], dkst[0:64, 1, :], mskt[0:64, 1, :], ALU.mult)
            a.tt(dksst[:, 0, :], dksst[:, 0, :], mskst[:, 0, :], ALU.mult)
            a.tt(dksst[0:64, 1, :], dksst[0:64, 1, :], mskst[0:64, 1, :], ALU.mult)
            a.dma("gpq", dvt[:, 0, :], dv[0:128, :])
            a.dma("gpq", dvt[0:64, 1, :], dv[128:192, :])

        HD = 64
        scale = 1.0 / math.sqrt(HD)
        for b in range(B_PER_CORE):
            base = b * T
            for j in range(T // 128):
                xt = xtile[j % 2]
                a.dma("syq", xt, x[base + j * 128:base + (j + 1) * 128, :])
                emit_ln_tile(a, None, xt, xln_tm, scratch)
                emit_tm_to_fm(a, xln_tm, [xlnf[:, c, j * 128:(j + 1) * 128] for c in range(8)],
                              identb, PSB[:, 0:128])
            if variant == "a":
                for tt in range(4):
                    tsl = slice(tt * 512, (tt + 1) * 512)
                    pz = PSA[:, 0:512]
                    pz2 = PSA[:, 512:1024]
                    for (wt, dst) in ((wkt, zk), (wvt, xv)):
                        for c in range(8):
                            a.mm(pz, wt[:, c, 0:128], xlnf[:, c, tsl], start=(c == 0), stop=(c == 7))
                        a.copy(dst[:, 0, tsl], pz)
                        for c in range(8):
                            a.mm(pz2[0:64, :], wt[:, c, 128:192], xlnf[:, c, tsl], start=(c == 0), stop=(c == 7))
                        a.copy(dst[0:64, 1, tsl], pz2[0:64, :])
            for j in range(T // 128):
                pv = PSA[:, 1024:2048]
                if variant == "b":
                    for hf in range(2):
                        for c in range(8):
                            a.mm(pv[:, hf * 512:(hf + 1) * 512],
                                 xlnf[:, c, j * 128:(j + 1) * 128],
                                 wv[:, c, hf * 512:(hf + 1) * 512],
                                 start=(c == 0), stop=(c == 7))
                else:
                    for hf in range(2):
                        a.mm(pv[:, hf * 512:(hf + 1) * 512], xv[:, 0, j * 128:(j + 1) * 128],
                             dvt[:, 0, hf * 512:(hf + 1) * 512], start=True, stop=False)
                        a.mm(pv[:, hf * 512:(hf + 1) * 512], xv[0:64, 1, j * 128:(j + 1) * 128],
                             dvt[0:64, 1, hf * 512:(hf + 1) * 512], start=False, stop=True)
                a.copy(vtm[:, j, :], pv)
            for h in range(16):
                hsl = slice(h * 64, (h + 1) * 64)
                for tt in range(4):
                    tsl = slice(tt * 512, (tt + 1) * 512)
                    pq = PSA[:, 0:512]
                    pk = PSA[:, 512:1024]
                    if variant == "b":
                        for c in range(8):
                            a.mm(pq[0:64, :], wq[:, c, hsl], xlnf[:, c, tsl], start=(c == 0), stop=(c == 7))
                        a.copy(qs[0:64, tsl], pq[0:64, :])
                        for c in range(8):
                            a.mm(pk[0:64, :], wk[:, c, hsl], xlnf[:, c, tsl], start=(c == 0), stop=(c == 7))
                        a.copy(ks[0:64, tsl], pk[0:64, :])
                    else:
                        pq2 = PSA[:, 1024:1536]
                        for c in range(8):
                            a.mm(pq[0:64, :], wq[:, c, hsl], xlnf[:, c, tsl], start=(c == 0), stop=(c == 7))
                        for c in range(8):
                            a.mm(pq2[0:64, :], wqs[:, c, hsl], xlnf[:, c, tsl], start=(c == 0), stop=(c == 7))
                        a.tt(t1[0:64, :], pq[0:64, :], cro[0:64, tsl], ALU.mult)
                        a.tt(t2[0:64, :], pq2[0:64, :], cro[64:128, tsl], ALU.mult)
                        a.tt(qs[0:64, tsl], t1[0:64, :], t2[0:64, :], ALU.add)
                        a.mm(pq[0:64, :], dkst[:, 0, hsl], zk[:, 0, tsl], start=True, stop=False)
                        a.mm(pq[0:64, :], dkst[0:64, 1, hsl], zk[0:64, 1, tsl], start=False, stop=True)
                        a.mm(pq2[0:64, :], dksst[:, 0, hsl], zk[:, 0, tsl], start=True, stop=False)
                        a.mm(pq2[0:64, :], dksst[0:64, 1, hsl], zk[0:64, 1, tsl], start=False, stop=True)
                        a.tt(t1[0:64, :], pq[0:64, :], cro[0:64, tsl], ALU.mult)
                        a.tt(t2[0:64, :], pq2[0:64, :], cro[64:128, tsl], ALU.mult)
                        a.tt(ks[0:64, tsl], t1[0:64, :], t2[0:64, :], ALU.add)
                if variant == "b":
                    a.memset(qs[64:65, :], 1.0)
                    a.dma("syq", ks[64:65, :], alibik[h:h + 1, :])
                    KR = 65
                else:
                    KR = 64
                for qt in range(T // 128):
                    L = (qt + 1) * 128
                    S = PSA[:, 0:L]
                    for kb in range((L + 511) // 512):
                        n = min(512, L - kb * 512)
                        a.mm(PSA[:, kb * 512:kb * 512 + n], qs[0:KR, qt * 128:(qt + 1) * 128],
                             ks[0:KR, kb * 512:kb * 512 + n], start=True, stop=True)
                    a.tt(PSA[:, L - 128:L], PSA[:, L - 128:L], diagm, ALU.add)
                    a.reduce(rowmax, S, ALU.max)
                    a.ts(nbias, rowmax, -scale, ALU.mult)
                    a.act(pbuf[:, 0:L], S, AF.Exp, bias=nbias[:, 0:1], scale=scale,
                          accum_out=rowsum)
                    a.op("vector", lambda e, _r=recl, _s=rowsum: e.reciprocal(out=_r, in_=_s),
                         reads=[rowsum], writes=[recl])
                    for kb in range(L // 128):
                        a.trans(PSB[:, kb * 128:(kb + 1) * 128], pbuf[:, kb * 128:(kb + 1) * 128], identb)
                    a.copy(pts[:, 0:L], PSB[:, 0:L])
                    for kb in range(L // 128):
                        a.mm(PSC[:, 0:64], pts[:, kb * 128:(kb + 1) * 128],
                             vtm[:, kb, hsl], start=(kb == 0), stop=(kb == L // 128 - 1))
                    a.ts(oall[:, qt, hsl], PSC[:, 0:64], recl[:, 0:1], ALU.mult)
            for j in range(T // 128):
                emit_tm_to_fm(a, oall[:, j, :], [of[:, c, :] for c in range(8)], identb,
                              PSB[:, 0:128])
                pd = PSA[:, 0:1024]
                for c in range(8):
                    for hf in range(2):
                        a.mm(pd[:, hf * 512:(hf + 1) * 512], of[:, c, :],
                             wo[:, c, hf * 512:(hf + 1) * 512],
                             start=(c == 0), stop=(c == 7))
                xt = xtile[j % 2]
                a.dma("scq", xt, x[base + j * 128:base + (j + 1) * 128, :])
                a.tt(xt, xt, pd, ALU.add)
                a.dma("gpq", xo[base + j * 128:base + (j + 1) * 128, :], xt)


def rope_tables_interleaved(hd=64, base=10000.0):
    half = hd // 2
    freq = 1.0 / base ** (np.arange(half, dtype=np.float32) / half)
    ang = np.outer(np.arange(T, dtype=np.float32), freq)
    cos = np.cos(ang).astype(np.float32).T
    sin = np.sin(ang).astype(np.float32).T
    C1 = np.zeros((hd, T), np.float32)
    C2 = np.zeros((hd, T), np.float32)
    C1[0::2] = cos
    C1[1::2] = cos
    C2[0::2] = -sin
    C2[1::2] = sin
    return C1, C2


def host_attn_a_decoders(key_decoder, value_decoder):
    F_, H_, KR_ = 32, 16, 3
    dk = np.zeros((192, D), np.float32)
    for cp in range(2):
        for c in range(2):
            cc = c ^ cp
            dk[np.arange(F_)[:, None, None] * 6 + np.arange(KR_)[None, :, None] * 2 + cp,
               np.arange(H_)[None, None, :] * 64 + 2 * np.arange(F_)[:, None, None] + c] =                 np.transpose(key_decoder[:, :, :, cc], (0, 2, 1))
    dks = dk.reshape(192, 16, 32, 2)[:, :, :, ::-1].reshape(192, D)
    row_odd = (np.arange(192) % 2 == 1)
    col_even = (np.arange(D) % 2 == 0)
    m = np.ones((192, D), np.float32)
    m[np.ix_(row_odd, col_even)] = -1.0
    ms = np.ones((192, D), np.float32)
    ms[np.ix_(row_odd, ~col_even)] = -1.0
    dv = np.zeros((192, D), np.float32)
    dv[:, :] = np.transpose(value_decoder, (1, 0, 2)).reshape(192, D)
    return (np.ascontiguousarray(dk), np.ascontiguousarray(dks),
            m.astype(ml_dtypes.bfloat16), ms.astype(ml_dtypes.bfloat16),
            np.ascontiguousarray(dv))


def run_attn_a(x_full, Wq, Wk, Wv, key_decoder, value_decoder, Wproj):
    Wq, Wk, Wv, Wproj = [np.asarray(w, np.float32) for w in (Wq, Wk, Wv, Wproj)]
    consts = host_constants()
    nc = build_attn("a")
    perm = np.arange(D) ^ 1
    dk, dks_sw, m, ms, dv = host_attn_a_decoders(key_decoder, value_decoder)
    C1, C2 = rope_tables_interleaved()
    in_common = {
        "wqT": np.ascontiguousarray(Wq), "wqTs": np.ascontiguousarray(Wq[:, perm]),
        "wkT": np.ascontiguousarray(Wk), "wvT": np.ascontiguousarray(Wv),
        "dks": dk, "dkss": dks_sw, "dkm": m, "dkms": ms, "dv": dv,
        "woT": np.ascontiguousarray(Wproj),
        "c1": C1.astype(ml_dtypes.bfloat16), "c2": C2.astype(ml_dtypes.bfloat16),
        "ident": consts["ident_bf"], "diagmask": consts["diagmask"],
    }
    in_maps = []
    for c in range(NCORES):
        mm_ = dict(in_common)
        mm_["x"] = x_full.reshape(16 * T, D)[c * B_PER_CORE * T:(c + 1) * B_PER_CORE * T]
        in_maps.append(mm_)
    res = run_bass_kernel_spmd(nc, in_maps, core_ids=list(range(NCORES)))
    out = np.concatenate([res.results[c]["xo"] for c in range(NCORES)], axis=0)
    return out.reshape(16, T, D)


def host_alibi(scale=8.0):
    slopes = np.array([2.0 ** (-(h + 1) / 2.0) for h in range(16)], np.float32)
    j = np.arange(T, dtype=np.float32)
    return (-slopes[:, None] * j[None, :] * scale).astype(ml_dtypes.bfloat16)


def run_attn_b(x_full, Wq, Wk, Wv, Wo):
    Wq, Wk, Wv, Wo = [np.asarray(w, np.float32) for w in (Wq, Wk, Wv, Wo)]
    consts = host_constants()
    nc = build_attn("b")
    in_common = {
        "wqT": np.ascontiguousarray(Wq.T), "wkT": np.ascontiguousarray(Wk.T),
        "wvT": np.ascontiguousarray(Wv.T), "woT": np.ascontiguousarray(Wo.T),
        "ident": consts["ident_bf"], "diagmask": consts["diagmask"],
        "alibik": host_alibi(),
    }
    in_maps = []
    for c in range(NCORES):
        m = dict(in_common)
        m["x"] = x_full.reshape(16 * T, D)[c * B_PER_CORE * T:(c + 1) * B_PER_CORE * T]
        in_maps.append(m)
    res = run_bass_kernel_spmd(nc, in_maps, core_ids=list(range(NCORES)))
    out = np.concatenate([res.results[c]["xo"] for c in range(NCORES)], axis=0)
    return out.reshape(16, T, D)


def emit_fp4_quant(a, src, dst, q, nb, width):
    absx, t1i, cq = q["absx"], q["t1i"], q["cq"]
    bmax, bscale, binv = q["bmax"], q["bscale"], q["binv"]
    w3 = (src, absx, AF.Abs)
    a.act(absx[:, 0:width], src, AF.Abs)
    a.op("vector", lambda e: e.tensor_reduce(
        out=bmax[:, 0:nb].unsqueeze(2), in_=absx[:, 0:width].rearrange("p (b w) -> p b w", w=16),
        axis=AX.X, op=ALU.max), reads=[absx], writes=[bmax])
    a.ts(bscale[:, 0:nb], bmax[:, 0:nb], 1.0 / 6.0, ALU.mult, 1e-12, ALU.max)
    a.op("vector", lambda e: e.reciprocal(out=binv[:, 0:nb], in_=bscale[:, 0:nb]),
         reads=[bscale], writes=[binv])
    bc = binv[:, 0:nb].unsqueeze(2).broadcast_to([128, nb, 16])
    a.op("vector", lambda e: e.tensor_tensor(
        out=absx[:, 0:width].rearrange("p (b w) -> p b w", w=16),
        in0=absx[:, 0:width].rearrange("p (b w) -> p b w", w=16), in1=bc, op=ALU.mult),
        reads=[absx, binv], writes=[absx])
    a.ts(t1i[:, 0:width], absx[:, 0:width], 4.0, ALU.min, 2.0, ALU.subtract, eng="gpsimd")
    a.ts(cq[:, 0:width], absx[:, 0:width], 5.0, ALU.is_gt, 2.0, ALU.mult)
    a.stt(cq[:, 0:width], t1i[:, 0:width], 0.0, cq[:, 0:width], ALU.max, ALU.add)
    a.ts(t1i[:, 0:width], absx[:, 0:width], 2.0, ALU.min, 2.0, ALU.mult, eng="gpsimd")
    a.stt(cq[:, 0:width], t1i[:, 0:width], 0.5, cq[:, 0:width], ALU.mult, ALU.add)
    a.act(absx[:, 0:width], src, AF.Sign)
    bs = bscale[:, 0:nb].unsqueeze(2).broadcast_to([128, nb, 16])
    a.op("vector", lambda e: e.tensor_tensor(
        out=absx[:, 0:width].rearrange("p (b w) -> p b w", w=16),
        in0=absx[:, 0:width].rearrange("p (b w) -> p b w", w=16), in1=bs, op=ALU.mult),
        reads=[absx, bscale], writes=[absx])
    a.tt(dst, cq[:, 0:width], absx[:, 0:width], ALU.mult)


def emit_l1_ffn(a, ctx, d, PS, com, xin, xout):
    hidden = 2816
    HC = hidden // 128
    NB_D = D // 16
    NB_H = hidden // 16
    NTOK = B_PER_CORE * T

    if True:
        sb = ctx.sb
        x, xo = xin, xout
        wg_in, wu_in, wd_in = d["wg"], d["wu"], d["wd"]
        identb, junk = com["identb"], com["junk"]
        xtile = sb("xt0", [128, D], F32)
        xln_tm = sb("xln_tm", [128, D], F32)
        scratch = {"xsq": sb("xsq", [128, D], BF16), "s1": sb("s1", [128, 1], F32),
                   "s2": sb("s2", [128, 1], F32), "mv": sb("mv", [128, 2], F32),
                   "rstd": sb("rstd", [128, 1], F32)}
        q = {"absx": sb("absx", [128, hidden], F32),
             "t1i": sb("t1i", [128, hidden], mybir.dt.int16),
             
             "cq": sb("cq", [128, hidden], F32),
             "bmax": sb("bmax", [128, NB_H], F32), "bscale": sb("bscale", [128, NB_H], F32),
             "binv": sb("binv", [128, NB_H], F32)}
        wqtm = sb("wqtm", [128, hidden], BF16)
        wgq = sb("wgq", [128, 8, hidden], BF16)
        wuq = sb("wuq", [128, 8, hidden], BF16)
        wdq = sb("wdq", [128, HC, D], BF16)
        xq_tm = sb("xq_tm", [128, D], BF16)
        xqf = sb("xqf", [128, 8, 128], BF16)
        htile = sb("htile", [128, hidden], F32)
        hq_tm = sb("hq_tm", [128, hidden], BF16)
        hqf = sb("hqf", [128, HC, 128], BF16)
        PSA = PS["PSA"]
        PSB = PS["PSB"][:, 0:256]


        for (w_in, dstfm, nrows, wwidth, nbw) in (
                (wg_in, wgq, hidden, D, NB_D), (wu_in, wuq, hidden, D, NB_D),
                (wd_in, wdq, D, hidden, NB_H)):
            RC = nrows // 128
            for rc in range(RC):
                src = htile[:, 0:wwidth]
                a.dma("syq", src, w_in[rc * 128:(rc + 1) * 128, :])
                emit_fp4_quant(a, src, wqtm[:, 0:wwidth], q, nbw, wwidth)
                for c in range(wwidth // 128):
                    a.trans(PSB[:, 0:128], wqtm[:, c * 128:(c + 1) * 128], identb)
                    a.copy(dstfm[:, c, rc * 128:(rc + 1) * 128], PSB[:, 0:128])

        ISL = [(i * 512, min(512, hidden - i * 512)) for i in range((hidden + 511) // 512)]
        for j in range(NTOK // 128):
            a.dma("syq", xtile, x[j * 128:(j + 1) * 128, :])
            emit_ln_tile(a, None, xtile, xln_tm, scratch)
            emit_fp4_quant(a, xln_tm, xq_tm, q, NB_D, D)
            for c in range(8):
                a.trans(PSB[:, 0:128], xq_tm[:, c * 128:(c + 1) * 128], identb)
                a.copy(xqf[:, c, :], PSB[:, 0:128])
            for (i0, n) in ISL:
                pg = PSA[:, 0:512]
                pu = PSA[:, 512:1024]
                for c in range(8):
                    a.mm(pg[:, 0:n], wgq[:, c, i0:i0 + n], xqf[:, c, :].transpose([0, 1]) if False else xqf[:, c, :],
                         start=(c == 0), stop=(c == 7)) if False else None
                for c in range(8):
                    a.mm(pg[:, 0:n], xqf[:, c, :], wgq[:, c, i0:i0 + n],
                         start=(c == 0), stop=(c == 7))
                for c in range(8):
                    a.mm(pu[:, 0:n], xqf[:, c, :], wuq[:, c, i0:i0 + n],
                         start=(c == 0), stop=(c == 7))
                a.act(htile[:, i0:i0 + n], pg[:, 0:n], AF.Silu)
                a.tt(htile[:, i0:i0 + n], htile[:, i0:i0 + n], pu[:, 0:n], ALU.mult)
            emit_fp4_quant(a, htile, hq_tm, q, NB_H, hidden)
            for c in range(HC):
                a.trans(PSB[:, 0:128], hq_tm[:, c * 128:(c + 1) * 128], identb)
                a.copy(hqf[:, c, :], PSB[:, 0:128])
            pd = PSA[:, 1024:2048]
            for hf in range(2):
                for c in range(HC):
                    a.mm(pd[:, hf * 512:(hf + 1) * 512], hqf[:, c, :],
                         wdq[:, c, hf * 512:(hf + 1) * 512],
                         start=(c == 0), stop=(c == HC - 1))
            a.tt(xtile, xtile, pd[:, 0:1024], ALU.add)
            a.dma("gpq", xo[j * 128:(j + 1) * 128, :], xtile)


def run_l1_ffn(x_full, w_gate, w_up, w_down):
    consts = host_constants()
    nc = build_l1_ffn()
    in_common = {"wg": np.ascontiguousarray(w_gate), "wu": np.ascontiguousarray(w_up),
                 "wd": np.ascontiguousarray(w_down), "ident": consts["ident_bf"]}
    in_maps = []
    for c in range(NCORES):
        m = dict(in_common)
        m["x"] = x_full.reshape(16 * T, D)[c * B_PER_CORE * T:(c + 1) * B_PER_CORE * T]
        in_maps.append(m)
    res = run_bass_kernel_spmd(nc, in_maps, core_ids=list(range(NCORES)))
    out = np.concatenate([res.results[c]["xo"] for c in range(NCORES)], axis=0)
    return out.reshape(16, T, D)


def emit_mamba(a, ctx, d, PS, com, xin, xout):
    E = 2048
    NSTATE = 16
    TC = 512
    NTC = T // TC
    NTOK = B_PER_CORE * T

    if True:
        sb = ctx.sb
        x, xo = xin, xout
        winT, wxT, wdtT, woutT = d["winT"], d["wxT"], d["wdtT"], d["woutT"]
        convr, convb, bdt, dprm, alr = d["convr"], d["convb"], d["bdt"], d["dprm"], d["alr"]
        sel_in, selt_in, selb_in = d["sel"], d["selt"], d["selb"]
        lnw_in, lnb_in = d["lnw"], d["lnb"]
        identb, junk = com["identb"], com["junk"]
        win = sb("win", [128, 8, 2 * E], BF16)
        wx = sb("wx", [128, 16, 96], BF16)
        wdt = sb("wdt", [64, E], BF16)
        wout = [sb(f"wout{i}", [128, 2, D], BF16) for i in range(2)]
        conw = sb("conw", [128, 64], F32)
        conb = sb("conb", [128, 16], F32)
        bdtt = sb("bdtt", [128, 16], F32)
        dprt = sb("dprt", [128, 16], F32)
        alrt = sb("alrt", [128, 256], F32)
        nega = alrt
        self_ = sb("selr", [128, 16, 128], BF16)
        selt = sb("seltr", [128, 16, 128], BF16)
        selbc = sb("selbc", [128, 256], BF16)
        lnw = sb("lnwr", [128, D], F32)
        lnb = sb("lnbr", [128, D], F32)
        xtile = sb("xt0", [128, D], F32)
        xln_tm = sb("xln_tm", [128, D], BF16)
        xlnf = sb("xlnf", [128, 8, TC], BF16)
        scratch = {"xsq": sb("xsq", [128, D], BF16), "s1": sb("s1", [128, 1], F32),
                   "s2": sb("s2", [128, 1], F32), "mv": sb("mv", [128, 2], F32),
                   "rstd": sb("rstd", [128, 1], F32)}
        xs = sb("xs", [128, 16, TC + 3], BF16)
        uf = sb("uf", [128, 16, TC], BF16)
        zf = sb("zf", [128, 16, TC], BF16)
        yf = sb("yf", [128, 16, TC], BF16)
        cacc = sb("cacc", [128, TC], F32)
        reld = cacc
        xdb = sb("xdb", [128, TC], BF16)
        brep = sb("brep", [128, TC], F32)
        crep = sb("crep", [128, TC], F32)
        deltac = sb("deltac", [128, TC], BF16)
        absd = sb("absd", [128, TC], F32)
        duc = sb("duc", [128, TC], BF16)
        at = sb("at", [128, TC], F32)
        bt = sb("bt", [128, TC], F32)
        ht = sb("ht", [128, TC], F32)
        gt = sb("gt", [128, TC], BF16)
        y2 = sb("y2", [128, TC], F32)
        zs = sb("zs", [128, TC], F32)
        hstate = sb("hstate", [128, 256], F32)
        mo = sb("mo", [128, D], F32)
        PSA = PS["PSA"]
        PSB = PS["PSB"][:, 0:256]

        for c in range(8):
            a.dma("gpq", win[:, c, :], winT[c * 128:(c + 1) * 128, :])
        for c in range(16):
            a.dma("gpq", wx[:, c, :], wxT[c * 128:(c + 1) * 128, :])
        a.dma("gpq", wdt, wdtT)
        for (dst, src) in ((conw, convr), (conb, convb), (bdtt, bdt), (dprt, dprm),
                           (alrt, alr), (lnw, lnw_in), (lnb, lnb_in)):
            a.dma("syq", dst, src)
        a.dma("syq", self_.rearrange("p a b -> p (a b)"), sel_in)
        a.dma("syq", selt.rearrange("p a b -> p (a b)"), selt_in)
        a.dma("syq", selbc, selb_in)
        a.act(nega, alrt, AF.Exp)
        a.ts(nega, nega, -1.0, ALU.mult)

        for b in range(B_PER_CORE):
            base = b * T
            a.memset(hstate, 0.0)
            for tc in range(NTC):
                t0 = tc * TC
                for j in range(TC // 128):
                    a.dma("syq", xtile, x[base + t0 + j * 128:base + t0 + (j + 1) * 128, :])
                    emit_ln_tile(a, None, xtile, xln_tm, scratch)
                    emit_tm_to_fm(a, xln_tm, [xlnf[:, c, j * 128:(j + 1) * 128] for c in range(8)],
                                  identb, PSB[:, 0:128])
                if tc == 0:
                    for c in range(16):
                        a.memset(xs[:, c, 0:3], 0.0)
                for ec in range(16):
                    pz = PSA[:, 0:512]
                    for c in range(8):
                        a.mm(pz, win[:, c, ec * 128:(ec + 1) * 128], xlnf[:, c, :],
                             start=(c == 0), stop=(c == 7))
                    a.copy(xs[:, ec, 3:TC + 3], pz)
                    pz2 = PSA[:, 512:1024]
                    for c in range(8):
                        a.mm(pz2, win[:, c, E + ec * 128:E + (ec + 1) * 128], xlnf[:, c, :],
                             start=(c == 0), stop=(c == 7))
                    a.copy(zf[:, ec, :], pz2)
                for ec in range(16):
                    a.ts(cacc, xs[:, ec, 0:TC], conw[:, ec * 4 + 0:ec * 4 + 1], ALU.mult)
                    for k in (1, 2, 3):
                        a.stt(cacc, xs[:, ec, k:TC + k], conw[:, ec * 4 + k:ec * 4 + k + 1],
                              cacc, ALU.mult, ALU.add)
                    a.act(uf[:, ec, :], cacc, AF.Silu, bias=conb[:, ec:ec + 1], scale=1.0)
                for ec in range(16):
                    a.copy(xs[:, ec, 0:3], xs[:, ec, TC:TC + 3])
                pxd = PSA[:, 1024:1536]
                for c in range(16):
                    a.mm(pxd[0:96, :], wx[:, c, :], uf[:, c, :], start=(c == 0), stop=(c == 15))
                a.copy(xdb[0:96, :], pxd[0:96, :])
                a.mm(PSA[0:128, 0:512], selbc[0:96, 0:128], xdb[0:96, :], start=True, stop=True)
                a.copy(brep, PSA[:, 0:512])
                a.mm(PSA[0:128, 512:1024], selbc[0:96, 128:256], xdb[0:96, :], start=True, stop=True)
                a.copy(crep, PSA[:, 512:1024])
                for dc in range(16):
                    pdl = PSA[:, 1536:2048]
                    a.mm(pdl, wdt[:, dc * 128:(dc + 1) * 128], xdb[0:64, :], start=True, stop=True)
                    a.act(absd, pdl, AF.Abs, bias=bdtt[:, dc:dc + 1], scale=1.0)
                    a.act(reld, pdl, AF.Relu, bias=bdtt[:, dc:dc + 1], scale=1.0)
                    a.act(absd, absd, AF.Exp, scale=-1.0)
                    a.act(absd, absd, AF.Ln, bias=1.0, scale=1.0)
                    a.tt(deltac, reld, absd, ALU.add)
                    a.tt(duc, deltac, uf[:, dc, :], ALU.mult)
                    for sub in range(16):
                        ct = dc * 16 + sub
                        prd = PSA[:, 0:512]
                        a.mm(prd, self_[:, sub, :], deltac, start=True, stop=True)
                        a.act(at, prd, AF.Exp, bias=0.0, scale=nega[:, ct:ct + 1])
                        prd2 = PSA[:, 512:1024]
                        a.mm(prd2, self_[:, sub, :], duc, start=True, stop=True)
                        a.tt(bt, prd2, brep, ALU.mult)
                        a.op("vector", lambda e, _a=at, _b=bt, _h=ht, _i=hstate[:, ct:ct + 1]:
                             e.tensor_tensor_scan(out=_h, data0=_a, data1=_b, initial=_i,
                                                  op0=ALU.mult, op1=ALU.add),
                             reads=[at, bt, hstate[:, ct:ct + 1]], writes=[ht])
                        a.copy(hstate[:, ct:ct + 1], ht[:, TC - 1:TC])
                        a.tt(gt, ht, crep, ALU.mult)
                        a.mm(PSA[:, 1024:1536], selt[:, sub, :], gt,
                             start=(sub == 0), stop=(sub == 15))
                    a.stt(y2, uf[:, dc, :], dprt[:, dc:dc + 1], PSA[:, 1024:1536],
                          ALU.mult, ALU.add)
                    a.act(zs, zf[:, dc, :], AF.Silu)
                    a.tt(yf[:, dc, :], y2, zs, ALU.mult)
                for j in range(TC // 128):
                    pd = PSA[:, 0:1024]
                    for ec in range(16):
                        w = wout[ec % 2]
                        a.dma("gpq", w[:, 0, :], woutT[ec * 128:(ec + 1) * 128, :])
                        for hf in range(2):
                            a.mm(pd[:, hf * 512:(hf + 1) * 512], yf[:, ec, j * 128:(j + 1) * 128],
                                 w[:, 0, hf * 512:(hf + 1) * 512],
                                 start=(ec == 0), stop=(ec == 15))
                    r0 = base + t0 + j * 128
                    a.dma("scq", xtile, x[r0:r0 + 128, :])
                    emit_ln_tile(a, None, xtile, xln_tm, scratch)
                    a.op("vector", lambda e, _o=mo, _p=pd, _x=xln_tm:
                         e.tensor_tensor(out=_o, in0=_p, in1=_x, op=ALU.add),
                         reads=[pd, xln_tm], writes=[mo])
                    emit_ln_tile(a, None, mo, mo, scratch)
                    a.tt(mo, mo, lnw, ALU.mult)
                    a.tt(mo, mo, lnb, ALU.add)
                    a.tt(xtile, xtile, mo, ALU.add)
                    a.dma("gpq", xo[r0:r0 + 128, :], xtile)


def host_mamba_consts():
    sel = np.zeros((16, 128, 128), np.float32)
    for sub in range(16):
        for p in range(128):
            sel[sub, 8 * sub + p % 8, p] = 1.0
    selt = np.transpose(sel, (0, 2, 1))
    sel_h = np.ascontiguousarray(np.transpose(sel, (1, 0, 2)).reshape(128, 16 * 128)).astype(ml_dtypes.bfloat16)
    selt_h = np.ascontiguousarray(np.transpose(selt, (1, 0, 2)).reshape(128, 16 * 128)).astype(ml_dtypes.bfloat16)
    selbc = np.zeros((128, 256), np.float32)
    for p in range(128):
        selbc[64 + p // 8, p] = 1.0
        selbc[80 + p // 8, 128 + p] = 1.0
    return sel_h, selt_h, selbc.astype(ml_dtypes.bfloat16)


def run_mamba(x_full, W_in, conv_w, conv_b, W_x, W_dt, b_dt, A_log, D_param, W_out, ln_w, ln_b):
    consts = host_constants()
    nc = build_mamba()
    sel_h, selt_h, selbc = host_mamba_consts()
    alr = np.zeros((128, 256), np.float32)
    p = np.arange(128)
    for ct in range(256):
        alr[p, ct] = A_log[8 * ct + (p % 8), p // 8]
    in_common = {
        "winT": np.ascontiguousarray(W_in.T), "wxT": np.ascontiguousarray(W_x.T),
        "wdtT": np.ascontiguousarray(W_dt.T), "woutT": np.ascontiguousarray(W_out.T),
        "convr": np.ascontiguousarray(conv_w[:, 0, :].reshape(16, 128, 4).transpose(1, 0, 2).reshape(128, 64)),
        "convb": np.ascontiguousarray(conv_b.reshape(16, 128).T),
        "bdt": np.ascontiguousarray(b_dt.reshape(16, 128).T),
        "dprm": np.ascontiguousarray(D_param.reshape(16, 128).T),
        "alr": alr,
        "sel": sel_h, "selt": selt_h, "selb": selbc,
        "lnw": np.ascontiguousarray(np.tile(ln_w[None, :], (128, 1))),
        "lnb": np.ascontiguousarray(np.tile(ln_b[None, :], (128, 1))),
        "ident": consts["ident_bf"],
    }
    in_maps = []
    for c in range(NCORES):
        m = dict(in_common)
        m["x"] = x_full.reshape(16 * T, D)[c * B_PER_CORE * T:(c + 1) * B_PER_CORE * T]
        in_maps.append(m)
    res = run_bass_kernel_spmd(nc, in_maps, core_ids=list(range(NCORES)))
    out = np.concatenate([res.results[c]["xo"] for c in range(NCORES)], axis=0)
    return out.reshape(16, T, D)


def emit_attn_pal(a, ctx, d, PS, com, xin, xout, pnscr):
    NTOK = B_PER_CORE * T

    if True:
        sb = ctx.sb
        x, xo = xin, xout
        wi_in, wp_in, wo_in, palT_in = d["wi"], d["wp"], d["wo"], d["palT"]
        w1g_in, w1d_in, b1_in, w2_in, b2_in = d["w1g"], d["w1d"], d["b1"], d["w2"], d["b2"]
        wc0_in, wc1_in, bc_in, wm_in, bm_in = d["wcc0"], d["wcc1"], d["bc"], d["wm"], d["bm"]
        cos_in, sin_in = d["cosr"], d["sinr"]
        m0_in, keepm_in = d["m0"], d["keepm"]
        iota256_in, iota16r_in, tcol_in, eqsel_in = d["iota256"], d["iota16r"], d["tcol"], d["eqsel"]
        selg_in, selx_in = d["selg"], d["selx"]
        identb, identf, diagm, junk = com["identb"], com["identf"], com["diagm"], com["junk"]
        wi = sb("wit", [128, 8, D], BF16)
        wp = sb("wpt", [128, 8, D], BF16)
        wot = sb("wot", [128, 8, D], BF16)
        palt = sb("paltt", [128, 2, D], BF16)
        w1g = sb("w1gt", [16, 64], BF16)
        w1d = sb("w1dt", [1, 64], BF16)
        b1 = sb("b1t", [64, 1], F32)
        w2 = sb("w2t", [64, 64], BF16)
        b2 = sb("b2t", [64, 1], F32)
        wc0 = sb("wc0t", [64, 1], BF16)
        wc1 = sb("wc1t", [64, 1], BF16)
        bct = sb("bct", [1, 2], F32)
        wm = sb("wmt", [64, 1], BF16)
        bmt = sb("bmt", [1, 1], F32)
        m0 = sb("m0t", [16, T], BF16)
        keepm = sb("keepmt", [128, 32], F32)
        iota256 = sb("iota256t", [128, 256], F32)
        iota16r = sb("iota16rt", [128, 256], F32)
        tcol = sb("tcolt", [128, 1], F32)
        eqsel = sb("eqselt", [128, 16], F32)
        selg = sb("selgt", [128, 16, 128], F32)
        selx = sb("selxt", [128, 8, 16], BF16)
        xtile = sb("xt0", [128, D], F32)
        xln_tm = sb("xln_tm", [128, D], BF16)
        xlnf = sb("xlnf", [128, 8, 128], BF16)
        scratch = {"xsq": sb("xsq", [128, D], BF16), "s1": sb("s1", [128, 1], F32),
                   "s2": sb("s2", [128, 1], F32), "mv": sb("mv", [128, 2], F32),
                   "rstd": sb("rstd", [128, 1], F32)}
        costl = sb("costl", [128, 512], F32)
        sintl = sb("sintl", [128, 512], F32)
        ptm = sb("ptm", [128, D], BF16)
        pnrm = sb("pnrm", [128, D], BF16)
        pf = sb("pf", [128, 8, T], BF16)
        itm = sb("itm", [128, 16, D], BF16)
        iff = sb("iff", [128, 8, 128], BF16)
        sx2 = sb("sx2", [128, T], F32)
        m1 = sb("m1t", [128, 8], F32)
        m2 = sb("m2t", [128, 8], F32)
        iu32 = sb("iu32", [128, 16], U32)
        idxf = sb("idxf", [128, 16], F32)
        tq = sb("tq", [128, 1], F32)
        d0 = sb("d0", [128, 16], F32)
        d0b = sb("d0b", [128, 16], BF16)
        drow = sb("drow", [128, 2048], BF16)
        gidxf = sb("gidxf", [128, 1], F32)
        gidxu = sb("gidxu", [128, 1], U32)
        gtmp = sb("gtmp", [128, 16], F32)
        gath = sb("gath", [128, D], BF16)
        afm = sb("afm", [128, 8, 128], BF16)
        gsb = sb("gsb", [128, 128], BF16)
        relbuf = sb("relbuf", [16, T], BF16)
        h1sb = sb("h1sb", [64, T], BF16)
        h2sb = sb("h2sb", [64, T], BF16)
        mixrow = sx2
        gxrow = sb("gxrow", [1, T], BF16)
        gyrow = sb("gyrow", [1, T], BF16)
        mixT = sb("mixT", [128, 16], F32)
        wsoft = sb("wsoft", [128, 16], F32)
        gxT = sb("gxT", [128, 16], F32)
        gyT = sb("gyT", [128, 16], F32)
        rowmax = sb("rowmax", [128, 1], F32)
        rowsum = sb("rowsum", [128, 1], F32)
        nbias = sb("nbias", [128, 1], F32)
        recl = sb("recl", [128, 1], F32)
        fx = sb("fx", [128, 16], F32)
        fy = sb("fy", [128, 16], F32)
        x0i = sb("x0i", [128, 16], mybir.dt.int16)
        x0f = sb("x0f", [128, 16], F32)
        x1f = sb("x1f", [128, 16], F32)
        wx = sb("wxq", [128, 16], F32)
        y0f = sb("y0f", [128, 16], F32)
        y1f = sb("y1f", [128, 16], F32)
        wy = sb("wyq", [128, 16], F32)
        axw = sb("axw", [128, 256], F32)
        byw = sb("byw", [128, 256], F32)
        ctmp = sb("ctmp", [128, 256], F32)
        coeff = sb("coeff", [128, 256], F32)
        coefb = sb("coefb", [128, 256], BF16)
        cf = sb("cf", [128, 2, 128], BF16)
        vtmb = sb("vtmb", [128, D], BF16)
        vf = sb("vf", [128, 8, 128], BF16)
        PSA = PS["PSA"]
        PSB = PS["PSB"][:, 0:1024]
        PSD = PS["PSD"]

        for c in range(8):
            a.dma("gpq", wi[:, c, :], wi_in[c * 128:(c + 1) * 128, :])
            a.dma("gpq", wp[:, c, :], wp_in[c * 128:(c + 1) * 128, :])
            a.dma("gpq", wot[:, c, :], wo_in[c * 128:(c + 1) * 128, :])
        a.dma("gpq", palt[:, 0, :], palT_in[0:128, :])
        a.dma("gpq", palt[:, 1, :], palT_in[128:256, :])
        for (dst, src) in ((w1g, w1g_in), (w1d, w1d_in), (w2, w2_in),
                           (wc0, wc0_in), (wc1, wc1_in), (wm, wm_in)):
            a.dma("gpq", dst, src)
        for (dst, src) in ((b1, b1_in), (b2, b2_in), (bct, bc_in), (bmt, bm_in),
                           (m0, m0_in), (keepm, keepm_in), (iota256, iota256_in),
                           (iota16r, iota16r_in), (tcol, tcol_in), (eqsel, eqsel_in)):
            a.dma("syq", dst, src)
        a.dma("syq", selg.rearrange("p a b -> p (a b)"), selg_in)
        a.dma("syq", selx.rearrange("p a b -> p (a b)"), selx_in)

        import concourse.bass as _bass
        for b in range(B_PER_CORE):
            base = b * T
            for j in range(T // 128):
                r0 = base + j * 128
                a.dma("syq", xtile, x[r0:r0 + 128, :])
                emit_ln_tile(a, None, xtile, xln_tm, scratch)
                emit_tm_to_fm(a, xln_tm, [xlnf[:, c, :] for c in range(8)], identb,
                              PSB[:, 0:128])
                a.dma("syq", costl, cos_in[j * 128:(j + 1) * 128, :])
                a.dma("syq", sintl, sin_in[j * 128:(j + 1) * 128, :])
                for (wt, is_p) in ((wp, True), (wi, False)):
                    pp = PSA[:, 0:1024]
                    for hf in range(2):
                        for c in range(8):
                            a.mm(pp[:, hf * 512:(hf + 1) * 512], xlnf[:, c, :],
                                 wt[:, c, hf * 512:(hf + 1) * 512],
                                 start=(c == 0), stop=(c == 7))
                    a.tt(sx2[:, 0:512], pp[:, 0:512], costl, ALU.mult)
                    a.tt(sx2[:, 512:1024], pp[:, 512:1024], sintl, ALU.mult)
                    a.tt(ptm[:, 0:512], sx2[:, 0:512], sx2[:, 512:1024], ALU.subtract)
                    a.tt(sx2[:, 0:512], pp[:, 512:1024], costl, ALU.mult)
                    a.tt(sx2[:, 512:1024], pp[:, 0:512], sintl, ALU.mult)
                    a.tt(ptm[:, 512:1024], sx2[:, 0:512], sx2[:, 512:1024], ALU.add)
                    a.act(scratch["xsq"], ptm, AF.Square, accum_out=scratch["s2"])
                    a.act(scratch["s1"], scratch["s2"], AF.Sqrt)
                    a.ts(scratch["s1"], scratch["s1"], 1e-12, ALU.max)
                    a.op("vector", lambda e, _o=scratch["rstd"], _i=scratch["s1"]:
                         e.reciprocal(out=_o, in_=_i), reads=[scratch["s1"]], writes=[scratch["rstd"]])
                    a.ts(pnrm, ptm, scratch["rstd"][:, 0:1], ALU.mult)
                    if is_p:
                        a.dma("gpq", pnscr[b * 4096 + j * 128:b * 4096 + (j + 1) * 128, :], pnrm)
                        emit_tm_to_fm(a, ptm, [pf[:, c, j * 128:(j + 1) * 128] for c in range(8)],
                                      identb, PSB[:, 0:128])
                    else:
                        a.dma("gpq", pnscr[b * 4096 + 2048 + j * 128:b * 4096 + 2048 + (j + 1) * 128, :], pnrm)
                        a.copy(itm[:, j, :], ptm)
            for qt in range(T // 128):
                L = (qt + 1) * 128
                emit_tm_to_fm(a, itm[:, qt, :], [iff[:, c, :] for c in range(8)], identb,
                              PSB[:, 0:128])
                for kb in range((L + 511) // 512):
                    n = min(512, L - kb * 512)
                    for c in range(8):
                        a.mm(PSA[:, kb * 512:kb * 512 + n], iff[:, c, :],
                             pf[:, c, kb * 512:kb * 512 + n], start=(c == 0), stop=(c == 7))
                S = PSA[:, 0:L]
                a.tt(PSA[:, L - 128:L], PSA[:, L - 128:L], diagm, ALU.add)
                a.op("vector", lambda e, _m=m1, _s=S: e.max(out=_m, in_=_s), reads=[S], writes=[m1])
                a.op("vector", lambda e, _o=sx2, _m=m1, _s=S, _L=L:
                     e.match_replace(out=_o[:, 0:_L], in_to_replace=_m, in_values=_s, imm_value=NEG),
                     reads=[S, m1], writes=[sx2])
                a.op("vector", lambda e, _m=m2, _s=sx2, _L=L: e.max(out=_m, in_=_s[:, 0:_L]),
                     reads=[sx2], writes=[m2])
                a.op("vector", lambda e, _o=iu32, _m=m1, _s=S: e.max_index(out=_o[:, 0:8], in_max=_m, in_values=_s),
                     reads=[S, m1], writes=[iu32])
                a.op("vector", lambda e, _o=iu32, _m=m2, _s=sx2, _L=L: e.max_index(out=_o[:, 8:16], in_max=_m, in_values=_s[:, 0:_L]),
                     reads=[sx2, m2], writes=[iu32])
                a.copy(idxf, iu32)
                a.ts(tq, tcol, float(qt * 128), ALU.add)
                a.ts(d0, idxf, -1.0, ALU.mult, tq[:, 0:1], ALU.add)
                a.ts(d0, d0, 0.0, ALU.max, 1.0 / 2048.0, ALU.mult)
                a.memset(d0[:, 15:16], 0.0)
                a.copy(d0b, d0)
                a.op("gpsimd", lambda e, _o=drow, _i=d0b:
                     e.dma_start(out=_o[0:1, 0:2048].rearrange("o (t k) -> o t k", k=16), in_=_i,
                                 allow_slow_non_contiguous=True),
                     reads=[d0b], writes=[drow], queue="gpq")
                a.ts(idxf[:, 15:16], tq, float(2048), ALU.add)
                a.ts(idxf, idxf, float(b * 4096), ALU.add)
                for g in range(16):
                    a.mm(PSD[:, 0:16], selg[:, g, :], idxf, start=True, stop=True)
                    a.tt(gtmp, PSD[:, 0:16], eqsel, ALU.mult)
                    a.reduce(gidxf, gtmp, ALU.add)
                    a.copy(gidxu, gidxf)
                    a.op("gpsimd", lambda e, _o=gath, _i=pnscr, _x=gidxu:
                         e.indirect_dma_start(out=_o, out_offset=None, in_=_i,
                                              in_offset=_bass.IndirectOffsetOnAxis(ap=_x, axis=0)),
                         reads=[pnscr, gidxu], writes=[gath], queue="gpq")
                    for c in range(8):
                        a.trans(PSB[:, c * 128:(c + 1) * 128], gath[:, c * 128:(c + 1) * 128], identb)
                    a.copy(afm.rearrange("p a b -> p (a b)"), PSB[:, 0:1024])
                    for c in range(8):
                        a.mm(PSD[:, 128:256], afm[:, c, :], afm[:, c, :],
                             start=(c == 0), stop=(c == 7))
                    a.copy(gsb, PSD[:, 128:256])
                    for i in range(8):
                        a.mm(PSD[0:16, 256:272].rearrange("p a -> p a") if False else PSD[0:16, 256 + i * 16:256 + (i + 1) * 16],
                             selx[:, i, :], gsb[:, i * 16:(i + 1) * 16], start=True, stop=True)
                    a.ts(relbuf[0:16, g * 128:(g + 1) * 128], PSD[0:16, 256:384],
                         1.0, ALU.min, -1.0, ALU.max)
                if qt == 0:
                    a.tt(relbuf[0:16, 0:2048], relbuf[0:16, 0:2048], m0, ALU.mult)
                for sl in range(4):
                    ssl = slice(sl * 512, (sl + 1) * 512)
                    a.mm(PSA[0:64, ssl], w1g, relbuf[0:16, ssl], start=True, stop=False)
                    a.mm(PSA[0:64, ssl], w1d, drow[0:1, ssl], start=False, stop=True)
                a.act(h1sb, PSA[0:64, 0:2048], AF.Gelu, bias=b1[:, 0:1], scale=1.0)
                for sl in range(4):
                    ssl = slice(sl * 512, (sl + 1) * 512)
                    a.mm(PSA[64:128, ssl], w2, h1sb[:, ssl], start=True, stop=True)
                a.act(h2sb, PSA[64:128, 0:2048], AF.Gelu, bias=b2[:, 0:1], scale=1.0)
                for (wv_, bi_, dst, func) in ((wc0, bct[0:1, 0:1], gxrow, AF.Tanh),
                                              (wc1, bct[0:1, 1:2], gyrow, AF.Tanh),
                                              (wm, bmt[0:1, 0:1], mixrow, AF.Identity)):
                    for sl in range(4):
                        ssl = slice(sl * 512, (sl + 1) * 512)
                        a.mm(PSA[0:1, ssl], wv_, h2sb[:, ssl], start=True, stop=True)
                    a.act(dst[0:1, 0:2048], PSA[0:1, 0:2048], func, bias=bi_, scale=1.0)
                for (src, dstT, idn) in ((mixrow, mixT, identf), (gxrow, gxT, identb), (gyrow, gyT, identb)):
                    for k in range(16):
                        if idn is identf:
                            a.trans(PSD[:, 384 + k:384 + k + 1],
                                    src[0:1, 0:2048].rearrange("o (t k) -> o k t", k=16)[:, k, :], idn[0:1, 0:1])
                        else:
                            a.trans(PSB[:, 512 + 2 * k:512 + 2 * k + 1],
                                    src[0:1, 0:2048].rearrange("o (t k) -> o k t", k=16)[:, k, :], idn[0:1, 0:1])
                    if idn is identf:
                        a.copy(dstT, PSD[:, 384:400])
                    else:
                        a.copy(dstT, PSB[:, 512:544].rearrange("p (k two) -> p k two", two=2)[:, :, 0])
                kslice = slice(16, 32) if qt == 0 else slice(0, 16)
                a.tt(mixT, mixT, keepm[:, kslice], ALU.add)
                a.reduce(rowmax, mixT, ALU.max)
                a.ts(nbias, rowmax, -1.0, ALU.mult)
                a.act(wsoft, mixT, AF.Exp, bias=nbias[:, 0:1], scale=1.0, accum_out=rowsum)
                a.op("vector", lambda e, _o=recl, _i=rowsum: e.reciprocal(out=_o, in_=_i),
                     reads=[rowsum], writes=[recl])
                a.ts(wsoft, wsoft, recl[:, 0:1], ALU.mult)
                for (gT, f_, i0_, f0_, f1_, w_) in ((gxT, fx, x0i, x0f, x1f, wx),
                                                    (gyT, fy, x0i, y0f, y1f, wy)):
                    a.ts(f_, gT, 1.0, ALU.add, 7.5, ALU.mult)
                    a.ts(i0_, f_, 0.5, ALU.subtract)
                    a.copy(f0_, i0_)
                    a.tt(w_, f_, f0_, ALU.subtract)
                    a.ts(f1_, f0_, 1.0, ALU.add, 15.0, ALU.min)
                a.op("vector", lambda e, _o=axw, _i=iota16r, _x=x0f:
                     e.tensor_tensor(out=_o.rearrange("p (k w) -> p k w", w=16),
                                     in0=_i.rearrange("p (k w) -> p k w", w=16),
                                     in1=_x.unsqueeze(2).broadcast_to([128, 16, 16]), op=ALU.is_equal),
                     reads=[iota16r, x0f], writes=[axw])
                a.op("vector", lambda e, _o=ctmp, _i=iota16r, _x=x1f:
                     e.tensor_tensor(out=_o.rearrange("p (k w) -> p k w", w=16),
                                     in0=_i.rearrange("p (k w) -> p k w", w=16),
                                     in1=_x.unsqueeze(2).broadcast_to([128, 16, 16]), op=ALU.is_equal),
                     reads=[iota16r, x1f], writes=[ctmp])
                a.tt(ctmp, ctmp, axw, ALU.subtract)
                a.op("vector", lambda e, _o=ctmp, _i=ctmp, _x=wx:
                     e.tensor_tensor(out=_o.rearrange("p (k w) -> p k w", w=16),
                                     in0=_i.rearrange("p (k w) -> p k w", w=16),
                                     in1=_x.unsqueeze(2).broadcast_to([128, 16, 16]), op=ALU.mult),
                     reads=[ctmp, wx], writes=[ctmp])
                a.tt(axw, axw, ctmp, ALU.add)
                a.op("vector", lambda e, _o=byw, _i=iota16r, _x=y0f:
                     e.tensor_tensor(out=_o.rearrange("p (k w) -> p k w", w=16),
                                     in0=_i.rearrange("p (k w) -> p k w", w=16),
                                     in1=_x.unsqueeze(2).broadcast_to([128, 16, 16]), op=ALU.is_equal),
                     reads=[iota16r, y0f], writes=[byw])
                a.op("vector", lambda e, _o=ctmp, _i=iota16r, _x=y1f:
                     e.tensor_tensor(out=_o.rearrange("p (k w) -> p k w", w=16),
                                     in0=_i.rearrange("p (k w) -> p k w", w=16),
                                     in1=_x.unsqueeze(2).broadcast_to([128, 16, 16]), op=ALU.is_equal),
                     reads=[iota16r, y1f], writes=[ctmp])
                a.tt(ctmp, ctmp, byw, ALU.subtract)
                a.op("vector", lambda e, _o=ctmp, _i=ctmp, _x=wy:
                     e.tensor_tensor(out=_o.rearrange("p (k w) -> p k w", w=16),
                                     in0=_i.rearrange("p (k w) -> p k w", w=16),
                                     in1=_x.unsqueeze(2).broadcast_to([128, 16, 16]), op=ALU.mult),
                     reads=[ctmp, wy], writes=[ctmp])
                a.tt(byw, byw, ctmp, ALU.add)
                a.op("vector", lambda e, _o=byw, _i=byw, _x=wsoft:
                     e.tensor_tensor(out=_o.rearrange("p (k w) -> p k w", w=16),
                                     in0=_i.rearrange("p (k w) -> p k w", w=16),
                                     in1=_x.unsqueeze(2).broadcast_to([128, 16, 16]), op=ALU.mult),
                     reads=[byw, wsoft], writes=[byw])
                first = True
                for k in range(16):
                    eng = "vector" if k % 2 == 0 else "gpsimd"
                    a.op(eng, lambda e, _o=ctmp, _b=byw, _a=axw, _k=k:
                         e.tensor_tensor(out=_o, in0=_b[:, _k * 16:(_k + 1) * 16].unsqueeze(2).broadcast_to([128, 16, 16]),
                                         in1=_a[:, _k * 16:(_k + 1) * 16].unsqueeze(1).broadcast_to([128, 16, 16]),
                                         op=ALU.mult),
                         reads=[byw, axw], writes=[ctmp])
                    if first:
                        a.copy(coeff, ctmp, eng="gpsimd" if eng == "vector" else "vector")
                        first = False
                    else:
                        a.tt(coeff, coeff, ctmp, ALU.add, eng="gpsimd" if eng == "vector" else "vector")
                a.copy(coefb, coeff)
                for c2 in range(2):
                    a.trans(PSB[:, 0:128], coefb[:, c2 * 128:(c2 + 1) * 128], identb)
                    a.copy(cf[:, c2, :], PSB[:, 0:128])
                pv = PSA[:, 1024:2048]
                for hf in range(2):
                    for c2 in range(2):
                        a.mm(pv[:, hf * 512:(hf + 1) * 512], cf[:, c2, :],
                             palt[:, c2, hf * 512:(hf + 1) * 512], start=(c2 == 0), stop=(c2 == 1))
                a.copy(vtmb, pv)
                emit_tm_to_fm(a, vtmb, [vf[:, c, :] for c in range(8)], identb, PSB[:, 0:128])
                pd = PSA[:, 0:1024]
                for c in range(8):
                    for hf in range(2):
                        a.mm(pd[:, hf * 512:(hf + 1) * 512], vf[:, c, :],
                             wot[:, c, hf * 512:(hf + 1) * 512], start=(c == 0), stop=(c == 7))
                r0 = base + qt * 128
                a.dma("scq", xtile, x[r0:r0 + 128, :])
                a.tt(xtile, xtile, pd, ALU.add)
                a.dma("gpq", xo[r0:r0 + 128, :], xtile)


def host_pal_consts():
    c = {}
    half = 512
    inv_freq = 1.0 / 10000.0 ** (np.arange(half, dtype=np.float32) / half)
    ang = np.arange(T, dtype=np.float32)[:, None] * inv_freq[None, :]
    c["cosr"] = np.cos(ang).astype(np.float32)
    c["sinr"] = np.sin(ang).astype(np.float32)
    m0 = np.zeros((16, T), np.float32)
    for t in range(128):
        for k in range(16):
            for j in range(16):
                ok = (k <= t and k < 15) and (j <= t or j == 15)
                m0[j, t * 16 + k] = 1.0 if ok else 0.0
    c["m0"] = m0.astype(ml_dtypes.bfloat16)
    km = np.zeros((128, 32), np.float32)
    km[:, 15] = NEG
    for p in range(128):
        for k in range(16):
            km[p, 16 + k] = NEG if (k > p or k == 15) else 0.0
    c["keepm"] = km
    c["iota256"] = np.tile(np.arange(256, dtype=np.float32)[None, :], (128, 1))
    c["iota16r"] = np.tile(np.tile(np.arange(16, dtype=np.float32), 16)[None, :], (128, 1))
    c["tcol"] = np.arange(128, dtype=np.float32)[:, None].copy()
    eq = np.zeros((128, 16), np.float32)
    for p in range(128):
        eq[p, p % 16] = 1.0
    c["eqsel"] = eq
    selg = np.zeros((16, 128, 128), np.float32)
    for g in range(16):
        for p in range(128):
            selg[g, 8 * g + p // 16, p] = 1.0
    c["selg"] = np.ascontiguousarray(np.transpose(selg, (1, 0, 2)).reshape(128, 16 * 128))
    selx = np.zeros((8, 128, 16), np.float32)
    for i in range(8):
        for j in range(16):
            selx[i, 16 * i + j, j] = 1.0
    c["selx"] = np.ascontiguousarray(np.transpose(selx, (1, 0, 2)).reshape(128, 8 * 16)).astype(ml_dtypes.bfloat16)
    return c


def run_attn_pal(x_full, Wi, Wp, palette, W1, b1, W2, b2, Wc, bc, Wm, bm, Wo):
    consts = host_constants()
    pc = host_pal_consts()
    nc = build_attn_pal()
    in_common = {
        "wi": np.ascontiguousarray(Wi), "wp": np.ascontiguousarray(Wp),
        "wo": np.ascontiguousarray(Wo[0]),
        "palT": np.ascontiguousarray(palette.reshape(D, 256).T),
        "w1g": np.ascontiguousarray(W1[0:16, :]), "w1d": np.ascontiguousarray(W1[16:17, :]),
        "b1": np.ascontiguousarray(b1[:, None]), "w2": np.ascontiguousarray(W2),
        "b2": np.ascontiguousarray(b2[:, None]),
        "wcc0": np.ascontiguousarray(Wc[:, 0:1]), "wcc1": np.ascontiguousarray(Wc[:, 1:2]),
        "bc": np.ascontiguousarray(bc[None, :]), "wm": np.ascontiguousarray(Wm),
        "bm": np.ascontiguousarray(bm[None, :]),
        "ident": consts["ident_bf"], "identf": consts["ident"], "diagmask": consts["diagmask"],
    }
    in_common.update(pc)
    in_maps = []
    for c in range(NCORES):
        m = dict(in_common)
        m["x"] = x_full.reshape(16 * T, D)[c * B_PER_CORE * T:(c + 1) * B_PER_CORE * T]
        in_maps.append(m)
    res = run_bass_kernel_spmd(nc, in_maps, core_ids=list(range(NCORES)))
    out = np.concatenate([res.results[c]["xo"] for c in range(NCORES)], axis=0)
    return out.reshape(16, T, D)



def build_l0_ffn_dbg():
    hidden = 3072
    HC = hidden // 128
    NTOK = B_PER_CORE * T
    NT = NTOK // 128
    NT512 = NTOK // 512
    nc = bass.Bass("TRN2", target_bir_lowering=False)
    x = nc.dram_tensor("x", [NTOK, D], F32, kind="ExternalInput").ap()
    wgT = nc.dram_tensor("wgT", [D, hidden], I32, kind="ExternalInput").ap()
    wuT = nc.dram_tensor("wuT", [D, hidden], I32, kind="ExternalInput").ap()
    wdT = nc.dram_tensor("wdT", [hidden, D], I32, kind="ExternalInput").ap()
    ident_in = nc.dram_tensor("ident", [128, 128], BF16, kind="ExternalInput").ap()
    xo = nc.dram_tensor("xo", [NTOK, D], F32, kind="ExternalOutput").ap()
    oxln = nc.dram_tensor("oxln", [NTOK, D], F32, kind="ExternalOutput").ap()
    oh = nc.dram_tensor("oh", [24, 128, NTOK], F32, kind="ExternalOutput").ap()
    opd = nc.dram_tensor("opd", [NTOK, D], F32, kind="ExternalOutput").ap()

    from contextlib import ExitStack
    with ExitStack() as stack:
        ctx = Ctx(nc, stack)
        sb, ps = ctx.sb, ctx.ps
        junk = sb("junk", [128, 256], F32)
        identb = sb("identb", [128, 128], BF16)
        xtile = [sb(f"xt{i}", [128, D], F32) for i in range(2)]
        xln_tm = sb("xln_tm", [128, D], BF16)
        xlnf = sb("xlnf", [128, 8, T // 2], BF16)
        scratch = {"xsq": sb("xsq", [128, D], BF16), "s1": sb("s1", [128, 1], F32),
                   "s2": sb("s2", [128, 1], F32), "mv": sb("mv", [128, 2], F32),
                   "rstd": sb("rstd", [128, 1], F32)}
        wgt = sb("wgt", [128, 8, hidden], BF16)
        wut = sb("wut", [128, 8, hidden], BF16)
        wdt = sb("wdtr", [128, HC, D], BF16)
        hbuf = sb("hbuf", [128, HC, 512], BF16)
        sg = [sb(f"sg{i}", [128, 512], BF16) for i in range(2)]
        pg = [ps(f"pg{i}", [128, 512], F32) for i in range(2)]
        pu = [ps(f"pu{i}", [128, 512], F32) for i in range(2)]
        pd = ps("pd", [128, D], F32)
        pxp = ps("pxp", [128, 128], BF16)
        a = A(nc)

        a.dma("syq", identb, ident_in)
        for c in range(8):
            a.dma("gpq", wgt[:, c, :], wgT[c * 128:(c + 1) * 128, :])
            a.dma("gpq", wut[:, c, :], wuT[c * 128:(c + 1) * 128, :])
        for hc in range(HC):
            a.dma("gpq", wdt[:, hc, :], wdT[hc * 128:(hc + 1) * 128, :])
        a.warmup(junk)

        NTSEQ = T // 256
        NT512SEQ = T // 1024
        for b in range(2 * B_PER_CORE):
            base = b * (T // 2)
            for j in range(NTSEQ):
                xt = xtile[j % 2]
                r0 = base + j * 128
                a.dma("syq", xt, x[r0:r0 + 128, :])
                emit_ln_tile(a, None, xt, xln_tm, scratch, extra_scale=1.0 / 127.0)
                a.dma("gpq", oxln[r0:r0 + 128, :], xln_tm)
                emit_tm_to_fm(a, xln_tm, [xlnf[:, c, j * 128:(j + 1) * 128] for c in range(8)],
                              identb, pxp)
            for tt in range(NT512SEQ):
                tsl = slice(tt * 512, (tt + 1) * 512)
                for hc in range(HC):
                    g = pg[hc % 2]
                    u = pu[hc % 2]
                    sgt = sg[hc % 2]
                    for c in range(8):
                        a.mm(g, wgt[:, c, hc * 128:(hc + 1) * 128], xlnf[:, c, tsl],
                             start=(c == 0), stop=(c == 7))
                    for c in range(8):
                        a.mm(u, wut[:, c, hc * 128:(hc + 1) * 128], xlnf[:, c, tsl],
                             start=(c == 0), stop=(c == 7))
                    a.act(sgt, g, AF.Silu)
                    a.stt(hbuf[:, hc, :], sgt, 1.0 / 127.0, u, ALU.mult, ALU.mult)
                    a.dma("gpq", oh[hc, :, base + tt * 512:base + tt * 512 + 512], hbuf[:, hc, :], tag="dbg")
                for st4 in range(4):
                    j = tt * 4 + st4
                    for hc in range(HC):
                        for half in range(2):
                            a.mm(pd[:, half * 512:(half + 1) * 512],
                                 hbuf[:, hc, st4 * 128:(st4 + 1) * 128],
                                 wdt[:, hc, half * 512:(half + 1) * 512],
                                 start=(hc == 0), stop=(hc == HC - 1))
                    xt = xtile[j % 2]
                    r0 = base + j * 128
                    a.dma("syq", xt, x[r0:r0 + 128, :])
                    a.copy(mvd := xt, xt) if False else None
                    a.dma("vq", opd[r0:r0 + 128, :], xt, tag="preres") if False else None
                    a.tt(xt, xt, pd, ALU.add)
                    a.dma("gpq", xo[r0:r0 + 128, :], xt)

        a.barrier_outputs([xo, oxln, oh])
        a.emit(stack)
    return nc




def build_fused():
    NTOK = B_PER_CORE * T
    nc = bass.Bass("TRN2", target_bir_lowering=False)
    dd = {}
    def D_(pfx, name, shape, dt_):
        dd[(pfx, name)] = nc.dram_tensor(pfx + name, shape, dt_, kind="ExternalInput").ap()
    x_in = nc.dram_tensor("x", [NTOK, D], F32, kind="ExternalInput").ap()
    xo = nc.dram_tensor("xo", [NTOK, D], F32, kind="ExternalOutput").ap()
    xs = [nc.dram_tensor(f"xs{i}", [NTOK, D], F32, kind="Internal").ap() for i in range(2)]
    pnscr = nc.dram_tensor("pnscr", [B_PER_CORE * 4096, D], BF16, kind="Internal").ap()
    D_("c_", "ident", [128, 128], BF16)
    D_("c_", "identf", [128, 128], F32)
    D_("c_", "diagmask", [128, 128], F32)
    for n, sh, dt_ in [("wqT", [D, D], F32), ("wqTs", [D, D], F32), ("wkT", [D, 192], F32),
                       ("wvT", [D, 192], F32), ("dks", [192, D], F32), ("dkss", [192, D], F32),
                       ("dkm", [192, D], BF16), ("dkms", [192, D], BF16), ("dv", [192, D], F32),
                       ("woT", [D, D], F32), ("c1", [64, T], BF16), ("c2", [64, T], BF16)]:
        D_("aa_", n, sh, dt_)
    for n, sh in [("wgT", [D, 3072]), ("wuT", [D, 3072]), ("wdT", [3072, D])]:
        D_("f0_", n, sh, I32)
    for n in ("wqT", "wkT", "wvT", "woT"):
        D_("ab_", n, [D, D], F32)
    D_("ab_", "alibik", [16, T], BF16)
    for n, sh, dt_ in [("winT", [D, 4096], F32), ("wxT", [2048, 96], F32), ("wdtT", [64, 2048], F32),
                       ("woutT", [2048, D], F32), ("convr", [128, 64], F32), ("convb", [128, 16], F32),
                       ("bdt", [128, 16], F32), ("dprm", [128, 16], F32), ("alr", [128, 256], F32),
                       ("sel", [128, 2048], BF16), ("selt", [128, 2048], BF16), ("selb", [128, 256], BF16),
                       ("lnw", [128, D], F32), ("lnb", [128, D], F32)]:
        D_("ma_", n, sh, dt_)
    for n, sh in [("wg", [2816, D]), ("wu", [2816, D]), ("wd", [D, 2816])]:
        D_("f1_", n, sh, F32)
    for n, sh, dt_ in [("wi", [D, D], F32), ("wp", [D, D], F32), ("wo", [D, D], F32),
                       ("palT", [256, D], F32), ("w1g", [16, 64], F32), ("w1d", [1, 64], F32),
                       ("b1", [64, 1], F32), ("w2", [64, 64], F32), ("b2", [64, 1], F32),
                       ("wcc0", [64, 1], F32), ("wcc1", [64, 1], F32), ("bc", [1, 2], F32),
                       ("wm", [64, 1], F32), ("bm", [1, 1], F32), ("cosr", [T, 512], F32),
                       ("sinr", [T, 512], F32), ("m0", [16, T], BF16), ("keepm", [128, 32], F32),
                       ("iota256", [128, 256], F32), ("iota16r", [128, 256], F32),
                       ("tcol", [128, 1], F32), ("eqsel", [128, 16], F32),
                       ("selg", [128, 2048], F32), ("selx", [128, 128], BF16)]:
        D_("pp_", n, sh, dt_)

    def sub(pfx):
        return {n: ap for (p, n), ap in dd.items() if p == pfx}

    from contextlib import ExitStack
    with ExitStack() as stack:
        ctx0 = Ctx(nc, stack)
        junk = ctx0.sb("junk", [128, 256], F32)
        identb = ctx0.sb("identb", [128, 128], BF16)
        identf = ctx0.sb("identfp", [128, 128], F32)
        diagm = ctx0.sb("diagmp", [128, 128], F32)
        com = {"junk": junk, "identb": identb, "identf": identf, "diagm": diagm}
        PS = {"PSA": ctx0.ps("PSA", [128, 2048], F32),
              "PSB": ctx0.ps("PSB", [128, 2048], BF16),
              "PSD": ctx0.ps("PSD", [128, 512], F32),
              "PSE": ctx0.ps("PSE", [128, 512], F32)}
        a = A(nc)
        a.dma("syq", identb, dd[("c_", "ident")])
        a.dma("syq", identf, dd[("c_", "identf")])
        a.dma("syq", diagm, dd[("c_", "diagmask")])
        a.memset(junk, 1.0)
        a.warmup(junk)
        for f in (AF.Exp, AF.Abs, AF.Square, AF.Gelu, AF.Tanh, AF.Sign, AF.Sqrt,
                  AF.Silu, AF.Relu, AF.Ln):
            a.act(junk, junk, f)

        chain = [
            (lambda a_, c_, xi, xo_: emit_attn(a_, c_, sub("aa_"), PS, com, xi, xo_, "a")),
            (lambda a_, c_, xi, xo_: emit_l0_ffn(a_, c_, sub("f0_"), PS, com, xi, xo_)),
            (lambda a_, c_, xi, xo_: emit_attn(a_, c_, sub("ab_"), PS, com, xi, xo_, "b")),
            (lambda a_, c_, xi, xo_: emit_l0_ffn(a_, c_, sub("f0_"), PS, com, xi, xo_)),
            (lambda a_, c_, xi, xo_: emit_mamba(a_, c_, sub("ma_"), PS, com, xi, xo_)),
            (lambda a_, c_, xi, xo_: emit_l1_ffn(a_, c_, sub("f1_"), PS, com, xi, xo_)),
            (lambda a_, c_, xi, xo_: emit_attn_pal(a_, c_, sub("pp_"), PS, com, xi, xo_, pnscr)),
            (lambda a_, c_, xi, xo_: emit_l1_ffn(a_, c_, sub("f1_"), PS, com, xi, xo_)),
        ]
        for k, em in enumerate(chain):
            xi = x_in if k == 0 else xs[k % 2]
            xo_ = xo if k == len(chain) - 1 else xs[(k + 1) % 2]
            with ExitStack() as substack:
                ctx = Ctx(nc, substack, prefix=f"s{k}_")
                em(a, ctx, xi, xo_)
            a.barrier()
        a.barrier_outputs([xo])
        a.emit(stack)
        nops = len(a.ops)
    return nc, nops


def _host_inputs(inp):
    consts = host_constants()
    m = {"c_ident": consts["ident_bf"], "c_identf": consts["ident"], "c_diagmask": consts["diagmask"]}
    Wq = np.asarray(inp["L0_attn_a_Wq"], np.float32)
    perm = np.arange(D) ^ 1
    dk, dks_sw, mk, mks, dv = host_attn_a_decoders(np.asarray(inp["L0_attn_a_key_decoder"], np.float32),
                                                   np.asarray(inp["L0_attn_a_value_decoder"], np.float32))
    C1, C2 = rope_tables_interleaved()
    m.update({"aa_wqT": np.ascontiguousarray(Wq), "aa_wqTs": np.ascontiguousarray(Wq[:, perm]),
              "aa_wkT": np.ascontiguousarray(np.asarray(inp["L0_attn_a_Wk"], np.float32)),
              "aa_wvT": np.ascontiguousarray(np.asarray(inp["L0_attn_a_Wv"], np.float32)),
              "aa_dks": dk, "aa_dkss": dks_sw, "aa_dkm": mk, "aa_dkms": mks, "aa_dv": dv,
              "aa_woT": np.ascontiguousarray(np.asarray(inp["L0_attn_a_Wproj"], np.float32)),
              "aa_c1": C1.astype(ml_dtypes.bfloat16), "aa_c2": C2.astype(ml_dtypes.bfloat16)})
    m.update({"f0_wgT": np.ascontiguousarray(np.asarray(inp["L0_ffn_w_gate"], np.int32).T),
              "f0_wuT": np.ascontiguousarray(np.asarray(inp["L0_ffn_w_up"], np.int32).T),
              "f0_wdT": np.ascontiguousarray(np.asarray(inp["L0_ffn_w_down"], np.int32).T)})
    m.update({"ab_wqT": np.ascontiguousarray(np.asarray(inp["L0_attn_b_Wq"], np.float32).T),
              "ab_wkT": np.ascontiguousarray(np.asarray(inp["L0_attn_b_Wk"], np.float32).T),
              "ab_wvT": np.ascontiguousarray(np.asarray(inp["L0_attn_b_Wv"], np.float32).T),
              "ab_woT": np.ascontiguousarray(np.asarray(inp["L0_attn_b_Wo"], np.float32).T),
              "ab_alibik": host_alibi()})
    sel_h, selt_h, selbc = host_mamba_consts()
    A_log = np.asarray(inp["L1_attn_a_A_log"], np.float32)
    alr = np.zeros((128, 256), np.float32)
    p_ = np.arange(128)
    for ct in range(256):
        alr[p_, ct] = A_log[8 * ct + (p_ % 8), p_ // 8]
    conv_w = np.asarray(inp["L1_attn_a_conv_w"], np.float32)
    m.update({"ma_winT": np.ascontiguousarray(np.asarray(inp["L1_attn_a_W_in"], np.float32).T),
              "ma_wxT": np.ascontiguousarray(np.asarray(inp["L1_attn_a_W_x"], np.float32).T),
              "ma_wdtT": np.ascontiguousarray(np.asarray(inp["L1_attn_a_W_dt"], np.float32).T),
              "ma_woutT": np.ascontiguousarray(np.asarray(inp["L1_attn_a_W_out"], np.float32).T),
              "ma_convr": np.ascontiguousarray(conv_w[:, 0, :].reshape(16, 128, 4).transpose(1, 0, 2).reshape(128, 64)),
              "ma_convb": np.ascontiguousarray(np.asarray(inp["L1_attn_a_conv_b"], np.float32).reshape(16, 128).T),
              "ma_bdt": np.ascontiguousarray(np.asarray(inp["L1_attn_a_b_dt"], np.float32).reshape(16, 128).T),
              "ma_dprm": np.ascontiguousarray(np.asarray(inp["L1_attn_a_D_param"], np.float32).reshape(16, 128).T),
              "ma_alr": alr, "ma_sel": sel_h, "ma_selt": selt_h, "ma_selb": selbc,
              "ma_lnw": np.ascontiguousarray(np.tile(np.asarray(inp["L1_attn_a_ln_w"], np.float32)[None, :], (128, 1))),
              "ma_lnb": np.ascontiguousarray(np.tile(np.asarray(inp["L1_attn_a_ln_b"], np.float32)[None, :], (128, 1)))})
    m.update({"f1_wg": np.ascontiguousarray(np.asarray(inp["L1_ffn_w_gate"], np.float32)),
              "f1_wu": np.ascontiguousarray(np.asarray(inp["L1_ffn_w_up"], np.float32)),
              "f1_wd": np.ascontiguousarray(np.asarray(inp["L1_ffn_w_down"], np.float32))})
    pc = host_pal_consts()
    m.update({"pp_" + k: v for k, v in pc.items()})
    m.update({"pp_wi": np.ascontiguousarray(np.asarray(inp["L1_attn_b_Wi"], np.float32)),
              "pp_wp": np.ascontiguousarray(np.asarray(inp["L1_attn_b_Wp"], np.float32)),
              "pp_wo": np.ascontiguousarray(np.asarray(inp["L1_attn_b_Wo"], np.float32)[0]),
              "pp_palT": np.ascontiguousarray(np.asarray(inp["L1_attn_b_palette"], np.float32).reshape(D, 256).T),
              "pp_w1g": np.ascontiguousarray(np.asarray(inp["L1_attn_b_W1"], np.float32)[0:16, :]),
              "pp_w1d": np.ascontiguousarray(np.asarray(inp["L1_attn_b_W1"], np.float32)[16:17, :]),
              "pp_b1": np.ascontiguousarray(np.asarray(inp["L1_attn_b_b1"], np.float32)[:, None]),
              "pp_w2": np.ascontiguousarray(np.asarray(inp["L1_attn_b_W2"], np.float32)),
              "pp_b2": np.ascontiguousarray(np.asarray(inp["L1_attn_b_b2"], np.float32)[:, None]),
              "pp_wcc0": np.ascontiguousarray(np.asarray(inp["L1_attn_b_Wc"], np.float32)[:, 0:1]),
              "pp_wcc1": np.ascontiguousarray(np.asarray(inp["L1_attn_b_Wc"], np.float32)[:, 1:2]),
              "pp_bc": np.ascontiguousarray(np.asarray(inp["L1_attn_b_bc"], np.float32)[None, :]),
              "pp_wm": np.ascontiguousarray(np.asarray(inp["L1_attn_b_Wm"], np.float32)),
              "pp_bm": np.ascontiguousarray(np.asarray(inp["L1_attn_b_bm"], np.float32)[None, :])})
    return m


def kernel(**inp):
    x = np.ascontiguousarray(np.asarray(inp["L0_attn_a_x"], dtype=np.float32)).reshape(16 * T, D)
    common = _host_inputs(inp)
    nc, nops = build_fused()
    in_maps = []
    for c in range(NCORES):
        m = dict(common)
        m["x"] = x[c * B_PER_CORE * T:(c + 1) * B_PER_CORE * T]
        in_maps.append(m)
    res = run_bass_kernel_spmd(nc, in_maps, core_ids=list(range(NCORES)))
    out = np.concatenate([res.results[c]["xo"] for c in range(NCORES)], axis=0)
    return out.reshape(16, T, D).astype(np.float32)


# version 05
# speedup vs baseline: 1.1810x; 1.1747x over previous
import math
import numpy as np
import ml_dtypes
import concourse.bass as bass
import concourse.mybir as mybir
from concourse.bass_utils import run_bass_kernel_spmd

F32 = mybir.dt.float32
BF16 = mybir.dt.bfloat16
I32 = mybir.dt.int32
U32 = mybir.dt.uint32
U8 = mybir.dt.uint8
AF = mybir.ActivationFunctionType
ALU = mybir.AluOpType
AX = mybir.AxisListType

NEG = -3.0e38

ENGINES = ("sync", "scalar", "vector", "tensor", "gpsimd")
DMAQ = {"syq": "sync", "scq": "scalar", "gpq": "gpsimd", "vq": "vector"}


def _regions(ap_or_list):
    out = []
    if ap_or_list is None:
        return out
    items = ap_or_list if isinstance(ap_or_list, (list, tuple)) else [ap_or_list]
    for ap in items:
        if ap is None:
            continue
        t = ap.tensor
        name = t.name
        esz = mybir.dt.size(ap.dtype)
        lo = ap.offset
        span = 1
        try:
            for st, cnt in ap.ap:
                span += abs(int(st)) * (int(cnt) - 1)
        except Exception:
            span = t.size if hasattr(t, "size") else 1 << 40
        out.append((name, lo * esz, (lo + span) * esz))
    return out


class Op:
    __slots__ = ("idx", "eng", "queue", "fn", "reads", "writes", "seq", "inc_needed",
                 "waits", "tag", "deps", "pre_drain", "dma_sid", "dma_use")

    def __init__(self, idx, eng, queue, fn, reads, writes, tag=""):
        self.idx = idx
        self.eng = eng
        self.queue = queue
        self.fn = fn
        self.reads = reads
        self.writes = writes
        self.seq = None
        self.inc_needed = False
        self.pre_drain = False
        self.waits = []
        self.tag = tag


class A:

    SERIALIZE = False

    def __init__(self, nc):
        self.nc = nc
        self.ops = []
        self.access = {}
        self.nwaits = 0

    def op(self, eng, fn, reads=None, writes=None, queue=None, tag=""):
        o = Op(len(self.ops), eng, queue, fn, _regions(reads), _regions(writes), tag)
        self.ops.append(o)
        deps = set()
        for (name, lo, hi) in o.reads:
            for (l2, h2, prev, w) in self.access.get(name, ()):
                if w and l2 < hi and lo < h2:
                    deps.add(prev)
        for (name, lo, hi) in o.writes:
            for (l2, h2, prev, w) in self.access.get(name, ()):
                if l2 < hi and lo < h2:
                    deps.add(prev)
        if A.SERIALIZE and self.ops[:-1]:
            deps.add(self.ops[-2])
        o.deps = deps
        for (name, lo, hi) in o.reads:
            self.access.setdefault(name, []).append((lo, hi, o, False))
        for (name, lo, hi) in o.writes:
            lst = self.access.setdefault(name, [])
            keep = [e for e in lst if not (lo <= e[0] and e[1] <= hi)]
            keep.append((lo, hi, o, True))
            self.access[name] = keep
        return o

    def _stream_name(self, o):
        return o.queue if o.queue else o.eng

    NDMASEM = 48

    def schedule(self):
        counters = {}
        dma_n = 0
        self.sem_uses = {}
        for o in self.ops:
            sname = self._stream_name(o)
            counters[sname] = counters.get(sname, 0) + 1
            o.seq = counters[sname]
            if o.queue is not None:
                sid = dma_n % self.NDMASEM
                dma_n += 1
                self.sem_uses[sid] = self.sem_uses.get(sid, 0) + 1
                o.dma_sid = sid
                o.dma_use = self.sem_uses[sid]
        hi_water = {}
        for o in self.ops:
            cons = self._stream_name(o)
            need = {}
            for d in o.deps:
                if d.queue is not None:
                    key = ("D", d.dma_sid)
                    need[key] = max(need.get(key, 0), d.dma_use)
                else:
                    pstream = self._stream_name(d)
                    if pstream == cons and o.queue is None:
                        if cons != "tensor" and (o.seq - d.seq) <= 2:
                            o.pre_drain = True
                        continue
                    if o.queue is not None and d.eng == o.eng:
                        continue
                    need[("E", pstream)] = max(need.get(("E", pstream), 0), d.seq)
            for k, seq in sorted(need.items()):
                key = (cons, k)
                if hi_water.get(key, 0) >= seq:
                    continue
                hi_water[key] = seq
                o.waits.append((k, seq))
        targets = {}
        for o in self.ops:
            for (k, seq) in o.waits:
                if k[0] == "E":
                    targets.setdefault(k[1], set()).add(seq)
        by_stream = {}
        for o in self.ops:
            if o.queue is None:
                by_stream.setdefault(self._stream_name(o), {})[o.seq] = o
        for pstream, seqs in targets.items():
            for sq in seqs:
                by_stream[pstream][sq].inc_needed = True
        for o in self.ops:
            if o.queue is not None:
                o.inc_needed = True

    def emit(self, stack):
        self.schedule()
        sems = {}
        streams = set(self._stream_name(o) for o in self.ops if o.queue is None)
        for s in streams:
            sems[s] = stack.enter_context(self.nc.semaphore("sem_" + s))
        dsems = [stack.enter_context(self.nc.semaphore(f"dma{i}"))
                 for i in range(min(self.NDMASEM, len(self.sem_uses) if self.sem_uses else 0) or 1)]
        inc_prefix = {}
        for o in self.ops:
            if o.queue is None:
                sname = self._stream_name(o)
                lst = inc_prefix.setdefault(sname, [0])
                lst.append(lst[-1] + (1 if o.inc_needed else 0))
        per_engine_ops = {}
        for o in self.ops:
            per_engine_ops.setdefault(o.eng, []).append(o)
        block = stack.enter_context(self.nc.Block())
        nwaits = [0]
        for eng_name in ENGINES:
            ops = per_engine_ops.get(eng_name, [])
            if not ops:
                continue
            def mk(ops, eng_name):
                def section(eng):
                    for o in ops:
                        for (k, seq) in o.waits:
                            if k[0] == "D":
                                eng.wait_ge(dsems[k[1]], seq * 16)
                            else:
                                val = inc_prefix[k[1]][seq]
                                if val > 0:
                                    eng.wait_ge(sems[k[1]], val)
                            nwaits[0] += 1
                        if o.queue is not None:
                            if o.dma_use > 1:
                                eng.wait_ge(dsems[o.dma_sid], (o.dma_use - 1) * 16)
                            instr = o.fn(eng)
                            instr.then_inc(dsems[o.dma_sid], 16)
                        else:
                            if o.pre_drain:
                                eng.drain()
                            instr = o.fn(eng)
                            if o.inc_needed:
                                if instr is None:
                                    instr = eng.nop()
                                if eng_name in ("vector", "scalar", "gpsimd", "tensor"):
                                    instr = eng.drain()
                                instr.then_inc(sems[self._stream_name(o)], 1)
                return section
            getattr(block, eng_name)(mk(ops, eng_name))
        self.nwaits = nwaits[0]

    def dma(self, queue, out, in_, tag="", **kw):
        eng = DMAQ[queue]
        return self.op(eng, lambda e: e.dma_start(out=out, in_=in_, **kw),
                       reads=[in_], writes=[out], queue=queue, tag=tag)

    def tt(self, out, in0, in1, op, eng="vector", tag=""):
        return self.op(eng, lambda e: e.tensor_tensor(out=out, in0=in0, in1=in1, op=op),
                       reads=[in0, in1], writes=[out], tag=tag)

    def ts(self, out, in0, s1, op0, s2=None, op1=None, eng="vector", accum_out=None, tag=""):
        r = [in0]
        if isinstance(s1, bass.AP) or hasattr(s1, "tensor"):
            r.append(s1)
        if s2 is not None and hasattr(s2, "tensor"):
            r.append(s2)
        w = [out] + ([accum_out] if accum_out is not None else [])
        if op1 is None:
            return self.op(eng, lambda e: e.tensor_scalar(out=out, in0=in0, scalar1=s1, scalar2=None, op0=op0, accum_out=accum_out),
                           reads=r, writes=w, tag=tag)
        return self.op(eng, lambda e: e.tensor_scalar(out=out, in0=in0, scalar1=s1, scalar2=s2, op0=op0, op1=op1, accum_out=accum_out),
                       reads=r, writes=w, tag=tag)

    def stt(self, out, in0, scalar, in1, op0, op1, eng="vector", tag=""):
        r = [in0, in1]
        if hasattr(scalar, "tensor"):
            r.append(scalar)
        return self.op(eng, lambda e: e.scalar_tensor_tensor(out=out, in0=in0, scalar=scalar, in1=in1, op0=op0, op1=op1),
                       reads=r, writes=[out], tag=tag)

    def act(self, out, in_, func, bias=0.0, scale=1.0, accum_out=None, tag=""):
        r = [in_]
        if hasattr(bias, "tensor"):
            r.append(bias)
        if hasattr(scale, "tensor"):
            r.append(scale)
        w = [out] + ([accum_out] if accum_out is not None else [])
        return self.op("scalar", lambda e: e.activation(out=out, in_=in_, func=func, bias=bias, scale=scale, accum_out=accum_out),
                       reads=r, writes=w, tag=tag)

    def copy(self, out, in_, eng="vector", tag=""):
        return self.op(eng, lambda e: e.tensor_copy(out=out, in_=in_),
                       reads=[in_], writes=[out], tag=tag)

    def memset(self, out, val, eng="vector", tag=""):
        return self.op(eng, lambda e: e.memset(out, val), reads=[], writes=[out], tag=tag)

    def mm(self, out, lhsT, rhs, start=True, stop=True, tag=""):
        rw = [out] if not start else []
        return self.op("tensor", lambda e: e.matmul(out, lhsT, rhs, start=start, stop=stop),
                       reads=[lhsT, rhs] + rw, writes=[out], tag=tag)

    def trans(self, out, in_, ident, tag=""):
        return self.op("tensor", lambda e: e.transpose(out, in_, ident),
                       reads=[in_, ident], writes=[out], tag=tag)

    def reduce(self, out, in_, op, absval=None, negate=None, eng="vector", tag=""):
        if op == ALU.add:
            return self.op(eng, lambda e: e.reduce_sum(out=out, in_=in_, axis=AX.X, apply_absolute_value=absval, negate=negate),
                           reads=[in_], writes=[out], tag=tag)
        if op == ALU.max:
            return self.op(eng, lambda e: e.reduce_max(out=out, in_=in_, axis=AX.X, apply_absolute_value=absval, negate=negate),
                           reads=[in_], writes=[out], tag=tag)
        return self.op(eng, lambda e: e.tensor_reduce(out=out, in_=in_, axis=AX.X, op=op, apply_absolute_value=absval, negate=negate),
                       reads=[in_], writes=[out], tag=tag)

    def warmup(self, junk, n=64):
        for i in range(n):
            self.ts(junk, junk, 1.0, ALU.mult, tag="warm")

    def barrier_outputs(self, outs):
        self.op("gpsimd", lambda e: e.nop(), reads=list(outs), writes=[], tag="fin")

    def barrier(self):
        lasts = {}
        for o in self.ops:
            lasts[self._stream_name(o)] = o
        dep_ops = set(lasts.values())
        marks = []
        for e in ENGINES:
            o = self.op(e, lambda eng: eng.nop(), reads=[], writes=[], tag="bar")
            o.deps |= dep_ops
            marks.append(o)
        mset = set(marks)
        for e in ENGINES:
            o = self.op(e, lambda eng: eng.nop(), reads=[], writes=[], tag="bar2")
            o.deps |= mset


class Ctx:

    def __init__(self, nc, stack, prefix=""):
        self.nc = nc
        self.stack = stack
        self.prefix = prefix
        self.tiles = {}

    def sb(self, name, shape, dtype):
        name = self.prefix + name
        t = self.stack.enter_context(self.nc.sbuf_tensor(name, shape, dtype))
        self.tiles[name] = t[:]
        return self.tiles[name]

    def ps(self, name, shape, dtype):
        name = self.prefix + name
        t = self.stack.enter_context(self.nc.psum_tensor(name, shape, dtype))
        self.tiles[name] = t[:]
        return self.tiles[name]


B_PER_CORE = 2
T = 2048
D = 1024
NCORES = 8


def host_constants():
    c = {}
    c["ident"] = np.eye(128, dtype=np.float32)
    c["ident_bf"] = np.eye(128, dtype=ml_dtypes.bfloat16)
    m = np.zeros((128, 128), np.float32)
    m[np.triu_indices(128, 1)] = NEG
    c["diagmask"] = m
    return c


def emit_ln_tile(a, t, x_tile, xln_tm, scratch, extra_scale=1.0, eps=1e-5, width=1024):
    xsq, s1, s2 = scratch["xsq"], scratch["s1"], scratch["s2"]
    mv, rstd = scratch["mv"], scratch["rstd"]
    inv = 1.0 / width
    a.reduce(s1, x_tile, ALU.add)
    a.act(xsq, x_tile, AF.Square, accum_out=s2)
    a.ts(mv[:, 0:1], s1, inv, ALU.mult)
    a.ts(mv[:, 1:2], s2, inv, ALU.mult)
    a.ts(s1, mv[:, 0:1], mv[:, 0:1], ALU.mult)
    a.stt(rstd, s1, -1.0, mv[:, 1:2], ALU.mult, ALU.add)
    a.ts(rstd, rstd, eps, ALU.add)
    a.act(rstd, rstd, AF.Ln)
    a.act(rstd, rstd, AF.Exp, scale=-0.5)
    if extra_scale != 1.0:
        a.ts(rstd, rstd, float(extra_scale), ALU.mult)
    a.ts(xln_tm, x_tile, mv[:, 0:1], ALU.subtract, rstd[:, 0:1], ALU.mult)


def emit_tm_to_fm(a, x_tm, out_fm_chunks, identb, pxp, nchunk=8):
    for c in range(nchunk):
        a.trans(pxp, x_tm[:, c * 128:(c + 1) * 128], identb)
        a.copy(out_fm_chunks[c], pxp)


def emit_l0_ffn(a, ctx, d, PS, com, xin, xout):
    hidden = 3072
    HC = hidden // 128
    NTOK = B_PER_CORE * T
    NT = NTOK // 128
    NT512 = NTOK // 512

    if True:
        sb = ctx.sb
        x, xo = xin, xout
        wgT, wuT, wdT = d["wgT"], d["wuT"], d["wdT"]
        identb = com["identb"]
        junk = com["junk"]
        xtile = [sb(f"xt{i}", [128, D], F32) for i in range(2)]
        xln_tm = sb("xln_tm", [128, D], BF16)
        xlnf = sb("xlnf", [128, 8, T], BF16)
        scratch = {"xsq": sb("xsq", [128, D], BF16), "s1": sb("s1", [128, 1], F32),
                   "s2": sb("s2", [128, 1], F32), "mv": sb("mv", [128, 2], F32),
                   "rstd": sb("rstd", [128, 1], F32)}
        wgt = sb("wgt", [128, 8, hidden], BF16)
        wut = sb("wut", [128, 8, hidden], BF16)
        wdt = [sb(f"wdt{i}", [128, D], BF16) for i in range(2)]
        hbuf = sb("hbuf", [128, HC, 512], BF16)
        sg = [sb(f"sg{i}", [128, 512], BF16) for i in range(2)]
        pg = [PS["PSD"], PS["PSE"]]
        pu = [PS["PSA"][:, 0:512], PS["PSA"][:, 512:1024]]
        pd = PS["PSA"][:, 1024:2048]
        pxp = PS["PSB"][:, 0:128]

        for c in range(8):
            a.dma("gpq", wgt[:, c, :], wgT[c * 128:(c + 1) * 128, :])
            a.dma("gpq", wut[:, c, :], wuT[c * 128:(c + 1) * 128, :])

        NTSEQ = T // 128
        NT512SEQ = T // 512
        for b in range(B_PER_CORE):
            base = b * T
            for j in range(NTSEQ):
                xt = xtile[j % 2]
                r0 = base + j * 128
                a.dma("syq", xt, x[r0:r0 + 128, :])
                emit_ln_tile(a, None, xt, xln_tm, scratch, extra_scale=1.0 / 127.0)
                emit_tm_to_fm(a, xln_tm, [xlnf[:, c, j * 128:(j + 1) * 128] for c in range(8)],
                              identb, pxp)
            for tt in range(NT512SEQ):
                tsl = slice(tt * 512, (tt + 1) * 512)
                for hc in range(HC):
                    g = pg[hc % 2]
                    u = pu[hc % 2]
                    sgt = sg[hc % 2]
                    for c in range(8):
                        a.mm(g, wgt[:, c, hc * 128:(hc + 1) * 128], xlnf[:, c, tsl],
                             start=(c == 0), stop=(c == 7))
                    for c in range(8):
                        a.mm(u, wut[:, c, hc * 128:(hc + 1) * 128], xlnf[:, c, tsl],
                             start=(c == 0), stop=(c == 7))
                    a.act(sgt, g, AF.Silu)
                    a.stt(hbuf[:, hc, :], sgt, 1.0 / 127.0, u, ALU.mult, ALU.mult)
                for st4 in range(4):
                    j = tt * 4 + st4
                    for hc in range(HC):
                        w = wdt[hc % 2]
                        a.dma("gpq", w, wdT[hc * 128:(hc + 1) * 128, :])
                        for half in range(2):
                            a.mm(pd[:, half * 512:(half + 1) * 512],
                                 hbuf[:, hc, st4 * 128:(st4 + 1) * 128],
                                 w[:, half * 512:(half + 1) * 512],
                                 start=(hc == 0), stop=(hc == HC - 1))
                    xt = xtile[j % 2]
                    r0 = base + j * 128
                    a.dma("syq", xt, x[r0:r0 + 128, :])
                    a.tt(xt, xt, pd, ALU.add)
                    a.dma("gpq", xo[r0:r0 + 128, :], xt)



def _shard(x, core):
    return np.ascontiguousarray(x.reshape(NCORES, -1, *x.shape[1:])[core])


def run_l0_ffn(x_full, w_gate, w_up, w_down):
    w_gate, w_up, w_down = [np.ascontiguousarray(np.asarray(w, np.int32)) for w in (w_gate, w_up, w_down)]
    """x_full [16, 2048, 1024] fp32; weights int32 as given by the reference."""
    consts = host_constants()
    nc = build_l0_ffn()
    wgT = np.ascontiguousarray(w_gate.T)
    wuT = np.ascontiguousarray(w_up.T)
    wdT = np.ascontiguousarray(w_down.T)
    in_maps = []
    for c in range(NCORES):
        in_maps.append({
            "x": x_full.reshape(16 * T, D)[c * B_PER_CORE * T:(c + 1) * B_PER_CORE * T],
            "wgT": wgT, "wuT": wuT, "wdT": wdT,
            "ident": consts["ident_bf"],
        })
    res = run_bass_kernel_spmd(nc, in_maps, core_ids=list(range(NCORES)))
    out = np.concatenate([res.results[c]["xo"] for c in range(NCORES)], axis=0)
    return out.reshape(16, T, D)


def emit_attn(a, ctx, d, PS, com, xin, xout, variant):
    NTOK = B_PER_CORE * T
    if True:
        sb = ctx.sb
        x, xo = xin, xout
        wqT, woT = d["wqT"], d["woT"]
        identb, diagm, junk = com["identb"], com["diagm"], com["junk"]
        if variant == "b":
            wkT, wvT, alibik = d["wkT"], d["wvT"], d["alibik"]
        else:
            wqTs, wkT, wvT = d["wqTs"], d["wkT"], d["wvT"]
            dks, dkss, dkm, dkms, dv = d["dks"], d["dkss"], d["dkm"], d["dkms"], d["dv"]
            c1, c2 = d["c1"], d["c2"]
        xtile = [sb("xt0", [128, D], F32)] * 2
        xln_tm = sb("xln_tm", [128, D], BF16)
        xlnf = sb("xlnf", [128, 8, T], BF16)
        scratch = {"xsq": sb("xsq", [128, D], BF16), "s1": sb("s1", [128, 1], F32),
                   "s2": sb("s2", [128, 1], F32), "mv": sb("mv", [128, 2], F32),
                   "rstd": sb("rstd", [128, 1], F32)}
        wq = sb("wq", [128, 8, D], BF16)
        wo = sb("wo", [128, 8, D], BF16)
        if variant == "b":
            wk = sb("wk", [128, 8, D], BF16)
            wv = sb("wv", [128, 8, D], BF16)

        else:
            wqs = sb("wqs", [128, 8, D], BF16)
            wkt = sb("wkt", [128, 8, 192], BF16)
            wvt = sb("wvt", [128, 8, 192], BF16)
            dkst = sb("dkst", [128, 2, D], BF16)
            dksst = sb("dksst", [128, 2, D], BF16)
            dvt = sb("dvt", [128, 2, D], BF16)
            cro = sb("cro", [128, T], BF16)
            zk = sb("zk", [128, 2, T], BF16)
            xv = None

            t1 = sb("t1", [128, 512], F32)
            t2 = sb("t2", [128, 512], F32)
        vtm = sb("vtm", [128, 16, D], BF16)
        oall = sb("oall", [128, 16, D], BF16)
        if variant == "a":
            xv = oall[:, 0:4, :].rearrange("p a b -> p (a b)").rearrange("p (c b) -> p c b", c=2)
        qs = sb("qs", [128, T], BF16)
        ks = sb("ks", [128, T], BF16)
        pbuf = sb("pbuf", [128, T], BF16)
        pts = sb("pts", [128, T], BF16)
        rowmax = sb("rowmax", [128, 1], F32)
        rowsum = sb("rowsum", [128, 1], F32)
        nbias = sb("nbias", [128, 1], F32)
        recl = sb("recl", [128, 1], F32)
        of = sb("of", [128, 8, 128], BF16)
        PSA = PS["PSA"]
        PSB = PS["PSB"]
        PSC = PS["PSD"][:, 0:128]

        for c in range(8):
            a.dma("gpq", wq[:, c, :], wqT[c * 128:(c + 1) * 128, :])
            a.dma("gpq", wo[:, c, :], woT[c * 128:(c + 1) * 128, :])
        if variant == "b":
            for c in range(8):
                a.dma("gpq", wk[:, c, :], wkT[c * 128:(c + 1) * 128, :])
                a.dma("gpq", wv[:, c, :], wvT[c * 128:(c + 1) * 128, :])

        else:
            for c in range(8):
                a.dma("gpq", wqs[:, c, :], wqTs[c * 128:(c + 1) * 128, :])
                a.dma("gpq", wkt[:, c, :192], wkT[c * 128:(c + 1) * 128, :])
                a.dma("gpq", wvt[:, c, :192], wvT[c * 128:(c + 1) * 128, :])
            a.dma("syq", cro[0:64, :], c1)
            a.dma("syq", cro[64:128, :], c2)
            a.dma("gpq", dkst[:, 0, :], dks[0:128, :])
            a.dma("gpq", dkst[0:64, 1, :], dks[128:192, :])
            a.dma("gpq", dksst[:, 0, :], dkss[0:128, :])
            a.dma("gpq", dksst[0:64, 1, :], dkss[128:192, :])
            mskt = vtm[:, 0:2, :]
            mskst = vtm[:, 2:4, :]
            a.dma("syq", mskt[:, 0, :], dkm[0:128, :])
            a.dma("syq", mskt[0:64, 1, :], dkm[128:192, :])
            a.dma("syq", mskst[:, 0, :], dkms[0:128, :])
            a.dma("syq", mskst[0:64, 1, :], dkms[128:192, :])
            a.tt(dkst[:, 0, :], dkst[:, 0, :], mskt[:, 0, :], ALU.mult)
            a.tt(dkst[0:64, 1, :], dkst[0:64, 1, :], mskt[0:64, 1, :], ALU.mult)
            a.tt(dksst[:, 0, :], dksst[:, 0, :], mskst[:, 0, :], ALU.mult)
            a.tt(dksst[0:64, 1, :], dksst[0:64, 1, :], mskst[0:64, 1, :], ALU.mult)
            a.dma("gpq", dvt[:, 0, :], dv[0:128, :])
            a.dma("gpq", dvt[0:64, 1, :], dv[128:192, :])

        HD = 64
        scale = 1.0 / math.sqrt(HD)
        for b in range(B_PER_CORE):
            base = b * T
            for j in range(T // 128):
                xt = xtile[j % 2]
                a.dma("syq", xt, x[base + j * 128:base + (j + 1) * 128, :])
                emit_ln_tile(a, None, xt, xln_tm, scratch)
                emit_tm_to_fm(a, xln_tm, [xlnf[:, c, j * 128:(j + 1) * 128] for c in range(8)],
                              identb, PSB[:, 0:128])
            if variant == "a":
                for tt in range(4):
                    tsl = slice(tt * 512, (tt + 1) * 512)
                    pz = PSA[:, 0:512]
                    pz2 = PSA[:, 512:1024]
                    for (wt, dst) in ((wkt, zk), (wvt, xv)):
                        for c in range(8):
                            a.mm(pz, wt[:, c, 0:128], xlnf[:, c, tsl], start=(c == 0), stop=(c == 7))
                        a.copy(dst[:, 0, tsl], pz)
                        for c in range(8):
                            a.mm(pz2[0:64, :], wt[:, c, 128:192], xlnf[:, c, tsl], start=(c == 0), stop=(c == 7))
                        a.copy(dst[0:64, 1, tsl], pz2[0:64, :])
            for j in range(T // 128):
                pv = PSA[:, 1024:2048]
                if variant == "b":
                    for hf in range(2):
                        for c in range(8):
                            a.mm(pv[:, hf * 512:(hf + 1) * 512],
                                 xlnf[:, c, j * 128:(j + 1) * 128],
                                 wv[:, c, hf * 512:(hf + 1) * 512],
                                 start=(c == 0), stop=(c == 7))
                else:
                    for hf in range(2):
                        a.mm(pv[:, hf * 512:(hf + 1) * 512], xv[:, 0, j * 128:(j + 1) * 128],
                             dvt[:, 0, hf * 512:(hf + 1) * 512], start=True, stop=False)
                        a.mm(pv[:, hf * 512:(hf + 1) * 512], xv[0:64, 1, j * 128:(j + 1) * 128],
                             dvt[0:64, 1, hf * 512:(hf + 1) * 512], start=False, stop=True)
                a.copy(vtm[:, j, :], pv)
            for h in range(16):
                hsl = slice(h * 64, (h + 1) * 64)
                for tt in range(4):
                    tsl = slice(tt * 512, (tt + 1) * 512)
                    pq = PSA[:, 0:512]
                    pk = PSA[:, 512:1024]
                    if variant == "b":
                        for c in range(8):
                            a.mm(pq[0:64, :], wq[:, c, hsl], xlnf[:, c, tsl], start=(c == 0), stop=(c == 7))
                        a.copy(qs[0:64, tsl], pq[0:64, :])
                        for c in range(8):
                            a.mm(pk[0:64, :], wk[:, c, hsl], xlnf[:, c, tsl], start=(c == 0), stop=(c == 7))
                        a.copy(ks[0:64, tsl], pk[0:64, :])
                    else:
                        pq2 = PSA[:, 1024:1536]
                        for c in range(8):
                            a.mm(pq[0:64, :], wq[:, c, hsl], xlnf[:, c, tsl], start=(c == 0), stop=(c == 7))
                        for c in range(8):
                            a.mm(pq2[0:64, :], wqs[:, c, hsl], xlnf[:, c, tsl], start=(c == 0), stop=(c == 7))
                        a.tt(t1[0:64, :], pq[0:64, :], cro[0:64, tsl], ALU.mult)
                        a.tt(t2[0:64, :], pq2[0:64, :], cro[64:128, tsl], ALU.mult)
                        a.tt(qs[0:64, tsl], t1[0:64, :], t2[0:64, :], ALU.add)
                        a.mm(pq[0:64, :], dkst[:, 0, hsl], zk[:, 0, tsl], start=True, stop=False)
                        a.mm(pq[0:64, :], dkst[0:64, 1, hsl], zk[0:64, 1, tsl], start=False, stop=True)
                        a.mm(pq2[0:64, :], dksst[:, 0, hsl], zk[:, 0, tsl], start=True, stop=False)
                        a.mm(pq2[0:64, :], dksst[0:64, 1, hsl], zk[0:64, 1, tsl], start=False, stop=True)
                        a.tt(t1[0:64, :], pq[0:64, :], cro[0:64, tsl], ALU.mult)
                        a.tt(t2[0:64, :], pq2[0:64, :], cro[64:128, tsl], ALU.mult)
                        a.tt(ks[0:64, tsl], t1[0:64, :], t2[0:64, :], ALU.add)
                if variant == "b":
                    a.memset(qs[64:65, :], 1.0)
                    a.dma("syq", ks[64:65, :], alibik[h:h + 1, :])
                    KR = 65
                else:
                    KR = 64
                for qt in range(T // 128):
                    L = (qt + 1) * 128
                    S = PSA[:, 0:L]
                    for kb in range((L + 511) // 512):
                        n = min(512, L - kb * 512)
                        a.mm(PSA[:, kb * 512:kb * 512 + n], qs[0:KR, qt * 128:(qt + 1) * 128],
                             ks[0:KR, kb * 512:kb * 512 + n], start=True, stop=True)
                    a.tt(PSA[:, L - 128:L], PSA[:, L - 128:L], diagm, ALU.add)
                    a.reduce(rowmax, S, ALU.max)
                    a.ts(nbias, rowmax, -scale, ALU.mult)
                    a.act(pbuf[:, 0:L], S, AF.Exp, bias=nbias[:, 0:1], scale=scale,
                          accum_out=rowsum)
                    a.op("vector", lambda e, _r=recl, _s=rowsum: e.reciprocal(out=_r, in_=_s),
                         reads=[rowsum], writes=[recl])
                    for kb in range(L // 128):
                        a.trans(PSB[:, kb * 128:(kb + 1) * 128], pbuf[:, kb * 128:(kb + 1) * 128], identb)
                    a.copy(pts[:, 0:L], PSB[:, 0:L])
                    for kb in range(L // 128):
                        a.mm(PSC[:, 0:64], pts[:, kb * 128:(kb + 1) * 128],
                             vtm[:, kb, hsl], start=(kb == 0), stop=(kb == L // 128 - 1))
                    a.ts(oall[:, qt, hsl], PSC[:, 0:64], recl[:, 0:1], ALU.mult)
            for j in range(T // 128):
                emit_tm_to_fm(a, oall[:, j, :], [of[:, c, :] for c in range(8)], identb,
                              PSB[:, 0:128])
                pd = PSA[:, 0:1024]
                for c in range(8):
                    for hf in range(2):
                        a.mm(pd[:, hf * 512:(hf + 1) * 512], of[:, c, :],
                             wo[:, c, hf * 512:(hf + 1) * 512],
                             start=(c == 0), stop=(c == 7))
                xt = xtile[j % 2]
                a.dma("scq", xt, x[base + j * 128:base + (j + 1) * 128, :])
                a.tt(xt, xt, pd, ALU.add)
                a.dma("gpq", xo[base + j * 128:base + (j + 1) * 128, :], xt)


def rope_tables_interleaved(hd=64, base=10000.0):
    half = hd // 2
    freq = 1.0 / base ** (np.arange(half, dtype=np.float32) / half)
    ang = np.outer(np.arange(T, dtype=np.float32), freq)
    cos = np.cos(ang).astype(np.float32).T
    sin = np.sin(ang).astype(np.float32).T
    C1 = np.zeros((hd, T), np.float32)
    C2 = np.zeros((hd, T), np.float32)
    C1[0::2] = cos
    C1[1::2] = cos
    C2[0::2] = -sin
    C2[1::2] = sin
    return C1, C2


def host_attn_a_decoders(key_decoder, value_decoder):
    F_, H_, KR_ = 32, 16, 3
    dk = np.zeros((192, D), np.float32)
    for cp in range(2):
        for c in range(2):
            cc = c ^ cp
            dk[np.arange(F_)[:, None, None] * 6 + np.arange(KR_)[None, :, None] * 2 + cp,
               np.arange(H_)[None, None, :] * 64 + 2 * np.arange(F_)[:, None, None] + c] =                 np.transpose(key_decoder[:, :, :, cc], (0, 2, 1))
    dks = dk.reshape(192, 16, 32, 2)[:, :, :, ::-1].reshape(192, D)
    row_odd = (np.arange(192) % 2 == 1)
    col_even = (np.arange(D) % 2 == 0)
    m = np.ones((192, D), np.float32)
    m[np.ix_(row_odd, col_even)] = -1.0
    ms = np.ones((192, D), np.float32)
    ms[np.ix_(row_odd, ~col_even)] = -1.0
    dv = np.zeros((192, D), np.float32)
    dv[:, :] = np.transpose(value_decoder, (1, 0, 2)).reshape(192, D)
    return (np.ascontiguousarray(dk), np.ascontiguousarray(dks),
            m.astype(ml_dtypes.bfloat16), ms.astype(ml_dtypes.bfloat16),
            np.ascontiguousarray(dv))


def run_attn_a(x_full, Wq, Wk, Wv, key_decoder, value_decoder, Wproj):
    Wq, Wk, Wv, Wproj = [np.asarray(w, np.float32) for w in (Wq, Wk, Wv, Wproj)]
    consts = host_constants()
    nc = build_attn("a")
    perm = np.arange(D) ^ 1
    dk, dks_sw, m, ms, dv = host_attn_a_decoders(key_decoder, value_decoder)
    C1, C2 = rope_tables_interleaved()
    in_common = {
        "wqT": np.ascontiguousarray(Wq), "wqTs": np.ascontiguousarray(Wq[:, perm]),
        "wkT": np.ascontiguousarray(Wk), "wvT": np.ascontiguousarray(Wv),
        "dks": dk, "dkss": dks_sw, "dkm": m, "dkms": ms, "dv": dv,
        "woT": np.ascontiguousarray(Wproj),
        "c1": C1.astype(ml_dtypes.bfloat16), "c2": C2.astype(ml_dtypes.bfloat16),
        "ident": consts["ident_bf"], "diagmask": consts["diagmask"],
    }
    in_maps = []
    for c in range(NCORES):
        mm_ = dict(in_common)
        mm_["x"] = x_full.reshape(16 * T, D)[c * B_PER_CORE * T:(c + 1) * B_PER_CORE * T]
        in_maps.append(mm_)
    res = run_bass_kernel_spmd(nc, in_maps, core_ids=list(range(NCORES)))
    out = np.concatenate([res.results[c]["xo"] for c in range(NCORES)], axis=0)
    return out.reshape(16, T, D)


def host_alibi(scale=8.0):
    slopes = np.array([2.0 ** (-(h + 1) / 2.0) for h in range(16)], np.float32)
    j = np.arange(T, dtype=np.float32)
    return (-slopes[:, None] * j[None, :] * scale).astype(ml_dtypes.bfloat16)


def run_attn_b(x_full, Wq, Wk, Wv, Wo):
    Wq, Wk, Wv, Wo = [np.asarray(w, np.float32) for w in (Wq, Wk, Wv, Wo)]
    consts = host_constants()
    nc = build_attn("b")
    in_common = {
        "wqT": np.ascontiguousarray(Wq.T), "wkT": np.ascontiguousarray(Wk.T),
        "wvT": np.ascontiguousarray(Wv.T), "woT": np.ascontiguousarray(Wo.T),
        "ident": consts["ident_bf"], "diagmask": consts["diagmask"],
        "alibik": host_alibi(),
    }
    in_maps = []
    for c in range(NCORES):
        m = dict(in_common)
        m["x"] = x_full.reshape(16 * T, D)[c * B_PER_CORE * T:(c + 1) * B_PER_CORE * T]
        in_maps.append(m)
    res = run_bass_kernel_spmd(nc, in_maps, core_ids=list(range(NCORES)))
    out = np.concatenate([res.results[c]["xo"] for c in range(NCORES)], axis=0)
    return out.reshape(16, T, D)


def emit_fp4_quant(a, src, dst, q, nb, width):
    absx, t1i, cq = q["absx"], q["t1i"], q["cq"]
    bmax, bscale, binv = q["bmax"], q["bscale"], q["binv"]
    w3 = (src, absx, AF.Abs)
    a.act(absx[:, 0:width], src, AF.Abs)
    a.op("vector", lambda e: e.tensor_reduce(
        out=bmax[:, 0:nb].unsqueeze(2), in_=absx[:, 0:width].rearrange("p (b w) -> p b w", w=16),
        axis=AX.X, op=ALU.max), reads=[absx], writes=[bmax])
    a.ts(bscale[:, 0:nb], bmax[:, 0:nb], 1.0 / 6.0, ALU.mult, 1e-12, ALU.max)
    a.op("vector", lambda e: e.reciprocal(out=binv[:, 0:nb], in_=bscale[:, 0:nb]),
         reads=[bscale], writes=[binv])
    bc = binv[:, 0:nb].unsqueeze(2).broadcast_to([128, nb, 16])
    a.op("vector", lambda e: e.tensor_tensor(
        out=absx[:, 0:width].rearrange("p (b w) -> p b w", w=16),
        in0=absx[:, 0:width].rearrange("p (b w) -> p b w", w=16), in1=bc, op=ALU.mult),
        reads=[absx, binv], writes=[absx])
    a.ts(t1i[:, 0:width], absx[:, 0:width], 4.0, ALU.min, 2.0, ALU.subtract)
    a.ts(cq[:, 0:width], absx[:, 0:width], 5.0, ALU.is_gt, 2.0, ALU.mult)
    a.stt(cq[:, 0:width], t1i[:, 0:width], 0.0, cq[:, 0:width], ALU.max, ALU.add)
    a.ts(t1i[:, 0:width], absx[:, 0:width], 2.0, ALU.min, 2.0, ALU.mult)
    a.stt(cq[:, 0:width], t1i[:, 0:width], 0.5, cq[:, 0:width], ALU.mult, ALU.add)
    a.act(absx[:, 0:width], src, AF.Sign)
    bs = bscale[:, 0:nb].unsqueeze(2).broadcast_to([128, nb, 16])
    a.op("vector", lambda e: e.tensor_tensor(
        out=absx[:, 0:width].rearrange("p (b w) -> p b w", w=16),
        in0=absx[:, 0:width].rearrange("p (b w) -> p b w", w=16), in1=bs, op=ALU.mult),
        reads=[absx, bscale], writes=[absx])
    a.tt(dst, cq[:, 0:width], absx[:, 0:width], ALU.mult)


def emit_l1_ffn(a, ctx, d, PS, com, xin, xout):
    hidden = 2816
    HC = hidden // 128
    NB_D = D // 16
    NB_H = hidden // 16
    NTOK = B_PER_CORE * T

    if True:
        sb = ctx.sb
        x, xo = xin, xout
        wg_in, wu_in, wd_in = d["wg"], d["wu"], d["wd"]
        identb, junk = com["identb"], com["junk"]
        xtile = sb("xt0", [128, D], F32)
        xln_tm = sb("xln_tm", [128, D], F32)
        scratch = {"xsq": sb("xsq", [128, D], BF16), "s1": sb("s1", [128, 1], F32),
                   "s2": sb("s2", [128, 1], F32), "mv": sb("mv", [128, 2], F32),
                   "rstd": sb("rstd", [128, 1], F32)}
        q = {"absx": sb("absx", [128, hidden], F32),
             "t1i": sb("t1i", [128, hidden], mybir.dt.int16),
             
             "cq": sb("cq", [128, hidden], F32),
             "bmax": sb("bmax", [128, NB_H], F32), "bscale": sb("bscale", [128, NB_H], F32),
             "binv": sb("binv", [128, NB_H], F32)}
        wqtm = sb("wqtm", [128, hidden], BF16)
        wgq = sb("wgq", [128, 8, hidden], BF16)
        wuq = sb("wuq", [128, 8, hidden], BF16)
        wdq = sb("wdq", [128, HC, D], BF16)
        xq_tm = sb("xq_tm", [128, D], BF16)
        xqf = sb("xqf", [128, 8, 128], BF16)
        htile = sb("htile", [128, hidden], F32)
        hq_tm = sb("hq_tm", [128, hidden], BF16)
        hqf = sb("hqf", [128, HC, 128], BF16)
        PSA = PS["PSA"]
        PSB = PS["PSB"][:, 0:256]


        for (w_in, dstfm, nrows, wwidth, nbw) in (
                (wg_in, wgq, hidden, D, NB_D), (wu_in, wuq, hidden, D, NB_D),
                (wd_in, wdq, D, hidden, NB_H)):
            RC = nrows // 128
            for rc in range(RC):
                src = htile[:, 0:wwidth]
                a.dma("syq", src, w_in[rc * 128:(rc + 1) * 128, :])
                emit_fp4_quant(a, src, wqtm[:, 0:wwidth], q, nbw, wwidth)
                for c in range(wwidth // 128):
                    a.trans(PSB[:, 0:128], wqtm[:, c * 128:(c + 1) * 128], identb)
                    a.copy(dstfm[:, c, rc * 128:(rc + 1) * 128], PSB[:, 0:128])

        ISL = [(i * 512, min(512, hidden - i * 512)) for i in range((hidden + 511) // 512)]
        for j in range(NTOK // 128):
            a.dma("syq", xtile, x[j * 128:(j + 1) * 128, :])
            emit_ln_tile(a, None, xtile, xln_tm, scratch)
            emit_fp4_quant(a, xln_tm, xq_tm, q, NB_D, D)
            for c in range(8):
                a.trans(PSB[:, 0:128], xq_tm[:, c * 128:(c + 1) * 128], identb)
                a.copy(xqf[:, c, :], PSB[:, 0:128])
            for (i0, n) in ISL:
                pg = PSA[:, 0:512]
                pu = PSA[:, 512:1024]
                for c in range(8):
                    a.mm(pg[:, 0:n], wgq[:, c, i0:i0 + n], xqf[:, c, :].transpose([0, 1]) if False else xqf[:, c, :],
                         start=(c == 0), stop=(c == 7)) if False else None
                for c in range(8):
                    a.mm(pg[:, 0:n], xqf[:, c, :], wgq[:, c, i0:i0 + n],
                         start=(c == 0), stop=(c == 7))
                for c in range(8):
                    a.mm(pu[:, 0:n], xqf[:, c, :], wuq[:, c, i0:i0 + n],
                         start=(c == 0), stop=(c == 7))
                a.act(htile[:, i0:i0 + n], pg[:, 0:n], AF.Silu)
                a.tt(htile[:, i0:i0 + n], htile[:, i0:i0 + n], pu[:, 0:n], ALU.mult)
            emit_fp4_quant(a, htile, hq_tm, q, NB_H, hidden)
            for c in range(HC):
                a.trans(PSB[:, 0:128], hq_tm[:, c * 128:(c + 1) * 128], identb)
                a.copy(hqf[:, c, :], PSB[:, 0:128])
            pd = PSA[:, 1024:2048]
            for hf in range(2):
                for c in range(HC):
                    a.mm(pd[:, hf * 512:(hf + 1) * 512], hqf[:, c, :],
                         wdq[:, c, hf * 512:(hf + 1) * 512],
                         start=(c == 0), stop=(c == HC - 1))
            a.tt(xtile, xtile, pd[:, 0:1024], ALU.add)
            a.dma("gpq", xo[j * 128:(j + 1) * 128, :], xtile)


def run_l1_ffn(x_full, w_gate, w_up, w_down):
    consts = host_constants()
    nc = build_l1_ffn()
    in_common = {"wg": np.ascontiguousarray(w_gate), "wu": np.ascontiguousarray(w_up),
                 "wd": np.ascontiguousarray(w_down), "ident": consts["ident_bf"]}
    in_maps = []
    for c in range(NCORES):
        m = dict(in_common)
        m["x"] = x_full.reshape(16 * T, D)[c * B_PER_CORE * T:(c + 1) * B_PER_CORE * T]
        in_maps.append(m)
    res = run_bass_kernel_spmd(nc, in_maps, core_ids=list(range(NCORES)))
    out = np.concatenate([res.results[c]["xo"] for c in range(NCORES)], axis=0)
    return out.reshape(16, T, D)


def emit_mamba(a, ctx, d, PS, com, xin, xout):
    E = 2048
    NSTATE = 16
    TC = 512
    NTC = T // TC
    NTOK = B_PER_CORE * T

    if True:
        sb = ctx.sb
        x, xo = xin, xout
        winT, wxT, wdtT, woutT = d["winT"], d["wxT"], d["wdtT"], d["woutT"]
        convr, convb, bdt, dprm, alr = d["convr"], d["convb"], d["bdt"], d["dprm"], d["alr"]
        sel_in, selt_in, selb_in = d["sel"], d["selt"], d["selb"]
        lnw_in, lnb_in = d["lnw"], d["lnb"]
        identb, junk = com["identb"], com["junk"]
        win = sb("win", [128, 8, 2 * E], BF16)
        wx = sb("wx", [128, 16, 96], BF16)
        wdt = sb("wdt", [64, E], BF16)
        wout = [sb(f"wout{i}", [128, 2, D], BF16) for i in range(2)]
        conw = sb("conw", [128, 64], F32)
        conb = sb("conb", [128, 16], F32)
        bdtt = sb("bdtt", [128, 16], F32)
        dprt = sb("dprt", [128, 16], F32)
        alrt = sb("alrt", [128, 256], F32)
        nega = alrt
        self_ = sb("selr", [128, 16, 128], BF16)
        selt = sb("seltr", [128, 16, 128], BF16)
        selbc = sb("selbc", [128, 256], BF16)
        lnw = sb("lnwr", [128, D], F32)
        lnb = sb("lnbr", [128, D], F32)
        xtile = sb("xt0", [128, D], F32)
        xln_tm = sb("xln_tm", [128, D], BF16)
        xlnf = sb("xlnf", [128, 8, TC], BF16)
        scratch = {"xsq": sb("xsq", [128, D], BF16), "s1": sb("s1", [128, 1], F32),
                   "s2": sb("s2", [128, 1], F32), "mv": sb("mv", [128, 2], F32),
                   "rstd": sb("rstd", [128, 1], F32)}
        xs = sb("xs", [128, 16, TC + 3], BF16)
        uf = sb("uf", [128, 16, TC], BF16)
        zf = sb("zf", [128, 16, TC], BF16)
        yf = sb("yf", [128, 16, TC], BF16)
        cacc = sb("cacc", [128, TC], F32)
        reld = cacc
        xdb = sb("xdb", [128, TC], BF16)
        brep = sb("brep", [128, TC], F32)
        crep = sb("crep", [128, TC], F32)
        deltac = sb("deltac", [128, TC], BF16)
        absd = sb("absd", [128, TC], F32)
        duc = sb("duc", [128, TC], BF16)
        at = sb("at", [128, TC], F32)
        bt = sb("bt", [128, TC], F32)
        ht = sb("ht", [128, TC], F32)
        gt = sb("gt", [128, TC], BF16)
        y2 = sb("y2", [128, TC], F32)
        zs = sb("zs", [128, TC], F32)
        hstate = sb("hstate", [128, 256], F32)
        mo = sb("mo", [128, D], F32)
        PSA = PS["PSA"]
        PSB = PS["PSB"][:, 0:256]

        for c in range(8):
            a.dma("gpq", win[:, c, :], winT[c * 128:(c + 1) * 128, :])
        for c in range(16):
            a.dma("gpq", wx[:, c, :], wxT[c * 128:(c + 1) * 128, :])
        a.dma("gpq", wdt, wdtT)
        for (dst, src) in ((conw, convr), (conb, convb), (bdtt, bdt), (dprt, dprm),
                           (alrt, alr), (lnw, lnw_in), (lnb, lnb_in)):
            a.dma("syq", dst, src)
        a.dma("syq", self_.rearrange("p a b -> p (a b)"), sel_in)
        a.dma("syq", selt.rearrange("p a b -> p (a b)"), selt_in)
        a.dma("syq", selbc, selb_in)
        a.act(nega, alrt, AF.Exp)
        a.ts(nega, nega, -1.0, ALU.mult)

        for b in range(B_PER_CORE):
            base = b * T
            a.memset(hstate, 0.0)
            for tc in range(NTC):
                t0 = tc * TC
                for j in range(TC // 128):
                    a.dma("syq", xtile, x[base + t0 + j * 128:base + t0 + (j + 1) * 128, :])
                    emit_ln_tile(a, None, xtile, xln_tm, scratch)
                    emit_tm_to_fm(a, xln_tm, [xlnf[:, c, j * 128:(j + 1) * 128] for c in range(8)],
                                  identb, PSB[:, 0:128])
                if tc == 0:
                    for c in range(16):
                        a.memset(xs[:, c, 0:3], 0.0)
                for ec in range(16):
                    pz = PSA[:, 0:512]
                    for c in range(8):
                        a.mm(pz, win[:, c, ec * 128:(ec + 1) * 128], xlnf[:, c, :],
                             start=(c == 0), stop=(c == 7))
                    a.copy(xs[:, ec, 3:TC + 3], pz)
                    pz2 = PSA[:, 512:1024]
                    for c in range(8):
                        a.mm(pz2, win[:, c, E + ec * 128:E + (ec + 1) * 128], xlnf[:, c, :],
                             start=(c == 0), stop=(c == 7))
                    a.copy(zf[:, ec, :], pz2)
                for ec in range(16):
                    a.ts(cacc, xs[:, ec, 0:TC], conw[:, ec * 4 + 0:ec * 4 + 1], ALU.mult)
                    for k in (1, 2, 3):
                        a.stt(cacc, xs[:, ec, k:TC + k], conw[:, ec * 4 + k:ec * 4 + k + 1],
                              cacc, ALU.mult, ALU.add)
                    a.act(uf[:, ec, :], cacc, AF.Silu, bias=conb[:, ec:ec + 1], scale=1.0)
                for ec in range(16):
                    a.copy(xs[:, ec, 0:3], xs[:, ec, TC:TC + 3])
                pxd = PSA[:, 1024:1536]
                for c in range(16):
                    a.mm(pxd[0:96, :], wx[:, c, :], uf[:, c, :], start=(c == 0), stop=(c == 15))
                a.copy(xdb[0:96, :], pxd[0:96, :])
                a.mm(PSA[0:128, 0:512], selbc[0:96, 0:128], xdb[0:96, :], start=True, stop=True)
                a.copy(brep, PSA[:, 0:512])
                a.mm(PSA[0:128, 512:1024], selbc[0:96, 128:256], xdb[0:96, :], start=True, stop=True)
                a.copy(crep, PSA[:, 512:1024])
                for dc in range(16):
                    pdl = PSA[:, 1536:2048]
                    a.mm(pdl, wdt[:, dc * 128:(dc + 1) * 128], xdb[0:64, :], start=True, stop=True)
                    a.act(absd, pdl, AF.Abs, bias=bdtt[:, dc:dc + 1], scale=1.0)
                    a.act(reld, pdl, AF.Relu, bias=bdtt[:, dc:dc + 1], scale=1.0)
                    a.act(absd, absd, AF.Exp, scale=-1.0)
                    a.act(absd, absd, AF.Ln, bias=1.0, scale=1.0)
                    a.tt(deltac, reld, absd, ALU.add)
                    a.tt(duc, deltac, uf[:, dc, :], ALU.mult)
                    for sub in range(16):
                        ct = dc * 16 + sub
                        prd = PSA[:, 0:512]
                        a.mm(prd, self_[:, sub, :], deltac, start=True, stop=True)
                        a.act(at, prd, AF.Exp, bias=0.0, scale=nega[:, ct:ct + 1])
                        prd2 = PSA[:, 512:1024]
                        a.mm(prd2, self_[:, sub, :], duc, start=True, stop=True)
                        a.tt(bt, prd2, brep, ALU.mult)
                        a.op("vector", lambda e, _a=at, _b=bt, _h=ht, _i=hstate[:, ct:ct + 1]:
                             e.tensor_tensor_scan(out=_h, data0=_a, data1=_b, initial=_i,
                                                  op0=ALU.mult, op1=ALU.add),
                             reads=[at, bt, hstate[:, ct:ct + 1]], writes=[ht])
                        a.copy(hstate[:, ct:ct + 1], ht[:, TC - 1:TC])
                        a.tt(gt, ht, crep, ALU.mult)
                        a.mm(PSA[:, 1024:1536], selt[:, sub, :], gt,
                             start=(sub == 0), stop=(sub == 15))
                    a.stt(y2, uf[:, dc, :], dprt[:, dc:dc + 1], PSA[:, 1024:1536],
                          ALU.mult, ALU.add)
                    a.act(zs, zf[:, dc, :], AF.Exp, scale=-1.0)
                    a.ts(zs, zs, 1.0, ALU.add)
                    a.op("vector", lambda e, _o=zs, _i=zs: e.reciprocal(out=_o, in_=_i),
                         reads=[zs], writes=[zs])
                    a.tt(zs, zs, zf[:, dc, :], ALU.mult)
                    a.tt(yf[:, dc, :], y2, zs, ALU.mult)
                for j in range(TC // 128):
                    pd = PSA[:, 0:1024]
                    for ec in range(16):
                        w = wout[ec % 2]
                        a.dma("gpq", w[:, 0, :], woutT[ec * 128:(ec + 1) * 128, :])
                        for hf in range(2):
                            a.mm(pd[:, hf * 512:(hf + 1) * 512], yf[:, ec, j * 128:(j + 1) * 128],
                                 w[:, 0, hf * 512:(hf + 1) * 512],
                                 start=(ec == 0), stop=(ec == 15))
                    r0 = base + t0 + j * 128
                    a.dma("scq", xtile, x[r0:r0 + 128, :])
                    emit_ln_tile(a, None, xtile, xln_tm, scratch)
                    a.op("vector", lambda e, _o=mo, _p=pd, _x=xln_tm:
                         e.tensor_tensor(out=_o, in0=_p, in1=_x, op=ALU.add),
                         reads=[pd, xln_tm], writes=[mo])
                    emit_ln_tile(a, None, mo, mo, scratch)
                    a.tt(mo, mo, lnw, ALU.mult)
                    a.tt(mo, mo, lnb, ALU.add)
                    a.tt(xtile, xtile, mo, ALU.add)
                    a.dma("gpq", xo[r0:r0 + 128, :], xtile)


def host_mamba_consts():
    sel = np.zeros((16, 128, 128), np.float32)
    for sub in range(16):
        for p in range(128):
            sel[sub, 8 * sub + p % 8, p] = 1.0
    selt = np.transpose(sel, (0, 2, 1))
    sel_h = np.ascontiguousarray(np.transpose(sel, (1, 0, 2)).reshape(128, 16 * 128)).astype(ml_dtypes.bfloat16)
    selt_h = np.ascontiguousarray(np.transpose(selt, (1, 0, 2)).reshape(128, 16 * 128)).astype(ml_dtypes.bfloat16)
    selbc = np.zeros((128, 256), np.float32)
    for p in range(128):
        selbc[64 + p // 8, p] = 1.0
        selbc[80 + p // 8, 128 + p] = 1.0
    return sel_h, selt_h, selbc.astype(ml_dtypes.bfloat16)


def run_mamba(x_full, W_in, conv_w, conv_b, W_x, W_dt, b_dt, A_log, D_param, W_out, ln_w, ln_b):
    consts = host_constants()
    nc = build_mamba()
    sel_h, selt_h, selbc = host_mamba_consts()
    alr = np.zeros((128, 256), np.float32)
    p = np.arange(128)
    for ct in range(256):
        alr[p, ct] = A_log[8 * ct + (p % 8), p // 8]
    in_common = {
        "winT": np.ascontiguousarray(W_in.T), "wxT": np.ascontiguousarray(W_x.T),
        "wdtT": np.ascontiguousarray(W_dt.T), "woutT": np.ascontiguousarray(W_out.T),
        "convr": np.ascontiguousarray(conv_w[:, 0, :].reshape(16, 128, 4).transpose(1, 0, 2).reshape(128, 64)),
        "convb": np.ascontiguousarray(conv_b.reshape(16, 128).T),
        "bdt": np.ascontiguousarray(b_dt.reshape(16, 128).T),
        "dprm": np.ascontiguousarray(D_param.reshape(16, 128).T),
        "alr": alr,
        "sel": sel_h, "selt": selt_h, "selb": selbc,
        "lnw": np.ascontiguousarray(np.tile(ln_w[None, :], (128, 1))),
        "lnb": np.ascontiguousarray(np.tile(ln_b[None, :], (128, 1))),
        "ident": consts["ident_bf"],
    }
    in_maps = []
    for c in range(NCORES):
        m = dict(in_common)
        m["x"] = x_full.reshape(16 * T, D)[c * B_PER_CORE * T:(c + 1) * B_PER_CORE * T]
        in_maps.append(m)
    res = run_bass_kernel_spmd(nc, in_maps, core_ids=list(range(NCORES)))
    out = np.concatenate([res.results[c]["xo"] for c in range(NCORES)], axis=0)
    return out.reshape(16, T, D)


def emit_attn_pal(a, ctx, d, PS, com, xin, xout, pnscr):
    NTOK = B_PER_CORE * T

    if True:
        sb = ctx.sb
        x, xo = xin, xout
        wi_in, wp_in, wo_in, palT_in = d["wi"], d["wp"], d["wo"], d["palT"]
        w1g_in, w1d_in, b1_in, w2_in, b2_in = d["w1g"], d["w1d"], d["b1"], d["w2"], d["b2"]
        wc0_in, wc1_in, bc_in, wm_in, bm_in = d["wcc0"], d["wcc1"], d["bc"], d["wm"], d["bm"]
        cos_in, sin_in = d["cosr"], d["sinr"]
        m0_in, keepm_in = d["m0"], d["keepm"]
        iota256_in, iota16r_in, tcol_in, eqsel_in = d["iota256"], d["iota16r"], d["tcol"], d["eqsel"]
        selg_in, selx_in = d["selg"], d["selx"]
        identb, identf, diagm, junk = com["identb"], com["identf"], com["diagm"], com["junk"]
        wi = sb("wit", [128, 8, D], BF16)
        wp = sb("wpt", [128, 8, D], BF16)
        wot = sb("wot", [128, 8, D], BF16)
        palt = sb("paltt", [128, 2, D], BF16)
        w1g = sb("w1gt", [16, 64], BF16)
        w1d = sb("w1dt", [1, 64], BF16)
        b1 = sb("b1t", [64, 1], F32)
        w2 = sb("w2t", [64, 64], BF16)
        b2 = sb("b2t", [64, 1], F32)
        wc0 = sb("wc0t", [64, 1], BF16)
        wc1 = sb("wc1t", [64, 1], BF16)
        bct = sb("bct", [1, 2], F32)
        wm = sb("wmt", [64, 1], BF16)
        bmt = sb("bmt", [1, 1], F32)
        m0 = sb("m0t", [16, T], BF16)
        keepm = sb("keepmt", [128, 32], F32)
        iota256 = sb("iota256t", [128, 256], F32)
        iota16r = sb("iota16rt", [128, 256], F32)
        tcol = sb("tcolt", [128, 1], F32)
        eqsel = sb("eqselt", [128, 16], F32)
        selg = sb("selgt", [128, 16, 128], F32)
        selx = sb("selxt", [128, 8, 16], BF16)
        xtile = sb("xt0", [128, D], F32)
        xln_tm = sb("xln_tm", [128, D], BF16)
        xlnf = sb("xlnf", [128, 8, 128], BF16)
        scratch = {"xsq": sb("xsq", [128, D], BF16), "s1": sb("s1", [128, 1], F32),
                   "s2": sb("s2", [128, 1], F32), "mv": sb("mv", [128, 2], F32),
                   "rstd": sb("rstd", [128, 1], F32)}
        costl = sb("costl", [128, 512], F32)
        sintl = sb("sintl", [128, 512], F32)
        ptm = sb("ptm", [128, D], BF16)
        pnrm = sb("pnrm", [128, D], BF16)
        pf = sb("pf", [128, 8, T], BF16)
        itm = sb("itm", [128, 16, D], BF16)
        iff = sb("iff", [128, 8, 128], BF16)
        sx2 = sb("sx2", [128, T], F32)
        m1 = sb("m1t", [128, 8], F32)
        m2 = sb("m2t", [128, 8], F32)
        iu32 = sb("iu32", [128, 16], U32)
        idxf = sb("idxf", [128, 16], F32)
        tq = sb("tq", [128, 1], F32)
        d0 = sb("d0", [128, 16], F32)
        d0b = sb("d0b", [128, 16], BF16)
        drow = sb("drow", [128, 2048], BF16)
        gidxf = sb("gidxf", [128, 1], F32)
        gidxu = sb("gidxu", [128, 1], U32)
        gtmp = sb("gtmp", [128, 16], F32)
        gath = sb("gath", [128, D], BF16)
        afm = sb("afm", [128, 8, 128], BF16)
        gsb = sb("gsb", [128, 128], BF16)
        relbuf = sb("relbuf", [16, T], BF16)
        h1sb = sb("h1sb", [64, T], BF16)
        h2sb = sb("h2sb", [64, T], BF16)
        mixrow = sx2
        gxrow = sb("gxrow", [1, T], BF16)
        gyrow = sb("gyrow", [1, T], BF16)
        mixT = sb("mixT", [128, 16], F32)
        wsoft = sb("wsoft", [128, 16], F32)
        gxT = sb("gxT", [128, 16], F32)
        gyT = sb("gyT", [128, 16], F32)
        rowmax = sb("rowmax", [128, 1], F32)
        rowsum = sb("rowsum", [128, 1], F32)
        nbias = sb("nbias", [128, 1], F32)
        recl = sb("recl", [128, 1], F32)
        fx = sb("fx", [128, 16], F32)
        fy = sb("fy", [128, 16], F32)
        x0i = sb("x0i", [128, 16], mybir.dt.int16)
        x0f = sb("x0f", [128, 16], F32)
        x1f = sb("x1f", [128, 16], F32)
        wx = sb("wxq", [128, 16], F32)
        y0f = sb("y0f", [128, 16], F32)
        y1f = sb("y1f", [128, 16], F32)
        wy = sb("wyq", [128, 16], F32)
        axw = sb("axw", [128, 256], F32)
        byw = sb("byw", [128, 256], F32)
        ctmp = sb("ctmp", [128, 256], F32)
        coeff = sb("coeff", [128, 256], F32)
        coefb = sb("coefb", [128, 256], BF16)
        cf = sb("cf", [128, 2, 128], BF16)
        vtmb = sb("vtmb", [128, D], BF16)
        vf = sb("vf", [128, 8, 128], BF16)
        PSA = PS["PSA"]
        PSB = PS["PSB"][:, 0:1024]
        PSD = PS["PSD"]

        for c in range(8):
            a.dma("gpq", wi[:, c, :], wi_in[c * 128:(c + 1) * 128, :])
            a.dma("gpq", wp[:, c, :], wp_in[c * 128:(c + 1) * 128, :])
            a.dma("gpq", wot[:, c, :], wo_in[c * 128:(c + 1) * 128, :])
        a.dma("gpq", palt[:, 0, :], palT_in[0:128, :])
        a.dma("gpq", palt[:, 1, :], palT_in[128:256, :])
        for (dst, src) in ((w1g, w1g_in), (w1d, w1d_in), (w2, w2_in),
                           (wc0, wc0_in), (wc1, wc1_in), (wm, wm_in)):
            a.dma("gpq", dst, src)
        for (dst, src) in ((b1, b1_in), (b2, b2_in), (bct, bc_in), (bmt, bm_in),
                           (m0, m0_in), (keepm, keepm_in), (iota256, iota256_in),
                           (iota16r, iota16r_in), (tcol, tcol_in), (eqsel, eqsel_in)):
            a.dma("syq", dst, src)
        a.dma("syq", selg.rearrange("p a b -> p (a b)"), selg_in)
        a.dma("syq", selx.rearrange("p a b -> p (a b)"), selx_in)

        import concourse.bass as _bass
        for b in range(B_PER_CORE):
            base = b * T
            for j in range(T // 128):
                r0 = base + j * 128
                a.dma("syq", xtile, x[r0:r0 + 128, :])
                emit_ln_tile(a, None, xtile, xln_tm, scratch)
                emit_tm_to_fm(a, xln_tm, [xlnf[:, c, :] for c in range(8)], identb,
                              PSB[:, 0:128])
                a.dma("syq", costl, cos_in[j * 128:(j + 1) * 128, :])
                a.dma("syq", sintl, sin_in[j * 128:(j + 1) * 128, :])
                for (wt, is_p) in ((wp, True), (wi, False)):
                    pp = PSA[:, 0:1024]
                    for hf in range(2):
                        for c in range(8):
                            a.mm(pp[:, hf * 512:(hf + 1) * 512], xlnf[:, c, :],
                                 wt[:, c, hf * 512:(hf + 1) * 512],
                                 start=(c == 0), stop=(c == 7))
                    a.tt(sx2[:, 0:512], pp[:, 0:512], costl, ALU.mult)
                    a.tt(sx2[:, 512:1024], pp[:, 512:1024], sintl, ALU.mult)
                    a.tt(ptm[:, 0:512], sx2[:, 0:512], sx2[:, 512:1024], ALU.subtract)
                    a.tt(sx2[:, 0:512], pp[:, 512:1024], costl, ALU.mult)
                    a.tt(sx2[:, 512:1024], pp[:, 0:512], sintl, ALU.mult)
                    a.tt(ptm[:, 512:1024], sx2[:, 0:512], sx2[:, 512:1024], ALU.add)
                    a.act(scratch["xsq"], ptm, AF.Square, accum_out=scratch["s2"])
                    a.act(scratch["s1"], scratch["s2"], AF.Ln)
                    a.act(scratch["rstd"], scratch["s1"], AF.Exp, scale=-0.5)
                    a.ts(scratch["rstd"], scratch["rstd"], 1e12, ALU.min)
                    a.ts(pnrm, ptm, scratch["rstd"][:, 0:1], ALU.mult)
                    if is_p:
                        a.dma("gpq", pnscr[b * 4096 + j * 128:b * 4096 + (j + 1) * 128, :], pnrm)
                        emit_tm_to_fm(a, ptm, [pf[:, c, j * 128:(j + 1) * 128] for c in range(8)],
                                      identb, PSB[:, 0:128])
                    else:
                        a.dma("gpq", pnscr[b * 4096 + 2048 + j * 128:b * 4096 + 2048 + (j + 1) * 128, :], pnrm)
                        a.copy(itm[:, j, :], ptm)
            for qt in range(T // 128):
                L = (qt + 1) * 128
                emit_tm_to_fm(a, itm[:, qt, :], [iff[:, c, :] for c in range(8)], identb,
                              PSB[:, 0:128])
                for kb in range((L + 511) // 512):
                    n = min(512, L - kb * 512)
                    for c in range(8):
                        a.mm(PSA[:, kb * 512:kb * 512 + n], iff[:, c, :],
                             pf[:, c, kb * 512:kb * 512 + n], start=(c == 0), stop=(c == 7))
                S = PSA[:, 0:L]
                a.tt(PSA[:, L - 128:L], PSA[:, L - 128:L], diagm, ALU.add)
                a.op("vector", lambda e, _m=m1, _s=S: e.max(out=_m, in_=_s), reads=[S], writes=[m1])
                a.op("vector", lambda e, _o=sx2, _m=m1, _s=S, _L=L:
                     e.match_replace(out=_o[:, 0:_L], in_to_replace=_m, in_values=_s, imm_value=NEG),
                     reads=[S, m1], writes=[sx2])
                a.op("vector", lambda e, _m=m2, _s=sx2, _L=L: e.max(out=_m, in_=_s[:, 0:_L]),
                     reads=[sx2], writes=[m2])
                a.op("vector", lambda e, _o=iu32, _m=m1, _s=S: e.max_index(out=_o[:, 0:8], in_max=_m, in_values=_s),
                     reads=[S, m1], writes=[iu32])
                a.op("vector", lambda e, _o=iu32, _m=m2, _s=sx2, _L=L: e.max_index(out=_o[:, 8:16], in_max=_m, in_values=_s[:, 0:_L]),
                     reads=[sx2, m2], writes=[iu32])
                a.copy(idxf, iu32)
                a.ts(tq, tcol, float(qt * 128), ALU.add)
                a.ts(d0, idxf, -1.0, ALU.mult, tq[:, 0:1], ALU.add)
                a.ts(d0, d0, 0.0, ALU.max, 1.0 / 2048.0, ALU.mult)
                a.memset(d0[:, 15:16], 0.0)
                a.copy(d0b, d0)
                a.op("gpsimd", lambda e, _o=drow, _i=d0b:
                     e.dma_start(out=_o[0:1, 0:2048].rearrange("o (t k) -> o t k", k=16), in_=_i,
                                 allow_slow_non_contiguous=True),
                     reads=[d0b], writes=[drow], queue="gpq")
                a.ts(idxf[:, 15:16], tq, float(2048), ALU.add)
                a.ts(idxf, idxf, float(b * 4096), ALU.add)
                for g in range(16):
                    a.mm(PSD[:, 0:16], selg[:, g, :], idxf, start=True, stop=True)
                    a.tt(gtmp, PSD[:, 0:16], eqsel, ALU.mult)
                    a.reduce(gidxf, gtmp, ALU.add)
                    a.copy(gidxu, gidxf)
                    a.op("gpsimd", lambda e, _o=gath, _i=pnscr, _x=gidxu:
                         e.indirect_dma_start(out=_o, out_offset=None, in_=_i,
                                              in_offset=_bass.IndirectOffsetOnAxis(ap=_x, axis=0)),
                         reads=[pnscr, gidxu], writes=[gath], queue="gpq")
                    for c in range(8):
                        a.trans(PSB[:, c * 128:(c + 1) * 128], gath[:, c * 128:(c + 1) * 128], identb)
                    a.copy(afm.rearrange("p a b -> p (a b)"), PSB[:, 0:1024])
                    for c in range(8):
                        a.mm(PSD[:, 128:256], afm[:, c, :], afm[:, c, :],
                             start=(c == 0), stop=(c == 7))
                    a.copy(gsb, PSD[:, 128:256])
                    for i in range(8):
                        a.mm(PSD[0:16, 256:272].rearrange("p a -> p a") if False else PSD[0:16, 256 + i * 16:256 + (i + 1) * 16],
                             selx[:, i, :], gsb[:, i * 16:(i + 1) * 16], start=True, stop=True)
                    a.ts(relbuf[0:16, g * 128:(g + 1) * 128], PSD[0:16, 256:384],
                         1.0, ALU.min, -1.0, ALU.max)
                if qt == 0:
                    a.tt(relbuf[0:16, 0:2048], relbuf[0:16, 0:2048], m0, ALU.mult)
                for sl in range(4):
                    ssl = slice(sl * 512, (sl + 1) * 512)
                    a.mm(PSA[0:64, ssl], w1g, relbuf[0:16, ssl], start=True, stop=False)
                    a.mm(PSA[0:64, ssl], w1d, drow[0:1, ssl], start=False, stop=True)
                a.act(h1sb, PSA[0:64, 0:2048], AF.Gelu, bias=b1[:, 0:1], scale=1.0)
                for sl in range(4):
                    ssl = slice(sl * 512, (sl + 1) * 512)
                    a.mm(PSA[64:128, ssl], w2, h1sb[:, ssl], start=True, stop=True)
                a.act(h2sb, PSA[64:128, 0:2048], AF.Gelu, bias=b2[:, 0:1], scale=1.0)
                for (wv_, bi_, dst, func) in ((wc0, bct[0:1, 0:1], gxrow, AF.Tanh),
                                              (wc1, bct[0:1, 1:2], gyrow, AF.Tanh),
                                              (wm, bmt[0:1, 0:1], mixrow, AF.Identity)):
                    for sl in range(4):
                        ssl = slice(sl * 512, (sl + 1) * 512)
                        a.mm(PSA[0:1, ssl], wv_, h2sb[:, ssl], start=True, stop=True)
                    a.act(dst[0:1, 0:2048], PSA[0:1, 0:2048], func, bias=bi_, scale=1.0)
                for (src, dstT, idn) in ((mixrow, mixT, identf), (gxrow, gxT, identb), (gyrow, gyT, identb)):
                    for k in range(16):
                        if idn is identf:
                            a.trans(PSD[:, 384 + k:384 + k + 1],
                                    src[0:1, 0:2048].rearrange("o (t k) -> o k t", k=16)[:, k, :], idn[0:1, 0:1])
                        else:
                            a.trans(PSB[:, 512 + 2 * k:512 + 2 * k + 1],
                                    src[0:1, 0:2048].rearrange("o (t k) -> o k t", k=16)[:, k, :], idn[0:1, 0:1])
                    if idn is identf:
                        a.copy(dstT, PSD[:, 384:400])
                    else:
                        a.copy(dstT, PSB[:, 512:544].rearrange("p (k two) -> p k two", two=2)[:, :, 0])
                kslice = slice(16, 32) if qt == 0 else slice(0, 16)
                a.tt(mixT, mixT, keepm[:, kslice], ALU.add)
                a.reduce(rowmax, mixT, ALU.max)
                a.ts(nbias, rowmax, -1.0, ALU.mult)
                a.act(wsoft, mixT, AF.Exp, bias=nbias[:, 0:1], scale=1.0, accum_out=rowsum)
                a.op("vector", lambda e, _o=recl, _i=rowsum: e.reciprocal(out=_o, in_=_i),
                     reads=[rowsum], writes=[recl])
                a.ts(wsoft, wsoft, recl[:, 0:1], ALU.mult)
                for (gT, f_, i0_, f0_, f1_, w_) in ((gxT, fx, x0i, x0f, x1f, wx),
                                                    (gyT, fy, x0i, y0f, y1f, wy)):
                    a.ts(f_, gT, 1.0, ALU.add, 7.5, ALU.mult)
                    a.ts(i0_, f_, 0.5, ALU.subtract)
                    a.copy(f0_, i0_)
                    a.tt(w_, f_, f0_, ALU.subtract)
                    a.ts(f1_, f0_, 1.0, ALU.add, 15.0, ALU.min)
                a.op("vector", lambda e, _o=axw, _i=iota16r, _x=x0f:
                     e.tensor_tensor(out=_o.rearrange("p (k w) -> p k w", w=16),
                                     in0=_i.rearrange("p (k w) -> p k w", w=16),
                                     in1=_x.unsqueeze(2).broadcast_to([128, 16, 16]), op=ALU.is_equal),
                     reads=[iota16r, x0f], writes=[axw])
                a.op("vector", lambda e, _o=ctmp, _i=iota16r, _x=x1f:
                     e.tensor_tensor(out=_o.rearrange("p (k w) -> p k w", w=16),
                                     in0=_i.rearrange("p (k w) -> p k w", w=16),
                                     in1=_x.unsqueeze(2).broadcast_to([128, 16, 16]), op=ALU.is_equal),
                     reads=[iota16r, x1f], writes=[ctmp])
                a.tt(ctmp, ctmp, axw, ALU.subtract)
                a.op("vector", lambda e, _o=ctmp, _i=ctmp, _x=wx:
                     e.tensor_tensor(out=_o.rearrange("p (k w) -> p k w", w=16),
                                     in0=_i.rearrange("p (k w) -> p k w", w=16),
                                     in1=_x.unsqueeze(2).broadcast_to([128, 16, 16]), op=ALU.mult),
                     reads=[ctmp, wx], writes=[ctmp])
                a.tt(axw, axw, ctmp, ALU.add)
                a.op("vector", lambda e, _o=byw, _i=iota16r, _x=y0f:
                     e.tensor_tensor(out=_o.rearrange("p (k w) -> p k w", w=16),
                                     in0=_i.rearrange("p (k w) -> p k w", w=16),
                                     in1=_x.unsqueeze(2).broadcast_to([128, 16, 16]), op=ALU.is_equal),
                     reads=[iota16r, y0f], writes=[byw])
                a.op("vector", lambda e, _o=ctmp, _i=iota16r, _x=y1f:
                     e.tensor_tensor(out=_o.rearrange("p (k w) -> p k w", w=16),
                                     in0=_i.rearrange("p (k w) -> p k w", w=16),
                                     in1=_x.unsqueeze(2).broadcast_to([128, 16, 16]), op=ALU.is_equal),
                     reads=[iota16r, y1f], writes=[ctmp])
                a.tt(ctmp, ctmp, byw, ALU.subtract)
                a.op("vector", lambda e, _o=ctmp, _i=ctmp, _x=wy:
                     e.tensor_tensor(out=_o.rearrange("p (k w) -> p k w", w=16),
                                     in0=_i.rearrange("p (k w) -> p k w", w=16),
                                     in1=_x.unsqueeze(2).broadcast_to([128, 16, 16]), op=ALU.mult),
                     reads=[ctmp, wy], writes=[ctmp])
                a.tt(byw, byw, ctmp, ALU.add)
                a.op("vector", lambda e, _o=byw, _i=byw, _x=wsoft:
                     e.tensor_tensor(out=_o.rearrange("p (k w) -> p k w", w=16),
                                     in0=_i.rearrange("p (k w) -> p k w", w=16),
                                     in1=_x.unsqueeze(2).broadcast_to([128, 16, 16]), op=ALU.mult),
                     reads=[byw, wsoft], writes=[byw])
                first = True
                for k in range(16):
                    eng = "vector"
                    a.op(eng, lambda e, _o=ctmp, _b=byw, _a=axw, _k=k:
                         e.tensor_tensor(out=_o, in0=_b[:, _k * 16:(_k + 1) * 16].unsqueeze(2).broadcast_to([128, 16, 16]),
                                         in1=_a[:, _k * 16:(_k + 1) * 16].unsqueeze(1).broadcast_to([128, 16, 16]),
                                         op=ALU.mult),
                         reads=[byw, axw], writes=[ctmp])
                    if first:
                        a.copy(coeff, ctmp)
                        first = False
                    else:
                        a.tt(coeff, coeff, ctmp, ALU.add)
                a.copy(coefb, coeff)
                for c2 in range(2):
                    a.trans(PSB[:, 0:128], coefb[:, c2 * 128:(c2 + 1) * 128], identb)
                    a.copy(cf[:, c2, :], PSB[:, 0:128])
                pv = PSA[:, 1024:2048]
                for hf in range(2):
                    for c2 in range(2):
                        a.mm(pv[:, hf * 512:(hf + 1) * 512], cf[:, c2, :],
                             palt[:, c2, hf * 512:(hf + 1) * 512], start=(c2 == 0), stop=(c2 == 1))
                a.copy(vtmb, pv)
                emit_tm_to_fm(a, vtmb, [vf[:, c, :] for c in range(8)], identb, PSB[:, 0:128])
                pd = PSA[:, 0:1024]
                for c in range(8):
                    for hf in range(2):
                        a.mm(pd[:, hf * 512:(hf + 1) * 512], vf[:, c, :],
                             wot[:, c, hf * 512:(hf + 1) * 512], start=(c == 0), stop=(c == 7))
                r0 = base + qt * 128
                a.dma("scq", xtile, x[r0:r0 + 128, :])
                a.tt(xtile, xtile, pd, ALU.add)
                a.dma("gpq", xo[r0:r0 + 128, :], xtile)


def host_pal_consts():
    c = {}
    half = 512
    inv_freq = 1.0 / 10000.0 ** (np.arange(half, dtype=np.float32) / half)
    ang = np.arange(T, dtype=np.float32)[:, None] * inv_freq[None, :]
    c["cosr"] = np.cos(ang).astype(np.float32)
    c["sinr"] = np.sin(ang).astype(np.float32)
    m0 = np.zeros((16, T), np.float32)
    for t in range(128):
        for k in range(16):
            for j in range(16):
                ok = (k <= t and k < 15) and (j <= t or j == 15)
                m0[j, t * 16 + k] = 1.0 if ok else 0.0
    c["m0"] = m0.astype(ml_dtypes.bfloat16)
    km = np.zeros((128, 32), np.float32)
    km[:, 15] = NEG
    for p in range(128):
        for k in range(16):
            km[p, 16 + k] = NEG if (k > p or k == 15) else 0.0
    c["keepm"] = km
    c["iota256"] = np.tile(np.arange(256, dtype=np.float32)[None, :], (128, 1))
    c["iota16r"] = np.tile(np.tile(np.arange(16, dtype=np.float32), 16)[None, :], (128, 1))
    c["tcol"] = np.arange(128, dtype=np.float32)[:, None].copy()
    eq = np.zeros((128, 16), np.float32)
    for p in range(128):
        eq[p, p % 16] = 1.0
    c["eqsel"] = eq
    selg = np.zeros((16, 128, 128), np.float32)
    for g in range(16):
        for p in range(128):
            selg[g, 8 * g + p // 16, p] = 1.0
    c["selg"] = np.ascontiguousarray(np.transpose(selg, (1, 0, 2)).reshape(128, 16 * 128))
    selx = np.zeros((8, 128, 16), np.float32)
    for i in range(8):
        for j in range(16):
            selx[i, 16 * i + j, j] = 1.0
    c["selx"] = np.ascontiguousarray(np.transpose(selx, (1, 0, 2)).reshape(128, 8 * 16)).astype(ml_dtypes.bfloat16)
    return c


def run_attn_pal(x_full, Wi, Wp, palette, W1, b1, W2, b2, Wc, bc, Wm, bm, Wo):
    consts = host_constants()
    pc = host_pal_consts()
    nc = build_attn_pal()
    in_common = {
        "wi": np.ascontiguousarray(Wi), "wp": np.ascontiguousarray(Wp),
        "wo": np.ascontiguousarray(Wo[0]),
        "palT": np.ascontiguousarray(palette.reshape(D, 256).T),
        "w1g": np.ascontiguousarray(W1[0:16, :]), "w1d": np.ascontiguousarray(W1[16:17, :]),
        "b1": np.ascontiguousarray(b1[:, None]), "w2": np.ascontiguousarray(W2),
        "b2": np.ascontiguousarray(b2[:, None]),
        "wcc0": np.ascontiguousarray(Wc[:, 0:1]), "wcc1": np.ascontiguousarray(Wc[:, 1:2]),
        "bc": np.ascontiguousarray(bc[None, :]), "wm": np.ascontiguousarray(Wm),
        "bm": np.ascontiguousarray(bm[None, :]),
        "ident": consts["ident_bf"], "identf": consts["ident"], "diagmask": consts["diagmask"],
    }
    in_common.update(pc)
    in_maps = []
    for c in range(NCORES):
        m = dict(in_common)
        m["x"] = x_full.reshape(16 * T, D)[c * B_PER_CORE * T:(c + 1) * B_PER_CORE * T]
        in_maps.append(m)
    res = run_bass_kernel_spmd(nc, in_maps, core_ids=list(range(NCORES)))
    out = np.concatenate([res.results[c]["xo"] for c in range(NCORES)], axis=0)
    return out.reshape(16, T, D)



def build_l0_ffn_dbg():
    hidden = 3072
    HC = hidden // 128
    NTOK = B_PER_CORE * T
    NT = NTOK // 128
    NT512 = NTOK // 512
    nc = bass.Bass("TRN2", target_bir_lowering=False)
    x = nc.dram_tensor("x", [NTOK, D], F32, kind="ExternalInput").ap()
    wgT = nc.dram_tensor("wgT", [D, hidden], I32, kind="ExternalInput").ap()
    wuT = nc.dram_tensor("wuT", [D, hidden], I32, kind="ExternalInput").ap()
    wdT = nc.dram_tensor("wdT", [hidden, D], I32, kind="ExternalInput").ap()
    ident_in = nc.dram_tensor("ident", [128, 128], BF16, kind="ExternalInput").ap()
    xo = nc.dram_tensor("xo", [NTOK, D], F32, kind="ExternalOutput").ap()
    oxln = nc.dram_tensor("oxln", [NTOK, D], F32, kind="ExternalOutput").ap()
    oh = nc.dram_tensor("oh", [24, 128, NTOK], F32, kind="ExternalOutput").ap()
    opd = nc.dram_tensor("opd", [NTOK, D], F32, kind="ExternalOutput").ap()

    from contextlib import ExitStack
    with ExitStack() as stack:
        ctx = Ctx(nc, stack)
        sb, ps = ctx.sb, ctx.ps
        junk = sb("junk", [128, 256], F32)
        identb = sb("identb", [128, 128], BF16)
        xtile = [sb(f"xt{i}", [128, D], F32) for i in range(2)]
        xln_tm = sb("xln_tm", [128, D], BF16)
        xlnf = sb("xlnf", [128, 8, T // 2], BF16)
        scratch = {"xsq": sb("xsq", [128, D], BF16), "s1": sb("s1", [128, 1], F32),
                   "s2": sb("s2", [128, 1], F32), "mv": sb("mv", [128, 2], F32),
                   "rstd": sb("rstd", [128, 1], F32)}
        wgt = sb("wgt", [128, 8, hidden], BF16)
        wut = sb("wut", [128, 8, hidden], BF16)
        wdt = sb("wdtr", [128, HC, D], BF16)
        hbuf = sb("hbuf", [128, HC, 512], BF16)
        sg = [sb(f"sg{i}", [128, 512], BF16) for i in range(2)]
        pg = [ps(f"pg{i}", [128, 512], F32) for i in range(2)]
        pu = [ps(f"pu{i}", [128, 512], F32) for i in range(2)]
        pd = ps("pd", [128, D], F32)
        pxp = ps("pxp", [128, 128], BF16)
        a = A(nc)

        a.dma("syq", identb, ident_in)
        for c in range(8):
            a.dma("gpq", wgt[:, c, :], wgT[c * 128:(c + 1) * 128, :])
            a.dma("gpq", wut[:, c, :], wuT[c * 128:(c + 1) * 128, :])
        for hc in range(HC):
            a.dma("gpq", wdt[:, hc, :], wdT[hc * 128:(hc + 1) * 128, :])
        a.warmup(junk)

        NTSEQ = T // 256
        NT512SEQ = T // 1024
        for b in range(2 * B_PER_CORE):
            base = b * (T // 2)
            for j in range(NTSEQ):
                xt = xtile[j % 2]
                r0 = base + j * 128
                a.dma("syq", xt, x[r0:r0 + 128, :])
                emit_ln_tile(a, None, xt, xln_tm, scratch, extra_scale=1.0 / 127.0)
                a.dma("gpq", oxln[r0:r0 + 128, :], xln_tm)
                emit_tm_to_fm(a, xln_tm, [xlnf[:, c, j * 128:(j + 1) * 128] for c in range(8)],
                              identb, pxp)
            for tt in range(NT512SEQ):
                tsl = slice(tt * 512, (tt + 1) * 512)
                for hc in range(HC):
                    g = pg[hc % 2]
                    u = pu[hc % 2]
                    sgt = sg[hc % 2]
                    for c in range(8):
                        a.mm(g, wgt[:, c, hc * 128:(hc + 1) * 128], xlnf[:, c, tsl],
                             start=(c == 0), stop=(c == 7))
                    for c in range(8):
                        a.mm(u, wut[:, c, hc * 128:(hc + 1) * 128], xlnf[:, c, tsl],
                             start=(c == 0), stop=(c == 7))
                    a.act(sgt, g, AF.Silu)
                    a.stt(hbuf[:, hc, :], sgt, 1.0 / 127.0, u, ALU.mult, ALU.mult)
                    a.dma("gpq", oh[hc, :, base + tt * 512:base + tt * 512 + 512], hbuf[:, hc, :], tag="dbg")
                for st4 in range(4):
                    j = tt * 4 + st4
                    for hc in range(HC):
                        for half in range(2):
                            a.mm(pd[:, half * 512:(half + 1) * 512],
                                 hbuf[:, hc, st4 * 128:(st4 + 1) * 128],
                                 wdt[:, hc, half * 512:(half + 1) * 512],
                                 start=(hc == 0), stop=(hc == HC - 1))
                    xt = xtile[j % 2]
                    r0 = base + j * 128
                    a.dma("syq", xt, x[r0:r0 + 128, :])
                    a.copy(mvd := xt, xt) if False else None
                    a.dma("vq", opd[r0:r0 + 128, :], xt, tag="preres") if False else None
                    a.tt(xt, xt, pd, ALU.add)
                    a.dma("gpq", xo[r0:r0 + 128, :], xt)

        a.barrier_outputs([xo, oxln, oh])
        a.emit(stack)
    return nc




def build_fused():
    NTOK = B_PER_CORE * T
    nc = bass.Bass("TRN2", target_bir_lowering=False)
    dd = {}
    def D_(pfx, name, shape, dt_):
        dd[(pfx, name)] = nc.dram_tensor(pfx + name, shape, dt_, kind="ExternalInput").ap()
    x_in = nc.dram_tensor("x", [NTOK, D], F32, kind="ExternalInput").ap()
    xo = nc.dram_tensor("xo", [NTOK, D], F32, kind="ExternalOutput").ap()
    xs = [nc.dram_tensor(f"xs{i}", [NTOK, D], F32, kind="Internal").ap() for i in range(2)]
    pnscr = nc.dram_tensor("pnscr", [B_PER_CORE * 4096, D], BF16, kind="Internal").ap()
    D_("c_", "ident", [128, 128], BF16)
    D_("c_", "identf", [128, 128], F32)
    D_("c_", "diagmask", [128, 128], F32)
    for n, sh, dt_ in [("wqT", [D, D], F32), ("wqTs", [D, D], F32), ("wkT", [D, 192], F32),
                       ("wvT", [D, 192], F32), ("dks", [192, D], F32), ("dkss", [192, D], F32),
                       ("dkm", [192, D], BF16), ("dkms", [192, D], BF16), ("dv", [192, D], F32),
                       ("woT", [D, D], F32), ("c1", [64, T], BF16), ("c2", [64, T], BF16)]:
        D_("aa_", n, sh, dt_)
    for n, sh in [("wgT", [D, 3072]), ("wuT", [D, 3072]), ("wdT", [3072, D])]:
        D_("f0_", n, sh, I32)
    for n in ("wqT", "wkT", "wvT", "woT"):
        D_("ab_", n, [D, D], F32)
    D_("ab_", "alibik", [16, T], BF16)
    for n, sh, dt_ in [("winT", [D, 4096], F32), ("wxT", [2048, 96], F32), ("wdtT", [64, 2048], F32),
                       ("woutT", [2048, D], F32), ("convr", [128, 64], F32), ("convb", [128, 16], F32),
                       ("bdt", [128, 16], F32), ("dprm", [128, 16], F32), ("alr", [128, 256], F32),
                       ("sel", [128, 2048], BF16), ("selt", [128, 2048], BF16), ("selb", [128, 256], BF16),
                       ("lnw", [128, D], F32), ("lnb", [128, D], F32)]:
        D_("ma_", n, sh, dt_)
    for n, sh in [("wg", [2816, D]), ("wu", [2816, D]), ("wd", [D, 2816])]:
        D_("f1_", n, sh, F32)
    for n, sh, dt_ in [("wi", [D, D], F32), ("wp", [D, D], F32), ("wo", [D, D], F32),
                       ("palT", [256, D], F32), ("w1g", [16, 64], F32), ("w1d", [1, 64], F32),
                       ("b1", [64, 1], F32), ("w2", [64, 64], F32), ("b2", [64, 1], F32),
                       ("wcc0", [64, 1], F32), ("wcc1", [64, 1], F32), ("bc", [1, 2], F32),
                       ("wm", [64, 1], F32), ("bm", [1, 1], F32), ("cosr", [T, 512], F32),
                       ("sinr", [T, 512], F32), ("m0", [16, T], BF16), ("keepm", [128, 32], F32),
                       ("iota256", [128, 256], F32), ("iota16r", [128, 256], F32),
                       ("tcol", [128, 1], F32), ("eqsel", [128, 16], F32),
                       ("selg", [128, 2048], F32), ("selx", [128, 128], BF16)]:
        D_("pp_", n, sh, dt_)

    def sub(pfx):
        return {n: ap for (p, n), ap in dd.items() if p == pfx}

    from contextlib import ExitStack
    with ExitStack() as stack:
        ctx0 = Ctx(nc, stack)
        junk = ctx0.sb("junk", [128, 256], F32)
        identb = ctx0.sb("identb", [128, 128], BF16)
        identf = ctx0.sb("identfp", [128, 128], F32)
        diagm = ctx0.sb("diagmp", [128, 128], F32)
        com = {"junk": junk, "identb": identb, "identf": identf, "diagm": diagm}
        PS = {"PSA": ctx0.ps("PSA", [128, 2048], F32),
              "PSB": ctx0.ps("PSB", [128, 2048], BF16),
              "PSD": ctx0.ps("PSD", [128, 512], F32),
              "PSE": ctx0.ps("PSE", [128, 512], F32)}
        a = A(nc)
        a.dma("syq", identb, dd[("c_", "ident")])
        a.dma("syq", identf, dd[("c_", "identf")])
        a.dma("syq", diagm, dd[("c_", "diagmask")])
        a.memset(junk, 1.0)
        a.warmup(junk)
        for f in (AF.Exp, AF.Abs, AF.Square, AF.Gelu, AF.Tanh, AF.Sign, AF.Sqrt,
                  AF.Silu, AF.Relu, AF.Ln):
            a.act(junk, junk, f)

        chain = [
            (lambda a_, c_, xi, xo_: emit_attn(a_, c_, sub("aa_"), PS, com, xi, xo_, "a")),
            (lambda a_, c_, xi, xo_: emit_l0_ffn(a_, c_, sub("f0_"), PS, com, xi, xo_)),
            (lambda a_, c_, xi, xo_: emit_attn(a_, c_, sub("ab_"), PS, com, xi, xo_, "b")),
            (lambda a_, c_, xi, xo_: emit_l0_ffn(a_, c_, sub("f0_"), PS, com, xi, xo_)),
            (lambda a_, c_, xi, xo_: emit_mamba(a_, c_, sub("ma_"), PS, com, xi, xo_)),
            (lambda a_, c_, xi, xo_: emit_l1_ffn(a_, c_, sub("f1_"), PS, com, xi, xo_)),
            (lambda a_, c_, xi, xo_: emit_attn_pal(a_, c_, sub("pp_"), PS, com, xi, xo_, pnscr)),
            (lambda a_, c_, xi, xo_: emit_l1_ffn(a_, c_, sub("f1_"), PS, com, xi, xo_)),
        ]
        for k, em in enumerate(chain):
            xi = x_in if k == 0 else xs[k % 2]
            xo_ = xo if k == len(chain) - 1 else xs[(k + 1) % 2]
            with ExitStack() as substack:
                ctx = Ctx(nc, substack, prefix=f"s{k}_")
                em(a, ctx, xi, xo_)
            a.barrier()
        a.barrier_outputs([xo])
        a.emit(stack)
        nops = len(a.ops)
    return nc, nops


def _host_inputs(inp):
    consts = host_constants()
    m = {"c_ident": consts["ident_bf"], "c_identf": consts["ident"], "c_diagmask": consts["diagmask"]}
    Wq = np.asarray(inp["L0_attn_a_Wq"], np.float32)
    perm = np.arange(D) ^ 1
    dk, dks_sw, mk, mks, dv = host_attn_a_decoders(np.asarray(inp["L0_attn_a_key_decoder"], np.float32),
                                                   np.asarray(inp["L0_attn_a_value_decoder"], np.float32))
    C1, C2 = rope_tables_interleaved()
    m.update({"aa_wqT": np.ascontiguousarray(Wq), "aa_wqTs": np.ascontiguousarray(Wq[:, perm]),
              "aa_wkT": np.ascontiguousarray(np.asarray(inp["L0_attn_a_Wk"], np.float32)),
              "aa_wvT": np.ascontiguousarray(np.asarray(inp["L0_attn_a_Wv"], np.float32)),
              "aa_dks": dk, "aa_dkss": dks_sw, "aa_dkm": mk, "aa_dkms": mks, "aa_dv": dv,
              "aa_woT": np.ascontiguousarray(np.asarray(inp["L0_attn_a_Wproj"], np.float32)),
              "aa_c1": C1.astype(ml_dtypes.bfloat16), "aa_c2": C2.astype(ml_dtypes.bfloat16)})
    m.update({"f0_wgT": np.ascontiguousarray(np.asarray(inp["L0_ffn_w_gate"], np.int32).T),
              "f0_wuT": np.ascontiguousarray(np.asarray(inp["L0_ffn_w_up"], np.int32).T),
              "f0_wdT": np.ascontiguousarray(np.asarray(inp["L0_ffn_w_down"], np.int32).T)})
    m.update({"ab_wqT": np.ascontiguousarray(np.asarray(inp["L0_attn_b_Wq"], np.float32).T),
              "ab_wkT": np.ascontiguousarray(np.asarray(inp["L0_attn_b_Wk"], np.float32).T),
              "ab_wvT": np.ascontiguousarray(np.asarray(inp["L0_attn_b_Wv"], np.float32).T),
              "ab_woT": np.ascontiguousarray(np.asarray(inp["L0_attn_b_Wo"], np.float32).T),
              "ab_alibik": host_alibi()})
    sel_h, selt_h, selbc = host_mamba_consts()
    A_log = np.asarray(inp["L1_attn_a_A_log"], np.float32)
    alr = np.zeros((128, 256), np.float32)
    p_ = np.arange(128)
    for ct in range(256):
        alr[p_, ct] = A_log[8 * ct + (p_ % 8), p_ // 8]
    conv_w = np.asarray(inp["L1_attn_a_conv_w"], np.float32)
    m.update({"ma_winT": np.ascontiguousarray(np.asarray(inp["L1_attn_a_W_in"], np.float32).T),
              "ma_wxT": np.ascontiguousarray(np.asarray(inp["L1_attn_a_W_x"], np.float32).T),
              "ma_wdtT": np.ascontiguousarray(np.asarray(inp["L1_attn_a_W_dt"], np.float32).T),
              "ma_woutT": np.ascontiguousarray(np.asarray(inp["L1_attn_a_W_out"], np.float32).T),
              "ma_convr": np.ascontiguousarray(conv_w[:, 0, :].reshape(16, 128, 4).transpose(1, 0, 2).reshape(128, 64)),
              "ma_convb": np.ascontiguousarray(np.asarray(inp["L1_attn_a_conv_b"], np.float32).reshape(16, 128).T),
              "ma_bdt": np.ascontiguousarray(np.asarray(inp["L1_attn_a_b_dt"], np.float32).reshape(16, 128).T),
              "ma_dprm": np.ascontiguousarray(np.asarray(inp["L1_attn_a_D_param"], np.float32).reshape(16, 128).T),
              "ma_alr": alr, "ma_sel": sel_h, "ma_selt": selt_h, "ma_selb": selbc,
              "ma_lnw": np.ascontiguousarray(np.tile(np.asarray(inp["L1_attn_a_ln_w"], np.float32)[None, :], (128, 1))),
              "ma_lnb": np.ascontiguousarray(np.tile(np.asarray(inp["L1_attn_a_ln_b"], np.float32)[None, :], (128, 1)))})
    m.update({"f1_wg": np.ascontiguousarray(np.asarray(inp["L1_ffn_w_gate"], np.float32)),
              "f1_wu": np.ascontiguousarray(np.asarray(inp["L1_ffn_w_up"], np.float32)),
              "f1_wd": np.ascontiguousarray(np.asarray(inp["L1_ffn_w_down"], np.float32))})
    pc = host_pal_consts()
    m.update({"pp_" + k: v for k, v in pc.items()})
    m.update({"pp_wi": np.ascontiguousarray(np.asarray(inp["L1_attn_b_Wi"], np.float32)),
              "pp_wp": np.ascontiguousarray(np.asarray(inp["L1_attn_b_Wp"], np.float32)),
              "pp_wo": np.ascontiguousarray(np.asarray(inp["L1_attn_b_Wo"], np.float32)[0]),
              "pp_palT": np.ascontiguousarray(np.asarray(inp["L1_attn_b_palette"], np.float32).reshape(D, 256).T),
              "pp_w1g": np.ascontiguousarray(np.asarray(inp["L1_attn_b_W1"], np.float32)[0:16, :]),
              "pp_w1d": np.ascontiguousarray(np.asarray(inp["L1_attn_b_W1"], np.float32)[16:17, :]),
              "pp_b1": np.ascontiguousarray(np.asarray(inp["L1_attn_b_b1"], np.float32)[:, None]),
              "pp_w2": np.ascontiguousarray(np.asarray(inp["L1_attn_b_W2"], np.float32)),
              "pp_b2": np.ascontiguousarray(np.asarray(inp["L1_attn_b_b2"], np.float32)[:, None]),
              "pp_wcc0": np.ascontiguousarray(np.asarray(inp["L1_attn_b_Wc"], np.float32)[:, 0:1]),
              "pp_wcc1": np.ascontiguousarray(np.asarray(inp["L1_attn_b_Wc"], np.float32)[:, 1:2]),
              "pp_bc": np.ascontiguousarray(np.asarray(inp["L1_attn_b_bc"], np.float32)[None, :]),
              "pp_wm": np.ascontiguousarray(np.asarray(inp["L1_attn_b_Wm"], np.float32)),
              "pp_bm": np.ascontiguousarray(np.asarray(inp["L1_attn_b_bm"], np.float32)[None, :])})
    return m


def kernel(**inp):
    x = np.ascontiguousarray(np.asarray(inp["L0_attn_a_x"], dtype=np.float32)).reshape(16 * T, D)
    common = _host_inputs(inp)
    nc, nops = build_fused()
    in_maps = []
    for c in range(NCORES):
        m = dict(common)
        m["x"] = x[c * B_PER_CORE * T:(c + 1) * B_PER_CORE * T]
        in_maps.append(m)
    res = run_bass_kernel_spmd(nc, in_maps, core_ids=list(range(NCORES)))
    out = np.concatenate([res.results[c]["xo"] for c in range(NCORES)], axis=0)
    return out.reshape(16, T, D).astype(np.float32)


# version 06
# speedup vs baseline: 1.1916x; 1.0089x over previous
import math
import numpy as np
import ml_dtypes
import concourse.bass as bass
import concourse.mybir as mybir
from concourse.bass_utils import run_bass_kernel_spmd

F32 = mybir.dt.float32
BF16 = mybir.dt.bfloat16
I32 = mybir.dt.int32
U32 = mybir.dt.uint32
U8 = mybir.dt.uint8
AF = mybir.ActivationFunctionType
ALU = mybir.AluOpType
AX = mybir.AxisListType

NEG = -3.0e38

ENGINES = ("sync", "scalar", "vector", "tensor", "gpsimd")
DMAQ = {"syq": "sync", "scq": "scalar", "gpq": "gpsimd", "vq": "vector"}


def _regions(ap_or_list):
    out = []
    if ap_or_list is None:
        return out
    items = ap_or_list if isinstance(ap_or_list, (list, tuple)) else [ap_or_list]
    for ap in items:
        if ap is None:
            continue
        t = ap.tensor
        name = t.name
        esz = mybir.dt.size(ap.dtype)
        lo = ap.offset
        span = 1
        try:
            for st, cnt in ap.ap:
                span += abs(int(st)) * (int(cnt) - 1)
        except Exception:
            span = t.size if hasattr(t, "size") else 1 << 40
        out.append((name, lo * esz, (lo + span) * esz))
    return out


class Op:
    __slots__ = ("idx", "eng", "queue", "fn", "reads", "writes", "seq", "inc_needed",
                 "waits", "tag", "deps", "pre_drain", "dma_sid", "dma_use")

    def __init__(self, idx, eng, queue, fn, reads, writes, tag=""):
        self.idx = idx
        self.eng = eng
        self.queue = queue
        self.fn = fn
        self.reads = reads
        self.writes = writes
        self.seq = None
        self.inc_needed = False
        self.pre_drain = False
        self.waits = []
        self.tag = tag


class A:

    SERIALIZE = False

    def __init__(self, nc):
        self.nc = nc
        self.ops = []
        self.access = {}
        self.nwaits = 0

    def op(self, eng, fn, reads=None, writes=None, queue=None, tag=""):
        o = Op(len(self.ops), eng, queue, fn, _regions(reads), _regions(writes), tag)
        self.ops.append(o)
        deps = set()
        for (name, lo, hi) in o.reads:
            for (l2, h2, prev, w) in self.access.get(name, ()):
                if w and l2 < hi and lo < h2:
                    deps.add(prev)
        for (name, lo, hi) in o.writes:
            for (l2, h2, prev, w) in self.access.get(name, ()):
                if l2 < hi and lo < h2:
                    deps.add(prev)
        if A.SERIALIZE and self.ops[:-1]:
            deps.add(self.ops[-2])
        o.deps = deps
        for (name, lo, hi) in o.reads:
            self.access.setdefault(name, []).append((lo, hi, o, False))
        for (name, lo, hi) in o.writes:
            lst = self.access.setdefault(name, [])
            keep = [e for e in lst if not (lo <= e[0] and e[1] <= hi)]
            keep.append((lo, hi, o, True))
            self.access[name] = keep
        return o

    def _stream_name(self, o):
        return o.queue if o.queue else o.eng

    NDMASEM = 48

    def schedule(self):
        counters = {}
        dma_n = 0
        self.sem_uses = {}
        for o in self.ops:
            sname = self._stream_name(o)
            counters[sname] = counters.get(sname, 0) + 1
            o.seq = counters[sname]
            if o.queue is not None:
                sid = dma_n % self.NDMASEM
                dma_n += 1
                self.sem_uses[sid] = self.sem_uses.get(sid, 0) + 1
                o.dma_sid = sid
                o.dma_use = self.sem_uses[sid]
        hi_water = {}
        for o in self.ops:
            cons = self._stream_name(o)
            need = {}
            for d in o.deps:
                if d.queue is not None:
                    key = ("D", d.dma_sid)
                    need[key] = max(need.get(key, 0), d.dma_use)
                else:
                    pstream = self._stream_name(d)
                    if pstream == cons and o.queue is None:
                        if cons != "tensor" and (o.seq - d.seq) <= 2:
                            for (n, lo, hi) in o.reads + o.writes:
                                for (n2, lo2, hi2) in d.writes:
                                    if n == n2 and lo2 < hi and lo < hi2 \
                                            and hi > hi2 - 1024:
                                        o.pre_drain = True
                        continue
                    if o.queue is not None and d.eng == o.eng:
                        continue
                    need[("E", pstream)] = max(need.get(("E", pstream), 0), d.seq)
            for k, seq in sorted(need.items()):
                key = (cons, k)
                if hi_water.get(key, 0) >= seq:
                    continue
                hi_water[key] = seq
                o.waits.append((k, seq))
        targets = {}
        for o in self.ops:
            for (k, seq) in o.waits:
                if k[0] == "E":
                    targets.setdefault(k[1], set()).add(seq)
        by_stream = {}
        for o in self.ops:
            if o.queue is None:
                by_stream.setdefault(self._stream_name(o), {})[o.seq] = o
        for pstream, seqs in targets.items():
            for sq in seqs:
                by_stream[pstream][sq].inc_needed = True
        for o in self.ops:
            if o.queue is not None:
                o.inc_needed = True

    def emit(self, stack):
        self.schedule()
        sems = {}
        streams = set(self._stream_name(o) for o in self.ops if o.queue is None)
        for s in streams:
            sems[s] = stack.enter_context(self.nc.semaphore("sem_" + s))
        dsems = [stack.enter_context(self.nc.semaphore(f"dma{i}"))
                 for i in range(min(self.NDMASEM, len(self.sem_uses) if self.sem_uses else 0) or 1)]
        inc_prefix = {}
        for o in self.ops:
            if o.queue is None:
                sname = self._stream_name(o)
                lst = inc_prefix.setdefault(sname, [0])
                lst.append(lst[-1] + (1 if o.inc_needed else 0))
        per_engine_ops = {}
        for o in self.ops:
            per_engine_ops.setdefault(o.eng, []).append(o)
        block = stack.enter_context(self.nc.Block())
        nwaits = [0]
        for eng_name in ENGINES:
            ops = per_engine_ops.get(eng_name, [])
            if not ops:
                continue
            def mk(ops, eng_name):
                def section(eng):
                    for o in ops:
                        for (k, seq) in o.waits:
                            if k[0] == "D":
                                eng.wait_ge(dsems[k[1]], seq * 16)
                            else:
                                val = inc_prefix[k[1]][seq]
                                if val > 0:
                                    eng.wait_ge(sems[k[1]], val)
                            nwaits[0] += 1
                        if o.queue is not None:
                            if o.dma_use > 1:
                                eng.wait_ge(dsems[o.dma_sid], (o.dma_use - 1) * 16)
                            instr = o.fn(eng)
                            instr.then_inc(dsems[o.dma_sid], 16)
                        else:
                            if o.pre_drain:
                                eng.drain()
                            instr = o.fn(eng)
                            if o.inc_needed:
                                if instr is None:
                                    instr = eng.nop()
                                if eng_name in ("vector", "scalar", "gpsimd", "tensor"):
                                    instr = eng.drain()
                                instr.then_inc(sems[self._stream_name(o)], 1)
                return section
            getattr(block, eng_name)(mk(ops, eng_name))
        self.nwaits = nwaits[0]

    def dma(self, queue, out, in_, tag="", **kw):
        eng = DMAQ[queue]
        return self.op(eng, lambda e: e.dma_start(out=out, in_=in_, **kw),
                       reads=[in_], writes=[out], queue=queue, tag=tag)

    def tt(self, out, in0, in1, op, eng="vector", tag=""):
        return self.op(eng, lambda e: e.tensor_tensor(out=out, in0=in0, in1=in1, op=op),
                       reads=[in0, in1], writes=[out], tag=tag)

    def ts(self, out, in0, s1, op0, s2=None, op1=None, eng="vector", accum_out=None, tag=""):
        r = [in0]
        if isinstance(s1, bass.AP) or hasattr(s1, "tensor"):
            r.append(s1)
        if s2 is not None and hasattr(s2, "tensor"):
            r.append(s2)
        w = [out] + ([accum_out] if accum_out is not None else [])
        if op1 is None:
            return self.op(eng, lambda e: e.tensor_scalar(out=out, in0=in0, scalar1=s1, scalar2=None, op0=op0, accum_out=accum_out),
                           reads=r, writes=w, tag=tag)
        return self.op(eng, lambda e: e.tensor_scalar(out=out, in0=in0, scalar1=s1, scalar2=s2, op0=op0, op1=op1, accum_out=accum_out),
                       reads=r, writes=w, tag=tag)

    def stt(self, out, in0, scalar, in1, op0, op1, eng="vector", tag=""):
        r = [in0, in1]
        if hasattr(scalar, "tensor"):
            r.append(scalar)
        return self.op(eng, lambda e: e.scalar_tensor_tensor(out=out, in0=in0, scalar=scalar, in1=in1, op0=op0, op1=op1),
                       reads=r, writes=[out], tag=tag)

    def act(self, out, in_, func, bias=0.0, scale=1.0, accum_out=None, tag=""):
        r = [in_]
        if hasattr(bias, "tensor"):
            r.append(bias)
        if hasattr(scale, "tensor"):
            r.append(scale)
        w = [out] + ([accum_out] if accum_out is not None else [])
        return self.op("scalar", lambda e: e.activation(out=out, in_=in_, func=func, bias=bias, scale=scale, accum_out=accum_out),
                       reads=r, writes=w, tag=tag)

    def copy(self, out, in_, eng="vector", tag=""):
        return self.op(eng, lambda e: e.tensor_copy(out=out, in_=in_),
                       reads=[in_], writes=[out], tag=tag)

    def memset(self, out, val, eng="vector", tag=""):
        return self.op(eng, lambda e: e.memset(out, val), reads=[], writes=[out], tag=tag)

    def mm(self, out, lhsT, rhs, start=True, stop=True, tag=""):
        rw = [out] if not start else []
        return self.op("tensor", lambda e: e.matmul(out, lhsT, rhs, start=start, stop=stop),
                       reads=[lhsT, rhs] + rw, writes=[out], tag=tag)

    def trans(self, out, in_, ident, tag=""):
        return self.op("tensor", lambda e: e.transpose(out, in_, ident),
                       reads=[in_, ident], writes=[out], tag=tag)

    def reduce(self, out, in_, op, absval=None, negate=None, eng="vector", tag=""):
        if op == ALU.add:
            return self.op(eng, lambda e: e.reduce_sum(out=out, in_=in_, axis=AX.X, apply_absolute_value=absval, negate=negate),
                           reads=[in_], writes=[out], tag=tag)
        if op == ALU.max:
            return self.op(eng, lambda e: e.reduce_max(out=out, in_=in_, axis=AX.X, apply_absolute_value=absval, negate=negate),
                           reads=[in_], writes=[out], tag=tag)
        return self.op(eng, lambda e: e.tensor_reduce(out=out, in_=in_, axis=AX.X, op=op, apply_absolute_value=absval, negate=negate),
                       reads=[in_], writes=[out], tag=tag)

    def warmup(self, junk, n=64):
        for i in range(n):
            self.ts(junk, junk, 1.0, ALU.mult, tag="warm")

    def barrier_outputs(self, outs):
        self.op("gpsimd", lambda e: e.nop(), reads=list(outs), writes=[], tag="fin")

    def barrier(self):
        lasts = {}
        for o in self.ops:
            lasts[self._stream_name(o)] = o
        dep_ops = set(lasts.values())
        marks = []
        for e in ENGINES:
            o = self.op(e, lambda eng: eng.nop(), reads=[], writes=[], tag="bar")
            o.deps |= dep_ops
            marks.append(o)
        mset = set(marks)
        for e in ENGINES:
            o = self.op(e, lambda eng: eng.nop(), reads=[], writes=[], tag="bar2")
            o.deps |= mset


class Ctx:

    def __init__(self, nc, stack, prefix=""):
        self.nc = nc
        self.stack = stack
        self.prefix = prefix
        self.tiles = {}

    def sb(self, name, shape, dtype):
        name = self.prefix + name
        t = self.stack.enter_context(self.nc.sbuf_tensor(name, shape, dtype))
        self.tiles[name] = t[:]
        return self.tiles[name]

    def ps(self, name, shape, dtype):
        name = self.prefix + name
        t = self.stack.enter_context(self.nc.psum_tensor(name, shape, dtype))
        self.tiles[name] = t[:]
        return self.tiles[name]


B_PER_CORE = 2
T = 2048
D = 1024
NCORES = 8


def host_constants():
    c = {}
    c["ident"] = np.eye(128, dtype=np.float32)
    c["ident_bf"] = np.eye(128, dtype=ml_dtypes.bfloat16)
    m = np.zeros((128, 128), np.float32)
    m[np.triu_indices(128, 1)] = NEG
    c["diagmask"] = m
    return c


def emit_ln_tile(a, t, x_tile, xln_tm, scratch, extra_scale=1.0, eps=1e-5, width=1024):
    xsq, s1, s2 = scratch["xsq"], scratch["s1"], scratch["s2"]
    mv, rstd = scratch["mv"], scratch["rstd"]
    inv = 1.0 / width
    a.reduce(s1, x_tile, ALU.add)
    a.act(xsq, x_tile, AF.Square, accum_out=s2)
    a.ts(mv[:, 0:1], s1, inv, ALU.mult)
    a.ts(mv[:, 1:2], s2, inv, ALU.mult)
    a.ts(s1, mv[:, 0:1], mv[:, 0:1], ALU.mult)
    a.stt(rstd, s1, -1.0, mv[:, 1:2], ALU.mult, ALU.add)
    a.ts(rstd, rstd, eps, ALU.add)
    a.act(rstd, rstd, AF.Ln)
    a.act(rstd, rstd, AF.Exp, scale=-0.5)
    if extra_scale != 1.0:
        a.ts(rstd, rstd, float(extra_scale), ALU.mult)
    a.ts(xln_tm, x_tile, mv[:, 0:1], ALU.subtract, rstd[:, 0:1], ALU.mult)


def emit_tm_to_fm(a, x_tm, out_fm_chunks, identb, pxp, nchunk=8):
    for c in range(nchunk):
        a.trans(pxp, x_tm[:, c * 128:(c + 1) * 128], identb)
        a.copy(out_fm_chunks[c], pxp)


def emit_l0_ffn(a, ctx, d, PS, com, xin, xout):
    hidden = 3072
    HC = hidden // 128
    NTOK = B_PER_CORE * T
    NT = NTOK // 128
    NT512 = NTOK // 512

    if True:
        sb = ctx.sb
        x, xo = xin, xout
        wgT, wuT, wdT = d["wgT"], d["wuT"], d["wdT"]
        identb = com["identb"]
        junk = com["junk"]
        xtile = [sb(f"xt{i}", [128, D], F32) for i in range(2)]
        xln_tm = sb("xln_tm", [128, D], BF16)
        xlnf = sb("xlnf", [128, 8, T], BF16)
        scratch = {"xsq": sb("xsq", [128, D], BF16), "s1": sb("s1", [128, 1], F32),
                   "s2": sb("s2", [128, 1], F32), "mv": sb("mv", [128, 2], F32),
                   "rstd": sb("rstd", [128, 1], F32)}
        wgt = sb("wgt", [128, 8, hidden], BF16)
        wut = sb("wut", [128, 8, hidden], BF16)
        wdt = [sb(f"wdt{i}", [128, D], BF16) for i in range(2)]
        hbuf = sb("hbuf", [128, HC, 512], BF16)
        sg = [sb(f"sg{i}", [128, 512], BF16) for i in range(2)]
        pg = [PS["PSD"], PS["PSE"]]
        pu = [PS["PSA"][:, 0:512], PS["PSA"][:, 512:1024]]
        pd = PS["PSA"][:, 1024:2048]
        pxp = PS["PSB"][:, 0:128]

        for c in range(8):
            a.dma("gpq", wgt[:, c, :], wgT[c * 128:(c + 1) * 128, :])
            a.dma("gpq", wut[:, c, :], wuT[c * 128:(c + 1) * 128, :])

        NTSEQ = T // 128
        NT512SEQ = T // 512
        for b in range(B_PER_CORE):
            base = b * T
            for j in range(NTSEQ):
                xt = xtile[j % 2]
                r0 = base + j * 128
                a.dma("syq", xt, x[r0:r0 + 128, :])
                emit_ln_tile(a, None, xt, xln_tm, scratch, extra_scale=1.0 / 127.0)
                emit_tm_to_fm(a, xln_tm, [xlnf[:, c, j * 128:(j + 1) * 128] for c in range(8)],
                              identb, pxp)
            for tt in range(NT512SEQ):
                tsl = slice(tt * 512, (tt + 1) * 512)
                for hc in range(HC):
                    g = pg[hc % 2]
                    u = pu[hc % 2]
                    sgt = sg[hc % 2]
                    for c in range(8):
                        a.mm(g, wgt[:, c, hc * 128:(hc + 1) * 128], xlnf[:, c, tsl],
                             start=(c == 0), stop=(c == 7))
                    for c in range(8):
                        a.mm(u, wut[:, c, hc * 128:(hc + 1) * 128], xlnf[:, c, tsl],
                             start=(c == 0), stop=(c == 7))
                    a.act(sgt, g, AF.Silu)
                    a.stt(hbuf[:, hc, :], sgt, 1.0 / 127.0, u, ALU.mult, ALU.mult)
                for st4 in range(4):
                    j = tt * 4 + st4
                    for hc in range(HC):
                        w = wdt[hc % 2]
                        a.dma("gpq", w, wdT[hc * 128:(hc + 1) * 128, :])
                        for half in range(2):
                            a.mm(pd[:, half * 512:(half + 1) * 512],
                                 hbuf[:, hc, st4 * 128:(st4 + 1) * 128],
                                 w[:, half * 512:(half + 1) * 512],
                                 start=(hc == 0), stop=(hc == HC - 1))
                    xt = xtile[j % 2]
                    r0 = base + j * 128
                    a.dma("syq", xt, x[r0:r0 + 128, :])
                    a.tt(xt, xt, pd, ALU.add)
                    a.dma("gpq", xo[r0:r0 + 128, :], xt)



def _shard(x, core):
    return np.ascontiguousarray(x.reshape(NCORES, -1, *x.shape[1:])[core])


def run_l0_ffn(x_full, w_gate, w_up, w_down):
    w_gate, w_up, w_down = [np.ascontiguousarray(np.asarray(w, np.int32)) for w in (w_gate, w_up, w_down)]
    """x_full [16, 2048, 1024] fp32; weights int32 as given by the reference."""
    consts = host_constants()
    nc = build_l0_ffn()
    wgT = np.ascontiguousarray(w_gate.T)
    wuT = np.ascontiguousarray(w_up.T)
    wdT = np.ascontiguousarray(w_down.T)
    in_maps = []
    for c in range(NCORES):
        in_maps.append({
            "x": x_full.reshape(16 * T, D)[c * B_PER_CORE * T:(c + 1) * B_PER_CORE * T],
            "wgT": wgT, "wuT": wuT, "wdT": wdT,
            "ident": consts["ident_bf"],
        })
    res = run_bass_kernel_spmd(nc, in_maps, core_ids=list(range(NCORES)))
    out = np.concatenate([res.results[c]["xo"] for c in range(NCORES)], axis=0)
    return out.reshape(16, T, D)


def emit_attn(a, ctx, d, PS, com, xin, xout, variant):
    NTOK = B_PER_CORE * T
    if True:
        sb = ctx.sb
        x, xo = xin, xout
        wqT, woT = d["wqT"], d["woT"]
        identb, diagm, junk = com["identb"], com["diagm"], com["junk"]
        if variant == "b":
            wkT, wvT, alibik = d["wkT"], d["wvT"], d["alibik"]
        else:
            wqTs, wkT, wvT = d["wqTs"], d["wkT"], d["wvT"]
            dks, dkss, dkm, dkms, dv = d["dks"], d["dkss"], d["dkm"], d["dkms"], d["dv"]
            c1, c2 = d["c1"], d["c2"]
        xtile = [sb("xt0", [128, D], F32)] * 2
        xln_tm = sb("xln_tm", [128, D], BF16)
        xlnf = sb("xlnf", [128, 8, T], BF16)
        scratch = {"xsq": sb("xsq", [128, D], BF16), "s1": sb("s1", [128, 1], F32),
                   "s2": sb("s2", [128, 1], F32), "mv": sb("mv", [128, 2], F32),
                   "rstd": sb("rstd", [128, 1], F32)}
        wq = sb("wq", [128, 8, D], BF16)
        wo = sb("wo", [128, 8, D], BF16)
        if variant == "b":
            wk = sb("wk", [128, 8, D], BF16)
            wv = sb("wv", [128, 8, D], BF16)

        else:
            wqs = sb("wqs", [128, 8, D], BF16)
            wkt = sb("wkt", [128, 8, 192], BF16)
            wvt = sb("wvt", [128, 8, 192], BF16)
            dkst = sb("dkst", [128, 2, D], BF16)
            dksst = sb("dksst", [128, 2, D], BF16)
            dvt = sb("dvt", [128, 2, D], BF16)
            cro = sb("cro", [128, T], BF16)
            zk = sb("zk", [128, 2, T], BF16)
            xv = None

            t1 = sb("t1", [128, 512], F32)
            t2 = sb("t2", [128, 512], F32)
        vtm = sb("vtm", [128, 16, D], BF16)
        oall = sb("oall", [128, 16, D], BF16)
        if variant == "a":
            xv = oall[:, 0:4, :].rearrange("p a b -> p (a b)").rearrange("p (c b) -> p c b", c=2)
        qs = sb("qs", [128, T], BF16)
        ks = sb("ks", [128, T], BF16)
        pbuf = sb("pbuf", [128, T], BF16)
        pts = sb("pts", [128, T], BF16)
        rowmax = sb("rowmax", [128, 1], F32)
        rowsum = sb("rowsum", [128, 1], F32)
        nbias = sb("nbias", [128, 1], F32)
        recl = sb("recl", [128, 1], F32)
        of = sb("of", [128, 8, 128], BF16)
        PSA = PS["PSA"]
        PSB = PS["PSB"]
        PSC = PS["PSD"][:, 0:128]

        for c in range(8):
            a.dma("gpq", wq[:, c, :], wqT[c * 128:(c + 1) * 128, :])
            a.dma("gpq", wo[:, c, :], woT[c * 128:(c + 1) * 128, :])
        if variant == "b":
            for c in range(8):
                a.dma("gpq", wk[:, c, :], wkT[c * 128:(c + 1) * 128, :])
                a.dma("gpq", wv[:, c, :], wvT[c * 128:(c + 1) * 128, :])

        else:
            for c in range(8):
                a.dma("gpq", wqs[:, c, :], wqTs[c * 128:(c + 1) * 128, :])
                a.dma("gpq", wkt[:, c, :192], wkT[c * 128:(c + 1) * 128, :])
                a.dma("gpq", wvt[:, c, :192], wvT[c * 128:(c + 1) * 128, :])
            a.dma("syq", cro[0:64, :], c1)
            a.dma("syq", cro[64:128, :], c2)
            a.dma("gpq", dkst[:, 0, :], dks[0:128, :])
            a.dma("gpq", dkst[0:64, 1, :], dks[128:192, :])
            a.dma("gpq", dksst[:, 0, :], dkss[0:128, :])
            a.dma("gpq", dksst[0:64, 1, :], dkss[128:192, :])
            mskt = vtm[:, 0:2, :]
            mskst = vtm[:, 2:4, :]
            a.dma("syq", mskt[:, 0, :], dkm[0:128, :])
            a.dma("syq", mskt[0:64, 1, :], dkm[128:192, :])
            a.dma("syq", mskst[:, 0, :], dkms[0:128, :])
            a.dma("syq", mskst[0:64, 1, :], dkms[128:192, :])
            a.tt(dkst[:, 0, :], dkst[:, 0, :], mskt[:, 0, :], ALU.mult)
            a.tt(dkst[0:64, 1, :], dkst[0:64, 1, :], mskt[0:64, 1, :], ALU.mult)
            a.tt(dksst[:, 0, :], dksst[:, 0, :], mskst[:, 0, :], ALU.mult)
            a.tt(dksst[0:64, 1, :], dksst[0:64, 1, :], mskst[0:64, 1, :], ALU.mult)
            a.dma("gpq", dvt[:, 0, :], dv[0:128, :])
            a.dma("gpq", dvt[0:64, 1, :], dv[128:192, :])

        HD = 64
        scale = 1.0 / math.sqrt(HD)
        for b in range(B_PER_CORE):
            base = b * T
            for j in range(T // 128):
                xt = xtile[j % 2]
                a.dma("syq", xt, x[base + j * 128:base + (j + 1) * 128, :])
                emit_ln_tile(a, None, xt, xln_tm, scratch)
                emit_tm_to_fm(a, xln_tm, [xlnf[:, c, j * 128:(j + 1) * 128] for c in range(8)],
                              identb, PSB[:, 0:128])
            if variant == "a":
                for tt in range(4):
                    tsl = slice(tt * 512, (tt + 1) * 512)
                    pz = PSA[:, 0:512]
                    pz2 = PSA[:, 512:1024]
                    for (wt, dst) in ((wkt, zk), (wvt, xv)):
                        for c in range(8):
                            a.mm(pz, wt[:, c, 0:128], xlnf[:, c, tsl], start=(c == 0), stop=(c == 7))
                        a.copy(dst[:, 0, tsl], pz)
                        for c in range(8):
                            a.mm(pz2[0:64, :], wt[:, c, 128:192], xlnf[:, c, tsl], start=(c == 0), stop=(c == 7))
                        a.copy(dst[0:64, 1, tsl], pz2[0:64, :])
            for j in range(T // 128):
                pv = PSA[:, 1024:2048]
                if variant == "b":
                    for hf in range(2):
                        for c in range(8):
                            a.mm(pv[:, hf * 512:(hf + 1) * 512],
                                 xlnf[:, c, j * 128:(j + 1) * 128],
                                 wv[:, c, hf * 512:(hf + 1) * 512],
                                 start=(c == 0), stop=(c == 7))
                else:
                    for hf in range(2):
                        a.mm(pv[:, hf * 512:(hf + 1) * 512], xv[:, 0, j * 128:(j + 1) * 128],
                             dvt[:, 0, hf * 512:(hf + 1) * 512], start=True, stop=False)
                        a.mm(pv[:, hf * 512:(hf + 1) * 512], xv[0:64, 1, j * 128:(j + 1) * 128],
                             dvt[0:64, 1, hf * 512:(hf + 1) * 512], start=False, stop=True)
                a.copy(vtm[:, j, :], pv)
            for h in range(16):
                hsl = slice(h * 64, (h + 1) * 64)
                for tt in range(4):
                    tsl = slice(tt * 512, (tt + 1) * 512)
                    pq = PSA[:, 0:512]
                    pk = PSA[:, 512:1024]
                    if variant == "b":
                        for c in range(8):
                            a.mm(pq[0:64, :], wq[:, c, hsl], xlnf[:, c, tsl], start=(c == 0), stop=(c == 7))
                        a.copy(qs[0:64, tsl], pq[0:64, :])
                        for c in range(8):
                            a.mm(pk[0:64, :], wk[:, c, hsl], xlnf[:, c, tsl], start=(c == 0), stop=(c == 7))
                        a.copy(ks[0:64, tsl], pk[0:64, :])
                    else:
                        pq2 = PSA[:, 1024:1536]
                        for c in range(8):
                            a.mm(pq[0:64, :], wq[:, c, hsl], xlnf[:, c, tsl], start=(c == 0), stop=(c == 7))
                        for c in range(8):
                            a.mm(pq2[0:64, :], wqs[:, c, hsl], xlnf[:, c, tsl], start=(c == 0), stop=(c == 7))
                        a.tt(t1[0:64, :], pq[0:64, :], cro[0:64, tsl], ALU.mult)
                        a.tt(t2[0:64, :], pq2[0:64, :], cro[64:128, tsl], ALU.mult)
                        a.tt(qs[0:64, tsl], t1[0:64, :], t2[0:64, :], ALU.add)
                        a.mm(pq[0:64, :], dkst[:, 0, hsl], zk[:, 0, tsl], start=True, stop=False)
                        a.mm(pq[0:64, :], dkst[0:64, 1, hsl], zk[0:64, 1, tsl], start=False, stop=True)
                        a.mm(pq2[0:64, :], dksst[:, 0, hsl], zk[:, 0, tsl], start=True, stop=False)
                        a.mm(pq2[0:64, :], dksst[0:64, 1, hsl], zk[0:64, 1, tsl], start=False, stop=True)
                        a.tt(t1[0:64, :], pq[0:64, :], cro[0:64, tsl], ALU.mult)
                        a.tt(t2[0:64, :], pq2[0:64, :], cro[64:128, tsl], ALU.mult)
                        a.tt(ks[0:64, tsl], t1[0:64, :], t2[0:64, :], ALU.add)
                if variant == "b":
                    a.memset(qs[64:65, :], 1.0)
                    a.dma("syq", ks[64:65, :], alibik[h:h + 1, :])
                    KR = 65
                else:
                    KR = 64
                for qt in range(T // 128):
                    L = (qt + 1) * 128
                    S = PSA[:, 0:L]
                    for kb in range((L + 511) // 512):
                        n = min(512, L - kb * 512)
                        a.mm(PSA[:, kb * 512:kb * 512 + n], qs[0:KR, qt * 128:(qt + 1) * 128],
                             ks[0:KR, kb * 512:kb * 512 + n], start=True, stop=True)
                    a.tt(PSA[:, L - 128:L], PSA[:, L - 128:L], diagm, ALU.add)
                    a.reduce(rowmax, S, ALU.max)
                    a.ts(nbias, rowmax, -scale, ALU.mult)
                    a.act(pbuf[:, 0:L], S, AF.Exp, bias=nbias[:, 0:1], scale=scale,
                          accum_out=rowsum)
                    a.op("vector", lambda e, _r=recl, _s=rowsum: e.reciprocal(out=_r, in_=_s),
                         reads=[rowsum], writes=[recl])
                    for kb in range(L // 128):
                        a.trans(PSB[:, kb * 128:(kb + 1) * 128], pbuf[:, kb * 128:(kb + 1) * 128], identb)
                    a.copy(pts[:, 0:L], PSB[:, 0:L])
                    for kb in range(L // 128):
                        a.mm(PSC[:, 0:64], pts[:, kb * 128:(kb + 1) * 128],
                             vtm[:, kb, hsl], start=(kb == 0), stop=(kb == L // 128 - 1))
                    a.ts(oall[:, qt, hsl], PSC[:, 0:64], recl[:, 0:1], ALU.mult)
            for j in range(T // 128):
                emit_tm_to_fm(a, oall[:, j, :], [of[:, c, :] for c in range(8)], identb,
                              PSB[:, 0:128])
                pd = PSA[:, 0:1024]
                for c in range(8):
                    for hf in range(2):
                        a.mm(pd[:, hf * 512:(hf + 1) * 512], of[:, c, :],
                             wo[:, c, hf * 512:(hf + 1) * 512],
                             start=(c == 0), stop=(c == 7))
                xt = xtile[j % 2]
                a.dma("scq", xt, x[base + j * 128:base + (j + 1) * 128, :])
                a.tt(xt, xt, pd, ALU.add)
                a.dma("gpq", xo[base + j * 128:base + (j + 1) * 128, :], xt)


def rope_tables_interleaved(hd=64, base=10000.0):
    half = hd // 2
    freq = 1.0 / base ** (np.arange(half, dtype=np.float32) / half)
    ang = np.outer(np.arange(T, dtype=np.float32), freq)
    cos = np.cos(ang).astype(np.float32).T
    sin = np.sin(ang).astype(np.float32).T
    C1 = np.zeros((hd, T), np.float32)
    C2 = np.zeros((hd, T), np.float32)
    C1[0::2] = cos
    C1[1::2] = cos
    C2[0::2] = -sin
    C2[1::2] = sin
    return C1, C2


def host_attn_a_decoders(key_decoder, value_decoder):
    F_, H_, KR_ = 32, 16, 3
    dk = np.zeros((192, D), np.float32)
    for cp in range(2):
        for c in range(2):
            cc = c ^ cp
            dk[np.arange(F_)[:, None, None] * 6 + np.arange(KR_)[None, :, None] * 2 + cp,
               np.arange(H_)[None, None, :] * 64 + 2 * np.arange(F_)[:, None, None] + c] =                 np.transpose(key_decoder[:, :, :, cc], (0, 2, 1))
    dks = dk.reshape(192, 16, 32, 2)[:, :, :, ::-1].reshape(192, D)
    row_odd = (np.arange(192) % 2 == 1)
    col_even = (np.arange(D) % 2 == 0)
    m = np.ones((192, D), np.float32)
    m[np.ix_(row_odd, col_even)] = -1.0
    ms = np.ones((192, D), np.float32)
    ms[np.ix_(row_odd, ~col_even)] = -1.0
    dv = np.zeros((192, D), np.float32)
    dv[:, :] = np.transpose(value_decoder, (1, 0, 2)).reshape(192, D)
    return (np.ascontiguousarray(dk), np.ascontiguousarray(dks),
            m.astype(ml_dtypes.bfloat16), ms.astype(ml_dtypes.bfloat16),
            np.ascontiguousarray(dv))


def run_attn_a(x_full, Wq, Wk, Wv, key_decoder, value_decoder, Wproj):
    Wq, Wk, Wv, Wproj = [np.asarray(w, np.float32) for w in (Wq, Wk, Wv, Wproj)]
    consts = host_constants()
    nc = build_attn("a")
    perm = np.arange(D) ^ 1
    dk, dks_sw, m, ms, dv = host_attn_a_decoders(key_decoder, value_decoder)
    C1, C2 = rope_tables_interleaved()
    in_common = {
        "wqT": np.ascontiguousarray(Wq), "wqTs": np.ascontiguousarray(Wq[:, perm]),
        "wkT": np.ascontiguousarray(Wk), "wvT": np.ascontiguousarray(Wv),
        "dks": dk, "dkss": dks_sw, "dkm": m, "dkms": ms, "dv": dv,
        "woT": np.ascontiguousarray(Wproj),
        "c1": C1.astype(ml_dtypes.bfloat16), "c2": C2.astype(ml_dtypes.bfloat16),
        "ident": consts["ident_bf"], "diagmask": consts["diagmask"],
    }
    in_maps = []
    for c in range(NCORES):
        mm_ = dict(in_common)
        mm_["x"] = x_full.reshape(16 * T, D)[c * B_PER_CORE * T:(c + 1) * B_PER_CORE * T]
        in_maps.append(mm_)
    res = run_bass_kernel_spmd(nc, in_maps, core_ids=list(range(NCORES)))
    out = np.concatenate([res.results[c]["xo"] for c in range(NCORES)], axis=0)
    return out.reshape(16, T, D)


def host_alibi(scale=8.0):
    slopes = np.array([2.0 ** (-(h + 1) / 2.0) for h in range(16)], np.float32)
    j = np.arange(T, dtype=np.float32)
    return (-slopes[:, None] * j[None, :] * scale).astype(ml_dtypes.bfloat16)


def run_attn_b(x_full, Wq, Wk, Wv, Wo):
    Wq, Wk, Wv, Wo = [np.asarray(w, np.float32) for w in (Wq, Wk, Wv, Wo)]
    consts = host_constants()
    nc = build_attn("b")
    in_common = {
        "wqT": np.ascontiguousarray(Wq.T), "wkT": np.ascontiguousarray(Wk.T),
        "wvT": np.ascontiguousarray(Wv.T), "woT": np.ascontiguousarray(Wo.T),
        "ident": consts["ident_bf"], "diagmask": consts["diagmask"],
        "alibik": host_alibi(),
    }
    in_maps = []
    for c in range(NCORES):
        m = dict(in_common)
        m["x"] = x_full.reshape(16 * T, D)[c * B_PER_CORE * T:(c + 1) * B_PER_CORE * T]
        in_maps.append(m)
    res = run_bass_kernel_spmd(nc, in_maps, core_ids=list(range(NCORES)))
    out = np.concatenate([res.results[c]["xo"] for c in range(NCORES)], axis=0)
    return out.reshape(16, T, D)


def emit_fp4_quant(a, src, dst, q, nb, width):
    absx, t1i, cq = q["absx"], q["t1i"], q["cq"]
    bmax, bscale, binv = q["bmax"], q["bscale"], q["binv"]
    w3 = (src, absx, AF.Abs)
    a.act(absx[:, 0:width], src, AF.Abs)
    a.op("vector", lambda e: e.tensor_reduce(
        out=bmax[:, 0:nb].unsqueeze(2), in_=absx[:, 0:width].rearrange("p (b w) -> p b w", w=16),
        axis=AX.X, op=ALU.max), reads=[absx], writes=[bmax])
    a.ts(bscale[:, 0:nb], bmax[:, 0:nb], 1.0 / 6.0, ALU.mult, 1e-12, ALU.max)
    a.op("vector", lambda e: e.reciprocal(out=binv[:, 0:nb], in_=bscale[:, 0:nb]),
         reads=[bscale], writes=[binv])
    bc = binv[:, 0:nb].unsqueeze(2).broadcast_to([128, nb, 16])
    a.op("vector", lambda e: e.tensor_tensor(
        out=absx[:, 0:width].rearrange("p (b w) -> p b w", w=16),
        in0=absx[:, 0:width].rearrange("p (b w) -> p b w", w=16), in1=bc, op=ALU.mult),
        reads=[absx, binv], writes=[absx])
    a.ts(t1i[:, 0:width], absx[:, 0:width], 4.0, ALU.min, 2.0, ALU.subtract)
    a.ts(cq[:, 0:width], absx[:, 0:width], 5.0, ALU.is_gt, 2.0, ALU.mult)
    a.stt(cq[:, 0:width], t1i[:, 0:width], 0.0, cq[:, 0:width], ALU.max, ALU.add)
    a.ts(t1i[:, 0:width], absx[:, 0:width], 2.0, ALU.min, 2.0, ALU.mult)
    a.stt(cq[:, 0:width], t1i[:, 0:width], 0.5, cq[:, 0:width], ALU.mult, ALU.add)
    a.act(absx[:, 0:width], src, AF.Sign)
    bs = bscale[:, 0:nb].unsqueeze(2).broadcast_to([128, nb, 16])
    a.op("vector", lambda e: e.tensor_tensor(
        out=absx[:, 0:width].rearrange("p (b w) -> p b w", w=16),
        in0=absx[:, 0:width].rearrange("p (b w) -> p b w", w=16), in1=bs, op=ALU.mult),
        reads=[absx, bscale], writes=[absx])
    a.tt(dst, cq[:, 0:width], absx[:, 0:width], ALU.mult)


def emit_l1_ffn(a, ctx, d, PS, com, xin, xout):
    hidden = 2816
    HC = hidden // 128
    NB_D = D // 16
    NB_H = hidden // 16
    NTOK = B_PER_CORE * T

    if True:
        sb = ctx.sb
        x, xo = xin, xout
        wg_in, wu_in, wd_in = d["wg"], d["wu"], d["wd"]
        identb, junk = com["identb"], com["junk"]
        xtile = sb("xt0", [128, D], F32)
        xln_tm = sb("xln_tm", [128, D], F32)
        scratch = {"xsq": sb("xsq", [128, D], BF16), "s1": sb("s1", [128, 1], F32),
                   "s2": sb("s2", [128, 1], F32), "mv": sb("mv", [128, 2], F32),
                   "rstd": sb("rstd", [128, 1], F32)}
        q = {"absx": sb("absx", [128, hidden], F32),
             "t1i": sb("t1i", [128, hidden], mybir.dt.int16),
             
             "cq": sb("cq", [128, hidden], F32),
             "bmax": sb("bmax", [128, NB_H], F32), "bscale": sb("bscale", [128, NB_H], F32),
             "binv": sb("binv", [128, NB_H], F32)}
        wqtm = sb("wqtm", [128, hidden], BF16)
        wgq = sb("wgq", [128, 8, hidden], BF16)
        wuq = sb("wuq", [128, 8, hidden], BF16)
        wdq = sb("wdq", [128, HC, D], BF16)
        xq_tm = sb("xq_tm", [128, D], BF16)
        xqf = sb("xqf", [128, 8, 128], BF16)
        htile = sb("htile", [128, hidden], F32)
        hq_tm = sb("hq_tm", [128, hidden], BF16)
        hqf = sb("hqf", [128, HC, 128], BF16)
        PSA = PS["PSA"]
        PSB = PS["PSB"][:, 0:256]


        for (w_in, dstfm, nrows, wwidth, nbw) in (
                (wg_in, wgq, hidden, D, NB_D), (wu_in, wuq, hidden, D, NB_D),
                (wd_in, wdq, D, hidden, NB_H)):
            RC = nrows // 128
            for rc in range(RC):
                src = htile[:, 0:wwidth]
                a.dma("syq", src, w_in[rc * 128:(rc + 1) * 128, :])
                emit_fp4_quant(a, src, wqtm[:, 0:wwidth], q, nbw, wwidth)
                for c in range(wwidth // 128):
                    a.trans(PSB[:, 0:128], wqtm[:, c * 128:(c + 1) * 128], identb)
                    a.copy(dstfm[:, c, rc * 128:(rc + 1) * 128], PSB[:, 0:128])

        ISL = [(i * 512, min(512, hidden - i * 512)) for i in range((hidden + 511) // 512)]
        for j in range(NTOK // 128):
            a.dma("syq", xtile, x[j * 128:(j + 1) * 128, :])
            emit_ln_tile(a, None, xtile, xln_tm, scratch)
            emit_fp4_quant(a, xln_tm, xq_tm, q, NB_D, D)
            for c in range(8):
                a.trans(PSB[:, 0:128], xq_tm[:, c * 128:(c + 1) * 128], identb)
                a.copy(xqf[:, c, :], PSB[:, 0:128])
            for (i0, n) in ISL:
                pg = PSA[:, 0:512]
                pu = PSA[:, 512:1024]
                for c in range(8):
                    a.mm(pg[:, 0:n], wgq[:, c, i0:i0 + n], xqf[:, c, :].transpose([0, 1]) if False else xqf[:, c, :],
                         start=(c == 0), stop=(c == 7)) if False else None
                for c in range(8):
                    a.mm(pg[:, 0:n], xqf[:, c, :], wgq[:, c, i0:i0 + n],
                         start=(c == 0), stop=(c == 7))
                for c in range(8):
                    a.mm(pu[:, 0:n], xqf[:, c, :], wuq[:, c, i0:i0 + n],
                         start=(c == 0), stop=(c == 7))
                a.act(htile[:, i0:i0 + n], pg[:, 0:n], AF.Silu)
                a.tt(htile[:, i0:i0 + n], htile[:, i0:i0 + n], pu[:, 0:n], ALU.mult)
            emit_fp4_quant(a, htile, hq_tm, q, NB_H, hidden)
            for c in range(HC):
                a.trans(PSB[:, 0:128], hq_tm[:, c * 128:(c + 1) * 128], identb)
                a.copy(hqf[:, c, :], PSB[:, 0:128])
            pd = PSA[:, 1024:2048]
            for hf in range(2):
                for c in range(HC):
                    a.mm(pd[:, hf * 512:(hf + 1) * 512], hqf[:, c, :],
                         wdq[:, c, hf * 512:(hf + 1) * 512],
                         start=(c == 0), stop=(c == HC - 1))
            a.tt(xtile, xtile, pd[:, 0:1024], ALU.add)
            a.dma("gpq", xo[j * 128:(j + 1) * 128, :], xtile)


def run_l1_ffn(x_full, w_gate, w_up, w_down):
    consts = host_constants()
    nc = build_l1_ffn()
    in_common = {"wg": np.ascontiguousarray(w_gate), "wu": np.ascontiguousarray(w_up),
                 "wd": np.ascontiguousarray(w_down), "ident": consts["ident_bf"]}
    in_maps = []
    for c in range(NCORES):
        m = dict(in_common)
        m["x"] = x_full.reshape(16 * T, D)[c * B_PER_CORE * T:(c + 1) * B_PER_CORE * T]
        in_maps.append(m)
    res = run_bass_kernel_spmd(nc, in_maps, core_ids=list(range(NCORES)))
    out = np.concatenate([res.results[c]["xo"] for c in range(NCORES)], axis=0)
    return out.reshape(16, T, D)


def emit_mamba(a, ctx, d, PS, com, xin, xout):
    E = 2048
    NSTATE = 16
    TC = 512
    NTC = T // TC
    NTOK = B_PER_CORE * T

    if True:
        sb = ctx.sb
        x, xo = xin, xout
        winT, wxT, wdtT, woutT = d["winT"], d["wxT"], d["wdtT"], d["woutT"]
        convr, convb, bdt, dprm, alr = d["convr"], d["convb"], d["bdt"], d["dprm"], d["alr"]
        sel_in, selt_in, selb_in = d["sel"], d["selt"], d["selb"]
        lnw_in, lnb_in = d["lnw"], d["lnb"]
        identb, junk = com["identb"], com["junk"]
        win = sb("win", [128, 8, 2 * E], BF16)
        wx = sb("wx", [128, 16, 96], BF16)
        wdt = sb("wdt", [64, E], BF16)
        wout = [sb(f"wout{i}", [128, 2, D], BF16) for i in range(2)]
        conw = sb("conw", [128, 64], F32)
        conb = sb("conb", [128, 16], F32)
        bdtt = sb("bdtt", [128, 16], F32)
        dprt = sb("dprt", [128, 16], F32)
        alrt = sb("alrt", [128, 256], F32)
        nega = alrt
        self_ = sb("selr", [128, 16, 128], BF16)
        selt = sb("seltr", [128, 16, 128], BF16)
        selbc = sb("selbc", [128, 256], BF16)
        lnw = sb("lnwr", [128, D], F32)
        lnb = sb("lnbr", [128, D], F32)
        xtile = sb("xt0", [128, D], F32)
        xln_tm = sb("xln_tm", [128, D], BF16)
        xlnf = sb("xlnf", [128, 8, TC], BF16)
        scratch = {"xsq": sb("xsq", [128, D], BF16), "s1": sb("s1", [128, 1], F32),
                   "s2": sb("s2", [128, 1], F32), "mv": sb("mv", [128, 2], F32),
                   "rstd": sb("rstd", [128, 1], F32)}
        xs = sb("xs", [128, 16, TC + 3], BF16)
        uf = sb("uf", [128, 16, TC], BF16)
        zf = sb("zf", [128, 16, TC], BF16)
        yf = sb("yf", [128, 16, TC], BF16)
        cacc = sb("cacc", [128, TC], F32)
        reld = cacc
        xdb = sb("xdb", [128, TC], BF16)
        brep = sb("brep", [128, TC], F32)
        crep = sb("crep", [128, TC], F32)
        deltac = sb("deltac", [128, TC], BF16)
        absd = sb("absd", [128, TC], F32)
        duc = sb("duc", [128, TC], BF16)
        at = sb("at", [128, TC], F32)
        bt = sb("bt", [128, TC], F32)
        ht = sb("ht", [128, TC], F32)
        gt = sb("gt", [128, TC], BF16)
        y2 = sb("y2", [128, TC], F32)
        zs = sb("zs", [128, TC], F32)
        hstate = sb("hstate", [128, 256], F32)
        mo = sb("mo", [128, D], F32)
        PSA = PS["PSA"]
        PSB = PS["PSB"][:, 0:256]

        for c in range(8):
            a.dma("gpq", win[:, c, :], winT[c * 128:(c + 1) * 128, :])
        for c in range(16):
            a.dma("gpq", wx[:, c, :], wxT[c * 128:(c + 1) * 128, :])
        a.dma("gpq", wdt, wdtT)
        for (dst, src) in ((conw, convr), (conb, convb), (bdtt, bdt), (dprt, dprm),
                           (alrt, alr), (lnw, lnw_in), (lnb, lnb_in)):
            a.dma("syq", dst, src)
        a.dma("syq", self_.rearrange("p a b -> p (a b)"), sel_in)
        a.dma("syq", selt.rearrange("p a b -> p (a b)"), selt_in)
        a.dma("syq", selbc, selb_in)
        a.act(nega, alrt, AF.Exp)
        a.ts(nega, nega, -1.0, ALU.mult)

        for b in range(B_PER_CORE):
            base = b * T
            a.memset(hstate, 0.0)
            for tc in range(NTC):
                t0 = tc * TC
                for j in range(TC // 128):
                    a.dma("syq", xtile, x[base + t0 + j * 128:base + t0 + (j + 1) * 128, :])
                    emit_ln_tile(a, None, xtile, xln_tm, scratch)
                    emit_tm_to_fm(a, xln_tm, [xlnf[:, c, j * 128:(j + 1) * 128] for c in range(8)],
                                  identb, PSB[:, 0:128])
                if tc == 0:
                    for c in range(16):
                        a.memset(xs[:, c, 0:3], 0.0)
                for ec in range(16):
                    pz = PSA[:, 0:512]
                    for c in range(8):
                        a.mm(pz, win[:, c, ec * 128:(ec + 1) * 128], xlnf[:, c, :],
                             start=(c == 0), stop=(c == 7))
                    a.copy(xs[:, ec, 3:TC + 3], pz)
                    pz2 = PSA[:, 512:1024]
                    for c in range(8):
                        a.mm(pz2, win[:, c, E + ec * 128:E + (ec + 1) * 128], xlnf[:, c, :],
                             start=(c == 0), stop=(c == 7))
                    a.copy(zf[:, ec, :], pz2)
                for ec in range(16):
                    a.ts(cacc, xs[:, ec, 0:TC], conw[:, ec * 4 + 0:ec * 4 + 1], ALU.mult)
                    for k in (1, 2, 3):
                        a.stt(cacc, xs[:, ec, k:TC + k], conw[:, ec * 4 + k:ec * 4 + k + 1],
                              cacc, ALU.mult, ALU.add)
                    a.act(uf[:, ec, :], cacc, AF.Silu, bias=conb[:, ec:ec + 1], scale=1.0)
                for ec in range(16):
                    a.copy(xs[:, ec, 0:3], xs[:, ec, TC:TC + 3])
                pxd = PSA[:, 1024:1536]
                for c in range(16):
                    a.mm(pxd[0:96, :], wx[:, c, :], uf[:, c, :], start=(c == 0), stop=(c == 15))
                a.copy(xdb[0:96, :], pxd[0:96, :])
                a.mm(PSA[0:128, 0:512], selbc[0:96, 0:128], xdb[0:96, :], start=True, stop=True)
                a.copy(brep, PSA[:, 0:512])
                a.mm(PSA[0:128, 512:1024], selbc[0:96, 128:256], xdb[0:96, :], start=True, stop=True)
                a.copy(crep, PSA[:, 512:1024])
                for dc in range(16):
                    pdl = PSA[:, 1536:2048]
                    a.mm(pdl, wdt[:, dc * 128:(dc + 1) * 128], xdb[0:64, :], start=True, stop=True)
                    a.act(absd, pdl, AF.Abs, bias=bdtt[:, dc:dc + 1], scale=1.0)
                    a.act(reld, pdl, AF.Relu, bias=bdtt[:, dc:dc + 1], scale=1.0)
                    a.act(absd, absd, AF.Exp, scale=-1.0)
                    a.act(absd, absd, AF.Ln, bias=1.0, scale=1.0)
                    a.tt(deltac, reld, absd, ALU.add)
                    a.tt(duc, deltac, uf[:, dc, :], ALU.mult)
                    for sub in range(16):
                        ct = dc * 16 + sub
                        prd = PSA[:, 0:512]
                        a.mm(prd, self_[:, sub, :], deltac, start=True, stop=True)
                        a.act(at, prd, AF.Exp, bias=0.0, scale=nega[:, ct:ct + 1])
                        prd2 = PSA[:, 512:1024]
                        a.mm(prd2, self_[:, sub, :], duc, start=True, stop=True)
                        a.tt(bt, prd2, brep, ALU.mult)
                        a.op("vector", lambda e, _a=at, _b=bt, _h=ht, _i=hstate[:, ct:ct + 1]:
                             e.tensor_tensor_scan(out=_h, data0=_a, data1=_b, initial=_i,
                                                  op0=ALU.mult, op1=ALU.add),
                             reads=[at, bt, hstate[:, ct:ct + 1]], writes=[ht])
                        a.copy(hstate[:, ct:ct + 1], ht[:, TC - 1:TC])
                        a.tt(gt, ht, crep, ALU.mult)
                        a.mm(PSA[:, 1024:1536], selt[:, sub, :], gt,
                             start=(sub == 0), stop=(sub == 15))
                    a.stt(y2, uf[:, dc, :], dprt[:, dc:dc + 1], PSA[:, 1024:1536],
                          ALU.mult, ALU.add)
                    a.act(zs, zf[:, dc, :], AF.Tanh, scale=0.5)
                    a.ts(zs, zs, 0.5, ALU.mult, 0.5, ALU.add)
                    a.tt(zs, zs, zf[:, dc, :], ALU.mult)
                    a.tt(yf[:, dc, :], y2, zs, ALU.mult)
                for j in range(TC // 128):
                    pd = PSA[:, 0:1024]
                    for ec in range(16):
                        w = wout[ec % 2]
                        a.dma("gpq", w[:, 0, :], woutT[ec * 128:(ec + 1) * 128, :])
                        for hf in range(2):
                            a.mm(pd[:, hf * 512:(hf + 1) * 512], yf[:, ec, j * 128:(j + 1) * 128],
                                 w[:, 0, hf * 512:(hf + 1) * 512],
                                 start=(ec == 0), stop=(ec == 15))
                    r0 = base + t0 + j * 128
                    a.dma("scq", xtile, x[r0:r0 + 128, :])
                    emit_ln_tile(a, None, xtile, xln_tm, scratch)
                    a.op("vector", lambda e, _o=mo, _p=pd, _x=xln_tm:
                         e.tensor_tensor(out=_o, in0=_p, in1=_x, op=ALU.add),
                         reads=[pd, xln_tm], writes=[mo])
                    emit_ln_tile(a, None, mo, mo, scratch)
                    a.tt(mo, mo, lnw, ALU.mult)
                    a.tt(mo, mo, lnb, ALU.add)
                    a.tt(xtile, xtile, mo, ALU.add)
                    a.dma("gpq", xo[r0:r0 + 128, :], xtile)


def host_mamba_consts():
    sel = np.zeros((16, 128, 128), np.float32)
    for sub in range(16):
        for p in range(128):
            sel[sub, 8 * sub + p % 8, p] = 1.0
    selt = np.transpose(sel, (0, 2, 1))
    sel_h = np.ascontiguousarray(np.transpose(sel, (1, 0, 2)).reshape(128, 16 * 128)).astype(ml_dtypes.bfloat16)
    selt_h = np.ascontiguousarray(np.transpose(selt, (1, 0, 2)).reshape(128, 16 * 128)).astype(ml_dtypes.bfloat16)
    selbc = np.zeros((128, 256), np.float32)
    for p in range(128):
        selbc[64 + p // 8, p] = 1.0
        selbc[80 + p // 8, 128 + p] = 1.0
    return sel_h, selt_h, selbc.astype(ml_dtypes.bfloat16)


def run_mamba(x_full, W_in, conv_w, conv_b, W_x, W_dt, b_dt, A_log, D_param, W_out, ln_w, ln_b):
    consts = host_constants()
    nc = build_mamba()
    sel_h, selt_h, selbc = host_mamba_consts()
    alr = np.zeros((128, 256), np.float32)
    p = np.arange(128)
    for ct in range(256):
        alr[p, ct] = A_log[8 * ct + (p % 8), p // 8]
    in_common = {
        "winT": np.ascontiguousarray(W_in.T), "wxT": np.ascontiguousarray(W_x.T),
        "wdtT": np.ascontiguousarray(W_dt.T), "woutT": np.ascontiguousarray(W_out.T),
        "convr": np.ascontiguousarray(conv_w[:, 0, :].reshape(16, 128, 4).transpose(1, 0, 2).reshape(128, 64)),
        "convb": np.ascontiguousarray(conv_b.reshape(16, 128).T),
        "bdt": np.ascontiguousarray(b_dt.reshape(16, 128).T),
        "dprm": np.ascontiguousarray(D_param.reshape(16, 128).T),
        "alr": alr,
        "sel": sel_h, "selt": selt_h, "selb": selbc,
        "lnw": np.ascontiguousarray(np.tile(ln_w[None, :], (128, 1))),
        "lnb": np.ascontiguousarray(np.tile(ln_b[None, :], (128, 1))),
        "ident": consts["ident_bf"],
    }
    in_maps = []
    for c in range(NCORES):
        m = dict(in_common)
        m["x"] = x_full.reshape(16 * T, D)[c * B_PER_CORE * T:(c + 1) * B_PER_CORE * T]
        in_maps.append(m)
    res = run_bass_kernel_spmd(nc, in_maps, core_ids=list(range(NCORES)))
    out = np.concatenate([res.results[c]["xo"] for c in range(NCORES)], axis=0)
    return out.reshape(16, T, D)


def emit_attn_pal(a, ctx, d, PS, com, xin, xout, pnscr):
    NTOK = B_PER_CORE * T

    if True:
        sb = ctx.sb
        x, xo = xin, xout
        wi_in, wp_in, wo_in, palT_in = d["wi"], d["wp"], d["wo"], d["palT"]
        w1g_in, w1d_in, b1_in, w2_in, b2_in = d["w1g"], d["w1d"], d["b1"], d["w2"], d["b2"]
        wc0_in, wc1_in, bc_in, wm_in, bm_in = d["wcc0"], d["wcc1"], d["bc"], d["wm"], d["bm"]
        cos_in, sin_in = d["cosr"], d["sinr"]
        m0_in, keepm_in = d["m0"], d["keepm"]
        iota256_in, iota16r_in, tcol_in, eqsel_in = d["iota256"], d["iota16r"], d["tcol"], d["eqsel"]
        selg_in, selx_in = d["selg"], d["selx"]
        identb, identf, diagm, junk = com["identb"], com["identf"], com["diagm"], com["junk"]
        wi = sb("wit", [128, 8, D], BF16)
        wp = sb("wpt", [128, 8, D], BF16)
        wot = sb("wot", [128, 8, D], BF16)
        palt = sb("paltt", [128, 2, D], BF16)
        w1g = sb("w1gt", [16, 64], BF16)
        w1d = sb("w1dt", [1, 64], BF16)
        b1 = sb("b1t", [64, 1], F32)
        w2 = sb("w2t", [64, 64], BF16)
        b2 = sb("b2t", [64, 1], F32)
        wc0 = sb("wc0t", [64, 1], BF16)
        wc1 = sb("wc1t", [64, 1], BF16)
        bct = sb("bct", [1, 2], F32)
        wm = sb("wmt", [64, 1], BF16)
        bmt = sb("bmt", [1, 1], F32)
        m0 = sb("m0t", [16, T], BF16)
        keepm = sb("keepmt", [128, 32], F32)
        iota256 = sb("iota256t", [128, 256], F32)
        iota16r = sb("iota16rt", [128, 256], F32)
        tcol = sb("tcolt", [128, 1], F32)
        eqsel = sb("eqselt", [128, 16], F32)
        selg = sb("selgt", [128, 16, 128], F32)
        selx = sb("selxt", [128, 8, 16], BF16)
        xtile = sb("xt0", [128, D], F32)
        xln_tm = sb("xln_tm", [128, D], BF16)
        xlnf = sb("xlnf", [128, 8, 128], BF16)
        scratch = {"xsq": sb("xsq", [128, D], BF16), "s1": sb("s1", [128, 1], F32),
                   "s2": sb("s2", [128, 1], F32), "mv": sb("mv", [128, 2], F32),
                   "rstd": sb("rstd", [128, 1], F32)}
        costl = sb("costl", [128, 512], F32)
        sintl = sb("sintl", [128, 512], F32)
        ptm = sb("ptm", [128, D], BF16)
        pnrm = sb("pnrm", [128, D], BF16)
        pf = sb("pf", [128, 8, T], BF16)
        itm = sb("itm", [128, 16, D], BF16)
        iff = sb("iff", [128, 8, 128], BF16)
        sx2 = sb("sx2", [128, T], F32)
        m1 = sb("m1t", [128, 8], F32)
        m2 = sb("m2t", [128, 8], F32)
        iu32 = sb("iu32", [128, 16], U32)
        idxf = sb("idxf", [128, 16], F32)
        tq = sb("tq", [128, 1], F32)
        d0 = sb("d0", [128, 16], F32)
        d0b = sb("d0b", [128, 16], BF16)
        drow = sb("drow", [128, 2048], BF16)
        gidxf = sb("gidxf", [128, 1], F32)
        gidxu = sb("gidxu", [128, 1], U32)
        gtmp = sb("gtmp", [128, 16], F32)
        gath = sb("gath", [128, D], BF16)
        afm = sb("afm", [128, 8, 128], BF16)
        gsb = sb("gsb", [128, 128], BF16)
        relbuf = sb("relbuf", [16, T], BF16)
        h1sb = sb("h1sb", [64, T], BF16)
        h2sb = sb("h2sb", [64, T], BF16)
        mixrow = sx2
        gxrow = sb("gxrow", [1, T], BF16)
        gyrow = sb("gyrow", [1, T], BF16)
        mixT = sb("mixT", [128, 16], F32)
        wsoft = sb("wsoft", [128, 16], F32)
        gxT = sb("gxT", [128, 16], F32)
        gyT = sb("gyT", [128, 16], F32)
        rowmax = sb("rowmax", [128, 1], F32)
        rowsum = sb("rowsum", [128, 1], F32)
        nbias = sb("nbias", [128, 1], F32)
        recl = sb("recl", [128, 1], F32)
        fx = sb("fx", [128, 16], F32)
        fy = sb("fy", [128, 16], F32)
        x0i = sb("x0i", [128, 16], mybir.dt.int16)
        x0f = sb("x0f", [128, 16], F32)
        x1f = sb("x1f", [128, 16], F32)
        wx = sb("wxq", [128, 16], F32)
        y0f = sb("y0f", [128, 16], F32)
        y1f = sb("y1f", [128, 16], F32)
        wy = sb("wyq", [128, 16], F32)
        axw = sb("axw", [128, 256], F32)
        byw = sb("byw", [128, 256], F32)
        ctmp = sb("ctmp", [128, 256], F32)
        coeff = sb("coeff", [128, 256], F32)
        coefb = sb("coefb", [128, 256], BF16)
        cf = sb("cf", [128, 2, 128], BF16)
        vtmb = sb("vtmb", [128, D], BF16)
        vf = sb("vf", [128, 8, 128], BF16)
        PSA = PS["PSA"]
        PSB = PS["PSB"][:, 0:1024]
        PSD = PS["PSD"]

        for c in range(8):
            a.dma("gpq", wi[:, c, :], wi_in[c * 128:(c + 1) * 128, :])
            a.dma("gpq", wp[:, c, :], wp_in[c * 128:(c + 1) * 128, :])
            a.dma("gpq", wot[:, c, :], wo_in[c * 128:(c + 1) * 128, :])
        a.dma("gpq", palt[:, 0, :], palT_in[0:128, :])
        a.dma("gpq", palt[:, 1, :], palT_in[128:256, :])
        for (dst, src) in ((w1g, w1g_in), (w1d, w1d_in), (w2, w2_in),
                           (wc0, wc0_in), (wc1, wc1_in), (wm, wm_in)):
            a.dma("gpq", dst, src)
        for (dst, src) in ((b1, b1_in), (b2, b2_in), (bct, bc_in), (bmt, bm_in),
                           (m0, m0_in), (keepm, keepm_in), (iota256, iota256_in),
                           (iota16r, iota16r_in), (tcol, tcol_in), (eqsel, eqsel_in)):
            a.dma("syq", dst, src)
        a.dma("syq", selg.rearrange("p a b -> p (a b)"), selg_in)
        a.dma("syq", selx.rearrange("p a b -> p (a b)"), selx_in)

        import concourse.bass as _bass
        for b in range(B_PER_CORE):
            base = b * T
            for j in range(T // 128):
                r0 = base + j * 128
                a.dma("syq", xtile, x[r0:r0 + 128, :])
                emit_ln_tile(a, None, xtile, xln_tm, scratch)
                emit_tm_to_fm(a, xln_tm, [xlnf[:, c, :] for c in range(8)], identb,
                              PSB[:, 0:128])
                a.dma("syq", costl, cos_in[j * 128:(j + 1) * 128, :])
                a.dma("syq", sintl, sin_in[j * 128:(j + 1) * 128, :])
                for (wt, is_p) in ((wp, True), (wi, False)):
                    pp = PSA[:, 0:1024]
                    for hf in range(2):
                        for c in range(8):
                            a.mm(pp[:, hf * 512:(hf + 1) * 512], xlnf[:, c, :],
                                 wt[:, c, hf * 512:(hf + 1) * 512],
                                 start=(c == 0), stop=(c == 7))
                    a.tt(sx2[:, 0:512], pp[:, 0:512], costl, ALU.mult)
                    a.tt(sx2[:, 512:1024], pp[:, 512:1024], sintl, ALU.mult)
                    a.tt(ptm[:, 0:512], sx2[:, 0:512], sx2[:, 512:1024], ALU.subtract)
                    a.tt(sx2[:, 0:512], pp[:, 512:1024], costl, ALU.mult)
                    a.tt(sx2[:, 512:1024], pp[:, 0:512], sintl, ALU.mult)
                    a.tt(ptm[:, 512:1024], sx2[:, 0:512], sx2[:, 512:1024], ALU.add)
                    a.act(scratch["xsq"], ptm, AF.Square, accum_out=scratch["s2"])
                    a.act(scratch["s1"], scratch["s2"], AF.Ln)
                    a.act(scratch["rstd"], scratch["s1"], AF.Exp, scale=-0.5)
                    a.ts(scratch["rstd"], scratch["rstd"], 1e12, ALU.min)
                    a.ts(pnrm, ptm, scratch["rstd"][:, 0:1], ALU.mult)
                    if is_p:
                        a.dma("gpq", pnscr[b * 4096 + j * 128:b * 4096 + (j + 1) * 128, :], pnrm)
                        emit_tm_to_fm(a, ptm, [pf[:, c, j * 128:(j + 1) * 128] for c in range(8)],
                                      identb, PSB[:, 0:128])
                    else:
                        a.dma("gpq", pnscr[b * 4096 + 2048 + j * 128:b * 4096 + 2048 + (j + 1) * 128, :], pnrm)
                        a.copy(itm[:, j, :], ptm)
            for qt in range(T // 128):
                L = (qt + 1) * 128
                emit_tm_to_fm(a, itm[:, qt, :], [iff[:, c, :] for c in range(8)], identb,
                              PSB[:, 0:128])
                for kb in range((L + 511) // 512):
                    n = min(512, L - kb * 512)
                    for c in range(8):
                        a.mm(PSA[:, kb * 512:kb * 512 + n], iff[:, c, :],
                             pf[:, c, kb * 512:kb * 512 + n], start=(c == 0), stop=(c == 7))
                S = PSA[:, 0:L]
                a.tt(PSA[:, L - 128:L], PSA[:, L - 128:L], diagm, ALU.add)
                a.op("vector", lambda e, _m=m1, _s=S: e.max(out=_m, in_=_s), reads=[S], writes=[m1])
                a.op("vector", lambda e, _o=sx2, _m=m1, _s=S, _L=L:
                     e.match_replace(out=_o[:, 0:_L], in_to_replace=_m, in_values=_s, imm_value=NEG),
                     reads=[S, m1], writes=[sx2])
                a.op("vector", lambda e, _m=m2, _s=sx2, _L=L: e.max(out=_m, in_=_s[:, 0:_L]),
                     reads=[sx2], writes=[m2])
                a.op("vector", lambda e, _o=iu32, _m=m1, _s=S: e.max_index(out=_o[:, 0:8], in_max=_m, in_values=_s),
                     reads=[S, m1], writes=[iu32])
                a.op("vector", lambda e, _o=iu32, _m=m2, _s=sx2, _L=L: e.max_index(out=_o[:, 8:16], in_max=_m, in_values=_s[:, 0:_L]),
                     reads=[sx2, m2], writes=[iu32])
                a.copy(idxf, iu32)
                a.ts(tq, tcol, float(qt * 128), ALU.add)
                a.ts(d0, idxf, -1.0, ALU.mult, tq[:, 0:1], ALU.add)
                a.ts(d0, d0, 0.0, ALU.max, 1.0 / 2048.0, ALU.mult)
                a.memset(d0[:, 15:16], 0.0)
                a.copy(d0b, d0)
                a.op("gpsimd", lambda e, _o=drow, _i=d0b:
                     e.dma_start(out=_o[0:1, 0:2048].rearrange("o (t k) -> o t k", k=16), in_=_i,
                                 allow_slow_non_contiguous=True),
                     reads=[d0b], writes=[drow], queue="gpq")
                a.ts(idxf[:, 15:16], tq, float(2048), ALU.add)
                a.ts(idxf, idxf, float(b * 4096), ALU.add)
                for g in range(16):
                    a.mm(PSD[:, 0:16], selg[:, g, :], idxf, start=True, stop=True)
                    a.tt(gtmp, PSD[:, 0:16], eqsel, ALU.mult)
                    a.reduce(gidxf, gtmp, ALU.add)
                    a.copy(gidxu, gidxf)
                    a.op("gpsimd", lambda e, _o=gath, _i=pnscr, _x=gidxu:
                         e.indirect_dma_start(out=_o, out_offset=None, in_=_i,
                                              in_offset=_bass.IndirectOffsetOnAxis(ap=_x, axis=0)),
                         reads=[pnscr, gidxu], writes=[gath], queue="gpq")
                    for c in range(8):
                        a.trans(PSB[:, c * 128:(c + 1) * 128], gath[:, c * 128:(c + 1) * 128], identb)
                    a.copy(afm.rearrange("p a b -> p (a b)"), PSB[:, 0:1024])
                    for c in range(8):
                        a.mm(PSD[:, 128:256], afm[:, c, :], afm[:, c, :],
                             start=(c == 0), stop=(c == 7))
                    a.copy(gsb, PSD[:, 128:256])
                    for i in range(8):
                        a.mm(PSD[0:16, 256:272].rearrange("p a -> p a") if False else PSD[0:16, 256 + i * 16:256 + (i + 1) * 16],
                             selx[:, i, :], gsb[:, i * 16:(i + 1) * 16], start=True, stop=True)
                    a.ts(relbuf[0:16, g * 128:(g + 1) * 128], PSD[0:16, 256:384],
                         1.0, ALU.min, -1.0, ALU.max)
                if qt == 0:
                    a.tt(relbuf[0:16, 0:2048], relbuf[0:16, 0:2048], m0, ALU.mult)
                for sl in range(4):
                    ssl = slice(sl * 512, (sl + 1) * 512)
                    a.mm(PSA[0:64, ssl], w1g, relbuf[0:16, ssl], start=True, stop=False)
                    a.mm(PSA[0:64, ssl], w1d, drow[0:1, ssl], start=False, stop=True)
                a.act(h1sb, PSA[0:64, 0:2048], AF.Gelu, bias=b1[:, 0:1], scale=1.0)
                for sl in range(4):
                    ssl = slice(sl * 512, (sl + 1) * 512)
                    a.mm(PSA[64:128, ssl], w2, h1sb[:, ssl], start=True, stop=True)
                a.act(h2sb, PSA[64:128, 0:2048], AF.Gelu, bias=b2[:, 0:1], scale=1.0)
                for (wv_, bi_, dst, func) in ((wc0, bct[0:1, 0:1], gxrow, AF.Tanh),
                                              (wc1, bct[0:1, 1:2], gyrow, AF.Tanh),
                                              (wm, bmt[0:1, 0:1], mixrow, AF.Identity)):
                    for sl in range(4):
                        ssl = slice(sl * 512, (sl + 1) * 512)
                        a.mm(PSA[0:1, ssl], wv_, h2sb[:, ssl], start=True, stop=True)
                    a.act(dst[0:1, 0:2048], PSA[0:1, 0:2048], func, bias=bi_, scale=1.0)
                for (src, dstT, idn) in ((mixrow, mixT, identf), (gxrow, gxT, identb), (gyrow, gyT, identb)):
                    for k in range(16):
                        if idn is identf:
                            a.trans(PSD[:, 384 + k:384 + k + 1],
                                    src[0:1, 0:2048].rearrange("o (t k) -> o k t", k=16)[:, k, :], idn[0:1, 0:1])
                        else:
                            a.trans(PSB[:, 512 + 2 * k:512 + 2 * k + 1],
                                    src[0:1, 0:2048].rearrange("o (t k) -> o k t", k=16)[:, k, :], idn[0:1, 0:1])
                    if idn is identf:
                        a.copy(dstT, PSD[:, 384:400])
                    else:
                        a.copy(dstT, PSB[:, 512:544].rearrange("p (k two) -> p k two", two=2)[:, :, 0])
                kslice = slice(16, 32) if qt == 0 else slice(0, 16)
                a.tt(mixT, mixT, keepm[:, kslice], ALU.add)
                a.reduce(rowmax, mixT, ALU.max)
                a.ts(nbias, rowmax, -1.0, ALU.mult)
                a.act(wsoft, mixT, AF.Exp, bias=nbias[:, 0:1], scale=1.0, accum_out=rowsum)
                a.op("vector", lambda e, _o=recl, _i=rowsum: e.reciprocal(out=_o, in_=_i),
                     reads=[rowsum], writes=[recl])
                a.ts(wsoft, wsoft, recl[:, 0:1], ALU.mult)
                for (gT, f_, i0_, f0_, f1_, w_) in ((gxT, fx, x0i, x0f, x1f, wx),
                                                    (gyT, fy, x0i, y0f, y1f, wy)):
                    a.ts(f_, gT, 1.0, ALU.add, 7.5, ALU.mult)
                    a.ts(i0_, f_, 0.5, ALU.subtract)
                    a.copy(f0_, i0_)
                    a.tt(w_, f_, f0_, ALU.subtract)
                    a.ts(f1_, f0_, 1.0, ALU.add, 15.0, ALU.min)
                a.op("vector", lambda e, _o=axw, _i=iota16r, _x=x0f:
                     e.tensor_tensor(out=_o.rearrange("p (k w) -> p k w", w=16),
                                     in0=_i.rearrange("p (k w) -> p k w", w=16),
                                     in1=_x.unsqueeze(2).broadcast_to([128, 16, 16]), op=ALU.is_equal),
                     reads=[iota16r, x0f], writes=[axw])
                a.op("vector", lambda e, _o=ctmp, _i=iota16r, _x=x1f:
                     e.tensor_tensor(out=_o.rearrange("p (k w) -> p k w", w=16),
                                     in0=_i.rearrange("p (k w) -> p k w", w=16),
                                     in1=_x.unsqueeze(2).broadcast_to([128, 16, 16]), op=ALU.is_equal),
                     reads=[iota16r, x1f], writes=[ctmp])
                a.tt(ctmp, ctmp, axw, ALU.subtract)
                a.op("vector", lambda e, _o=ctmp, _i=ctmp, _x=wx:
                     e.tensor_tensor(out=_o.rearrange("p (k w) -> p k w", w=16),
                                     in0=_i.rearrange("p (k w) -> p k w", w=16),
                                     in1=_x.unsqueeze(2).broadcast_to([128, 16, 16]), op=ALU.mult),
                     reads=[ctmp, wx], writes=[ctmp])
                a.tt(axw, axw, ctmp, ALU.add)
                a.op("vector", lambda e, _o=byw, _i=iota16r, _x=y0f:
                     e.tensor_tensor(out=_o.rearrange("p (k w) -> p k w", w=16),
                                     in0=_i.rearrange("p (k w) -> p k w", w=16),
                                     in1=_x.unsqueeze(2).broadcast_to([128, 16, 16]), op=ALU.is_equal),
                     reads=[iota16r, y0f], writes=[byw])
                a.op("vector", lambda e, _o=ctmp, _i=iota16r, _x=y1f:
                     e.tensor_tensor(out=_o.rearrange("p (k w) -> p k w", w=16),
                                     in0=_i.rearrange("p (k w) -> p k w", w=16),
                                     in1=_x.unsqueeze(2).broadcast_to([128, 16, 16]), op=ALU.is_equal),
                     reads=[iota16r, y1f], writes=[ctmp])
                a.tt(ctmp, ctmp, byw, ALU.subtract)
                a.op("vector", lambda e, _o=ctmp, _i=ctmp, _x=wy:
                     e.tensor_tensor(out=_o.rearrange("p (k w) -> p k w", w=16),
                                     in0=_i.rearrange("p (k w) -> p k w", w=16),
                                     in1=_x.unsqueeze(2).broadcast_to([128, 16, 16]), op=ALU.mult),
                     reads=[ctmp, wy], writes=[ctmp])
                a.tt(byw, byw, ctmp, ALU.add)
                a.op("vector", lambda e, _o=byw, _i=byw, _x=wsoft:
                     e.tensor_tensor(out=_o.rearrange("p (k w) -> p k w", w=16),
                                     in0=_i.rearrange("p (k w) -> p k w", w=16),
                                     in1=_x.unsqueeze(2).broadcast_to([128, 16, 16]), op=ALU.mult),
                     reads=[byw, wsoft], writes=[byw])
                first = True
                for k in range(16):
                    eng = "vector"
                    a.op(eng, lambda e, _o=ctmp, _b=byw, _a=axw, _k=k:
                         e.tensor_tensor(out=_o, in0=_b[:, _k * 16:(_k + 1) * 16].unsqueeze(2).broadcast_to([128, 16, 16]),
                                         in1=_a[:, _k * 16:(_k + 1) * 16].unsqueeze(1).broadcast_to([128, 16, 16]),
                                         op=ALU.mult),
                         reads=[byw, axw], writes=[ctmp])
                    if first:
                        a.copy(coeff, ctmp)
                        first = False
                    else:
                        a.tt(coeff, coeff, ctmp, ALU.add)
                a.copy(coefb, coeff)
                for c2 in range(2):
                    a.trans(PSB[:, 0:128], coefb[:, c2 * 128:(c2 + 1) * 128], identb)
                    a.copy(cf[:, c2, :], PSB[:, 0:128])
                pv = PSA[:, 1024:2048]
                for hf in range(2):
                    for c2 in range(2):
                        a.mm(pv[:, hf * 512:(hf + 1) * 512], cf[:, c2, :],
                             palt[:, c2, hf * 512:(hf + 1) * 512], start=(c2 == 0), stop=(c2 == 1))
                a.copy(vtmb, pv)
                emit_tm_to_fm(a, vtmb, [vf[:, c, :] for c in range(8)], identb, PSB[:, 0:128])
                pd = PSA[:, 0:1024]
                for c in range(8):
                    for hf in range(2):
                        a.mm(pd[:, hf * 512:(hf + 1) * 512], vf[:, c, :],
                             wot[:, c, hf * 512:(hf + 1) * 512], start=(c == 0), stop=(c == 7))
                r0 = base + qt * 128
                a.dma("scq", xtile, x[r0:r0 + 128, :])
                a.tt(xtile, xtile, pd, ALU.add)
                a.dma("gpq", xo[r0:r0 + 128, :], xtile)


def host_pal_consts():
    c = {}
    half = 512
    inv_freq = 1.0 / 10000.0 ** (np.arange(half, dtype=np.float32) / half)
    ang = np.arange(T, dtype=np.float32)[:, None] * inv_freq[None, :]
    c["cosr"] = np.cos(ang).astype(np.float32)
    c["sinr"] = np.sin(ang).astype(np.float32)
    m0 = np.zeros((16, T), np.float32)
    for t in range(128):
        for k in range(16):
            for j in range(16):
                ok = (k <= t and k < 15) and (j <= t or j == 15)
                m0[j, t * 16 + k] = 1.0 if ok else 0.0
    c["m0"] = m0.astype(ml_dtypes.bfloat16)
    km = np.zeros((128, 32), np.float32)
    km[:, 15] = NEG
    for p in range(128):
        for k in range(16):
            km[p, 16 + k] = NEG if (k > p or k == 15) else 0.0
    c["keepm"] = km
    c["iota256"] = np.tile(np.arange(256, dtype=np.float32)[None, :], (128, 1))
    c["iota16r"] = np.tile(np.tile(np.arange(16, dtype=np.float32), 16)[None, :], (128, 1))
    c["tcol"] = np.arange(128, dtype=np.float32)[:, None].copy()
    eq = np.zeros((128, 16), np.float32)
    for p in range(128):
        eq[p, p % 16] = 1.0
    c["eqsel"] = eq
    selg = np.zeros((16, 128, 128), np.float32)
    for g in range(16):
        for p in range(128):
            selg[g, 8 * g + p // 16, p] = 1.0
    c["selg"] = np.ascontiguousarray(np.transpose(selg, (1, 0, 2)).reshape(128, 16 * 128))
    selx = np.zeros((8, 128, 16), np.float32)
    for i in range(8):
        for j in range(16):
            selx[i, 16 * i + j, j] = 1.0
    c["selx"] = np.ascontiguousarray(np.transpose(selx, (1, 0, 2)).reshape(128, 8 * 16)).astype(ml_dtypes.bfloat16)
    return c


def run_attn_pal(x_full, Wi, Wp, palette, W1, b1, W2, b2, Wc, bc, Wm, bm, Wo):
    consts = host_constants()
    pc = host_pal_consts()
    nc = build_attn_pal()
    in_common = {
        "wi": np.ascontiguousarray(Wi), "wp": np.ascontiguousarray(Wp),
        "wo": np.ascontiguousarray(Wo[0]),
        "palT": np.ascontiguousarray(palette.reshape(D, 256).T),
        "w1g": np.ascontiguousarray(W1[0:16, :]), "w1d": np.ascontiguousarray(W1[16:17, :]),
        "b1": np.ascontiguousarray(b1[:, None]), "w2": np.ascontiguousarray(W2),
        "b2": np.ascontiguousarray(b2[:, None]),
        "wcc0": np.ascontiguousarray(Wc[:, 0:1]), "wcc1": np.ascontiguousarray(Wc[:, 1:2]),
        "bc": np.ascontiguousarray(bc[None, :]), "wm": np.ascontiguousarray(Wm),
        "bm": np.ascontiguousarray(bm[None, :]),
        "ident": consts["ident_bf"], "identf": consts["ident"], "diagmask": consts["diagmask"],
    }
    in_common.update(pc)
    in_maps = []
    for c in range(NCORES):
        m = dict(in_common)
        m["x"] = x_full.reshape(16 * T, D)[c * B_PER_CORE * T:(c + 1) * B_PER_CORE * T]
        in_maps.append(m)
    res = run_bass_kernel_spmd(nc, in_maps, core_ids=list(range(NCORES)))
    out = np.concatenate([res.results[c]["xo"] for c in range(NCORES)], axis=0)
    return out.reshape(16, T, D)



def build_l0_ffn_dbg():
    hidden = 3072
    HC = hidden // 128
    NTOK = B_PER_CORE * T
    NT = NTOK // 128
    NT512 = NTOK // 512
    nc = bass.Bass("TRN2", target_bir_lowering=False)
    x = nc.dram_tensor("x", [NTOK, D], F32, kind="ExternalInput").ap()
    wgT = nc.dram_tensor("wgT", [D, hidden], I32, kind="ExternalInput").ap()
    wuT = nc.dram_tensor("wuT", [D, hidden], I32, kind="ExternalInput").ap()
    wdT = nc.dram_tensor("wdT", [hidden, D], I32, kind="ExternalInput").ap()
    ident_in = nc.dram_tensor("ident", [128, 128], BF16, kind="ExternalInput").ap()
    xo = nc.dram_tensor("xo", [NTOK, D], F32, kind="ExternalOutput").ap()
    oxln = nc.dram_tensor("oxln", [NTOK, D], F32, kind="ExternalOutput").ap()
    oh = nc.dram_tensor("oh", [24, 128, NTOK], F32, kind="ExternalOutput").ap()
    opd = nc.dram_tensor("opd", [NTOK, D], F32, kind="ExternalOutput").ap()

    from contextlib import ExitStack
    with ExitStack() as stack:
        ctx = Ctx(nc, stack)
        sb, ps = ctx.sb, ctx.ps
        junk = sb("junk", [128, 256], F32)
        identb = sb("identb", [128, 128], BF16)
        xtile = [sb(f"xt{i}", [128, D], F32) for i in range(2)]
        xln_tm = sb("xln_tm", [128, D], BF16)
        xlnf = sb("xlnf", [128, 8, T // 2], BF16)
        scratch = {"xsq": sb("xsq", [128, D], BF16), "s1": sb("s1", [128, 1], F32),
                   "s2": sb("s2", [128, 1], F32), "mv": sb("mv", [128, 2], F32),
                   "rstd": sb("rstd", [128, 1], F32)}
        wgt = sb("wgt", [128, 8, hidden], BF16)
        wut = sb("wut", [128, 8, hidden], BF16)
        wdt = sb("wdtr", [128, HC, D], BF16)
        hbuf = sb("hbuf", [128, HC, 512], BF16)
        sg = [sb(f"sg{i}", [128, 512], BF16) for i in range(2)]
        pg = [ps(f"pg{i}", [128, 512], F32) for i in range(2)]
        pu = [ps(f"pu{i}", [128, 512], F32) for i in range(2)]
        pd = ps("pd", [128, D], F32)
        pxp = ps("pxp", [128, 128], BF16)
        a = A(nc)

        a.dma("syq", identb, ident_in)
        for c in range(8):
            a.dma("gpq", wgt[:, c, :], wgT[c * 128:(c + 1) * 128, :])
            a.dma("gpq", wut[:, c, :], wuT[c * 128:(c + 1) * 128, :])
        for hc in range(HC):
            a.dma("gpq", wdt[:, hc, :], wdT[hc * 128:(hc + 1) * 128, :])
        a.warmup(junk)

        NTSEQ = T // 256
        NT512SEQ = T // 1024
        for b in range(2 * B_PER_CORE):
            base = b * (T // 2)
            for j in range(NTSEQ):
                xt = xtile[j % 2]
                r0 = base + j * 128
                a.dma("syq", xt, x[r0:r0 + 128, :])
                emit_ln_tile(a, None, xt, xln_tm, scratch, extra_scale=1.0 / 127.0)
                a.dma("gpq", oxln[r0:r0 + 128, :], xln_tm)
                emit_tm_to_fm(a, xln_tm, [xlnf[:, c, j * 128:(j + 1) * 128] for c in range(8)],
                              identb, pxp)
            for tt in range(NT512SEQ):
                tsl = slice(tt * 512, (tt + 1) * 512)
                for hc in range(HC):
                    g = pg[hc % 2]
                    u = pu[hc % 2]
                    sgt = sg[hc % 2]
                    for c in range(8):
                        a.mm(g, wgt[:, c, hc * 128:(hc + 1) * 128], xlnf[:, c, tsl],
                             start=(c == 0), stop=(c == 7))
                    for c in range(8):
                        a.mm(u, wut[:, c, hc * 128:(hc + 1) * 128], xlnf[:, c, tsl],
                             start=(c == 0), stop=(c == 7))
                    a.act(sgt, g, AF.Silu)
                    a.stt(hbuf[:, hc, :], sgt, 1.0 / 127.0, u, ALU.mult, ALU.mult)
                    a.dma("gpq", oh[hc, :, base + tt * 512:base + tt * 512 + 512], hbuf[:, hc, :], tag="dbg")
                for st4 in range(4):
                    j = tt * 4 + st4
                    for hc in range(HC):
                        for half in range(2):
                            a.mm(pd[:, half * 512:(half + 1) * 512],
                                 hbuf[:, hc, st4 * 128:(st4 + 1) * 128],
                                 wdt[:, hc, half * 512:(half + 1) * 512],
                                 start=(hc == 0), stop=(hc == HC - 1))
                    xt = xtile[j % 2]
                    r0 = base + j * 128
                    a.dma("syq", xt, x[r0:r0 + 128, :])
                    a.copy(mvd := xt, xt) if False else None
                    a.dma("vq", opd[r0:r0 + 128, :], xt, tag="preres") if False else None
                    a.tt(xt, xt, pd, ALU.add)
                    a.dma("gpq", xo[r0:r0 + 128, :], xt)

        a.barrier_outputs([xo, oxln, oh])
        a.emit(stack)
    return nc




def build_fused():
    NTOK = B_PER_CORE * T
    nc = bass.Bass("TRN2", target_bir_lowering=False)
    dd = {}
    def D_(pfx, name, shape, dt_):
        dd[(pfx, name)] = nc.dram_tensor(pfx + name, shape, dt_, kind="ExternalInput").ap()
    x_in = nc.dram_tensor("x", [NTOK, D], F32, kind="ExternalInput").ap()
    xo = nc.dram_tensor("xo", [NTOK, D], F32, kind="ExternalOutput").ap()
    xs = [nc.dram_tensor(f"xs{i}", [NTOK, D], F32, kind="Internal").ap() for i in range(2)]
    pnscr = nc.dram_tensor("pnscr", [B_PER_CORE * 4096, D], BF16, kind="Internal").ap()
    D_("c_", "ident", [128, 128], BF16)
    D_("c_", "identf", [128, 128], F32)
    D_("c_", "diagmask", [128, 128], F32)
    for n, sh, dt_ in [("wqT", [D, D], F32), ("wqTs", [D, D], F32), ("wkT", [D, 192], F32),
                       ("wvT", [D, 192], F32), ("dks", [192, D], F32), ("dkss", [192, D], F32),
                       ("dkm", [192, D], BF16), ("dkms", [192, D], BF16), ("dv", [192, D], F32),
                       ("woT", [D, D], F32), ("c1", [64, T], BF16), ("c2", [64, T], BF16)]:
        D_("aa_", n, sh, dt_)
    for n, sh in [("wgT", [D, 3072]), ("wuT", [D, 3072]), ("wdT", [3072, D])]:
        D_("f0_", n, sh, I32)
    for n in ("wqT", "wkT", "wvT", "woT"):
        D_("ab_", n, [D, D], F32)
    D_("ab_", "alibik", [16, T], BF16)
    for n, sh, dt_ in [("winT", [D, 4096], F32), ("wxT", [2048, 96], F32), ("wdtT", [64, 2048], F32),
                       ("woutT", [2048, D], F32), ("convr", [128, 64], F32), ("convb", [128, 16], F32),
                       ("bdt", [128, 16], F32), ("dprm", [128, 16], F32), ("alr", [128, 256], F32),
                       ("sel", [128, 2048], BF16), ("selt", [128, 2048], BF16), ("selb", [128, 256], BF16),
                       ("lnw", [128, D], F32), ("lnb", [128, D], F32)]:
        D_("ma_", n, sh, dt_)
    for n, sh in [("wg", [2816, D]), ("wu", [2816, D]), ("wd", [D, 2816])]:
        D_("f1_", n, sh, F32)
    for n, sh, dt_ in [("wi", [D, D], F32), ("wp", [D, D], F32), ("wo", [D, D], F32),
                       ("palT", [256, D], F32), ("w1g", [16, 64], F32), ("w1d", [1, 64], F32),
                       ("b1", [64, 1], F32), ("w2", [64, 64], F32), ("b2", [64, 1], F32),
                       ("wcc0", [64, 1], F32), ("wcc1", [64, 1], F32), ("bc", [1, 2], F32),
                       ("wm", [64, 1], F32), ("bm", [1, 1], F32), ("cosr", [T, 512], F32),
                       ("sinr", [T, 512], F32), ("m0", [16, T], BF16), ("keepm", [128, 32], F32),
                       ("iota256", [128, 256], F32), ("iota16r", [128, 256], F32),
                       ("tcol", [128, 1], F32), ("eqsel", [128, 16], F32),
                       ("selg", [128, 2048], F32), ("selx", [128, 128], BF16)]:
        D_("pp_", n, sh, dt_)

    def sub(pfx):
        return {n: ap for (p, n), ap in dd.items() if p == pfx}

    from contextlib import ExitStack
    with ExitStack() as stack:
        ctx0 = Ctx(nc, stack)
        junk = ctx0.sb("junk", [128, 256], F32)
        identb = ctx0.sb("identb", [128, 128], BF16)
        identf = ctx0.sb("identfp", [128, 128], F32)
        diagm = ctx0.sb("diagmp", [128, 128], F32)
        com = {"junk": junk, "identb": identb, "identf": identf, "diagm": diagm}
        PS = {"PSA": ctx0.ps("PSA", [128, 2048], F32),
              "PSB": ctx0.ps("PSB", [128, 2048], BF16),
              "PSD": ctx0.ps("PSD", [128, 512], F32),
              "PSE": ctx0.ps("PSE", [128, 512], F32)}
        a = A(nc)
        a.dma("syq", identb, dd[("c_", "ident")])
        a.dma("syq", identf, dd[("c_", "identf")])
        a.dma("syq", diagm, dd[("c_", "diagmask")])
        a.memset(junk, 1.0)
        a.warmup(junk)
        for f in (AF.Exp, AF.Abs, AF.Square, AF.Gelu, AF.Tanh, AF.Sign, AF.Sqrt,
                  AF.Silu, AF.Relu, AF.Ln):
            a.act(junk, junk, f)

        chain = [
            (lambda a_, c_, xi, xo_: emit_attn(a_, c_, sub("aa_"), PS, com, xi, xo_, "a")),
            (lambda a_, c_, xi, xo_: emit_l0_ffn(a_, c_, sub("f0_"), PS, com, xi, xo_)),
            (lambda a_, c_, xi, xo_: emit_attn(a_, c_, sub("ab_"), PS, com, xi, xo_, "b")),
            (lambda a_, c_, xi, xo_: emit_l0_ffn(a_, c_, sub("f0_"), PS, com, xi, xo_)),
            (lambda a_, c_, xi, xo_: emit_mamba(a_, c_, sub("ma_"), PS, com, xi, xo_)),
            (lambda a_, c_, xi, xo_: emit_l1_ffn(a_, c_, sub("f1_"), PS, com, xi, xo_)),
            (lambda a_, c_, xi, xo_: emit_attn_pal(a_, c_, sub("pp_"), PS, com, xi, xo_, pnscr)),
            (lambda a_, c_, xi, xo_: emit_l1_ffn(a_, c_, sub("f1_"), PS, com, xi, xo_)),
        ]
        for k, em in enumerate(chain):
            xi = x_in if k == 0 else xs[k % 2]
            xo_ = xo if k == len(chain) - 1 else xs[(k + 1) % 2]
            with ExitStack() as substack:
                ctx = Ctx(nc, substack, prefix=f"s{k}_")
                em(a, ctx, xi, xo_)
            a.barrier()
        a.barrier_outputs([xo])
        a.emit(stack)
        nops = len(a.ops)
    return nc, nops


def _host_inputs(inp):
    consts = host_constants()
    m = {"c_ident": consts["ident_bf"], "c_identf": consts["ident"], "c_diagmask": consts["diagmask"]}
    Wq = np.asarray(inp["L0_attn_a_Wq"], np.float32)
    perm = np.arange(D) ^ 1
    dk, dks_sw, mk, mks, dv = host_attn_a_decoders(np.asarray(inp["L0_attn_a_key_decoder"], np.float32),
                                                   np.asarray(inp["L0_attn_a_value_decoder"], np.float32))
    C1, C2 = rope_tables_interleaved()
    m.update({"aa_wqT": np.ascontiguousarray(Wq), "aa_wqTs": np.ascontiguousarray(Wq[:, perm]),
              "aa_wkT": np.ascontiguousarray(np.asarray(inp["L0_attn_a_Wk"], np.float32)),
              "aa_wvT": np.ascontiguousarray(np.asarray(inp["L0_attn_a_Wv"], np.float32)),
              "aa_dks": dk, "aa_dkss": dks_sw, "aa_dkm": mk, "aa_dkms": mks, "aa_dv": dv,
              "aa_woT": np.ascontiguousarray(np.asarray(inp["L0_attn_a_Wproj"], np.float32)),
              "aa_c1": C1.astype(ml_dtypes.bfloat16), "aa_c2": C2.astype(ml_dtypes.bfloat16)})
    m.update({"f0_wgT": np.ascontiguousarray(np.asarray(inp["L0_ffn_w_gate"], np.int32).T),
              "f0_wuT": np.ascontiguousarray(np.asarray(inp["L0_ffn_w_up"], np.int32).T),
              "f0_wdT": np.ascontiguousarray(np.asarray(inp["L0_ffn_w_down"], np.int32).T)})
    m.update({"ab_wqT": np.ascontiguousarray(np.asarray(inp["L0_attn_b_Wq"], np.float32).T),
              "ab_wkT": np.ascontiguousarray(np.asarray(inp["L0_attn_b_Wk"], np.float32).T),
              "ab_wvT": np.ascontiguousarray(np.asarray(inp["L0_attn_b_Wv"], np.float32).T),
              "ab_woT": np.ascontiguousarray(np.asarray(inp["L0_attn_b_Wo"], np.float32).T),
              "ab_alibik": host_alibi()})
    sel_h, selt_h, selbc = host_mamba_consts()
    A_log = np.asarray(inp["L1_attn_a_A_log"], np.float32)
    alr = np.zeros((128, 256), np.float32)
    p_ = np.arange(128)
    for ct in range(256):
        alr[p_, ct] = A_log[8 * ct + (p_ % 8), p_ // 8]
    conv_w = np.asarray(inp["L1_attn_a_conv_w"], np.float32)
    m.update({"ma_winT": np.ascontiguousarray(np.asarray(inp["L1_attn_a_W_in"], np.float32).T),
              "ma_wxT": np.ascontiguousarray(np.asarray(inp["L1_attn_a_W_x"], np.float32).T),
              "ma_wdtT": np.ascontiguousarray(np.asarray(inp["L1_attn_a_W_dt"], np.float32).T),
              "ma_woutT": np.ascontiguousarray(np.asarray(inp["L1_attn_a_W_out"], np.float32).T),
              "ma_convr": np.ascontiguousarray(conv_w[:, 0, :].reshape(16, 128, 4).transpose(1, 0, 2).reshape(128, 64)),
              "ma_convb": np.ascontiguousarray(np.asarray(inp["L1_attn_a_conv_b"], np.float32).reshape(16, 128).T),
              "ma_bdt": np.ascontiguousarray(np.asarray(inp["L1_attn_a_b_dt"], np.float32).reshape(16, 128).T),
              "ma_dprm": np.ascontiguousarray(np.asarray(inp["L1_attn_a_D_param"], np.float32).reshape(16, 128).T),
              "ma_alr": alr, "ma_sel": sel_h, "ma_selt": selt_h, "ma_selb": selbc,
              "ma_lnw": np.ascontiguousarray(np.tile(np.asarray(inp["L1_attn_a_ln_w"], np.float32)[None, :], (128, 1))),
              "ma_lnb": np.ascontiguousarray(np.tile(np.asarray(inp["L1_attn_a_ln_b"], np.float32)[None, :], (128, 1)))})
    m.update({"f1_wg": np.ascontiguousarray(np.asarray(inp["L1_ffn_w_gate"], np.float32)),
              "f1_wu": np.ascontiguousarray(np.asarray(inp["L1_ffn_w_up"], np.float32)),
              "f1_wd": np.ascontiguousarray(np.asarray(inp["L1_ffn_w_down"], np.float32))})
    pc = host_pal_consts()
    m.update({"pp_" + k: v for k, v in pc.items()})
    m.update({"pp_wi": np.ascontiguousarray(np.asarray(inp["L1_attn_b_Wi"], np.float32)),
              "pp_wp": np.ascontiguousarray(np.asarray(inp["L1_attn_b_Wp"], np.float32)),
              "pp_wo": np.ascontiguousarray(np.asarray(inp["L1_attn_b_Wo"], np.float32)[0]),
              "pp_palT": np.ascontiguousarray(np.asarray(inp["L1_attn_b_palette"], np.float32).reshape(D, 256).T),
              "pp_w1g": np.ascontiguousarray(np.asarray(inp["L1_attn_b_W1"], np.float32)[0:16, :]),
              "pp_w1d": np.ascontiguousarray(np.asarray(inp["L1_attn_b_W1"], np.float32)[16:17, :]),
              "pp_b1": np.ascontiguousarray(np.asarray(inp["L1_attn_b_b1"], np.float32)[:, None]),
              "pp_w2": np.ascontiguousarray(np.asarray(inp["L1_attn_b_W2"], np.float32)),
              "pp_b2": np.ascontiguousarray(np.asarray(inp["L1_attn_b_b2"], np.float32)[:, None]),
              "pp_wcc0": np.ascontiguousarray(np.asarray(inp["L1_attn_b_Wc"], np.float32)[:, 0:1]),
              "pp_wcc1": np.ascontiguousarray(np.asarray(inp["L1_attn_b_Wc"], np.float32)[:, 1:2]),
              "pp_bc": np.ascontiguousarray(np.asarray(inp["L1_attn_b_bc"], np.float32)[None, :]),
              "pp_wm": np.ascontiguousarray(np.asarray(inp["L1_attn_b_Wm"], np.float32)),
              "pp_bm": np.ascontiguousarray(np.asarray(inp["L1_attn_b_bm"], np.float32)[None, :])})
    return m


def kernel(**inp):
    x = np.ascontiguousarray(np.asarray(inp["L0_attn_a_x"], dtype=np.float32)).reshape(16 * T, D)
    common = _host_inputs(inp)
    nc, nops = build_fused()
    in_maps = []
    for c in range(NCORES):
        m = dict(common)
        m["x"] = x[c * B_PER_CORE * T:(c + 1) * B_PER_CORE * T]
        in_maps.append(m)
    res = run_bass_kernel_spmd(nc, in_maps, core_ids=list(range(NCORES)))
    out = np.concatenate([res.results[c]["xo"] for c in range(NCORES)], axis=0)
    return out.reshape(16, T, D).astype(np.float32)
